# version 22
# baseline (speedup 1.0000x reference)
import numpy as np
import concourse.bass as bass
import concourse.mybir as mybir
from concourse.ap import AP
from concourse.bass_utils import run_bass_kernel_spmd

F32 = mybir.dt.float32
BF16 = mybir.dt.bfloat16
AF = mybir.ActivationFunctionType
ALU = mybir.AluOpType
AXX = mybir.AxisListType.X

E = 1024
T = 256
PRELOADS = True
SEQ = 2048
BPC = 2
NCORES = 8
HID = 4096
NB = 512
PI = float(np.pi)


class Ctx:

    ENGINES = ("tensor", "vector", "scalar", "gpsimd", "sync")

    def __init__(self, nc):
        self.nc = nc
        self.eng = {"tensor": nc.tensor, "vector": nc.vector,
                    "scalar": nc.scalar, "gpsimd": nc.gpsimd, "sync": nc.sync}
        self.eng_sem = {}
        self.instrs = {e: [] for e in self.ENGINES}
        self.inc_total = {e: 0 for e in self.ENGINES}
        self.drained_upto = {e: 0 for e in self.ENGINES}
        self.last_write = {}
        self.readers = {}
        self.waited = {}
        self.dma_sems = {}

    def sem_for(self, engname):
        if engname not in self.eng_sem:
            self.eng_sem[engname] = self.nc.alloc_semaphore(name=f"es_{engname}")
        return self.eng_sem[engname]

    def stream(self, name):
        if name not in self.dma_sems:
            self.dma_sems[name] = [self.nc.alloc_semaphore(name=f"ds_{name}"), 0]
        return self.dma_sems[name]

    def _ensure_inc(self, engname, idx):
        lst = self.instrs[engname]
        for j in range(idx, len(lst)):
            if lst[j][1] is not None:
                return lst[j][1]
        instr, _ = lst[idx]
        self.inc_total[engname] += 1
        v = self.inc_total[engname]
        instr.then_inc(self.sem_for(engname), 1)
        lst[idx] = (instr, v)
        return v

    def _wait(self, waiter, sem, val):
        k = (waiter, id(sem))
        if self.waited.get(k, 0) >= val:
            return
        self.eng[waiter].wait_ge(sem, val)
        self.waited[k] = val

    def _dep(self, waiter, prod, pe_skip):
        if prod[0] == "eng":
            _, pname, pidx = prod
            if pname == waiter:
                if pname == "tensor" and pe_skip:
                    return
                if pidx >= self.drained_upto[pname]:
                    ins = self.eng[pname].drain()
                    self.instrs[pname].append((ins, None))
                    self.drained_upto[pname] = len(self.instrs[pname])
            else:
                v = self._ensure_inc(pname, pidx)
                self._wait(waiter, self.sem_for(pname), v)
        else:
            _, sname, _cnt = prod
            sem, total = self.dma_sems[sname]
            self._wait(waiter, sem, 16 * total)

    def _edges(self, waiter, reads, writes, pe_skip):
        seen = set()

        def dep1(p):
            if p is None or p in seen:
                return
            seen.add(p)
            self._dep(waiter, p, pe_skip)

        for k in reads:
            dep1(self.last_write.get(k))
        for k in writes:
            dep1(self.last_write.get(k))
            for p2 in self.readers.get(k, ()):
                dep1(p2)

    def _record(self, rec, reads, writes):
        for k in writes:
            self.last_write[k] = rec
            self.readers[k] = []
        for k in reads:
            self.readers.setdefault(k, []).append(rec)

    def op(self, engname, fn, reads=(), writes=(), pe_skip=False):
        self._edges(engname, reads, writes, pe_skip)
        ins = fn(self.eng[engname])
        idx = len(self.instrs[engname])
        self.instrs[engname].append((ins, None))
        self._record(("eng", engname, idx), reads, writes)
        return ins

    def dma(self, engname, streamname, out, in_, reads=(), writes=()):
        self._edges(engname, reads, writes, False)
        st = self.stream(streamname)
        ins = self.eng[engname].dma_start(out=out, in_=in_)
        ins.then_inc(st[0], 16)
        st[1] += 1
        self.instrs[engname].append((ins, None))
        self._record(("dma", streamname, st[1]), reads, writes)
        return ins

    def finish(self, waiter="sync"):
        for name in self.dma_sems:
            sem, total = self.dma_sems[name]
            if total:
                self._wait(waiter, sem, 16 * total)


class RowAlloc:
    def __init__(self, t, nslots, width):
        self.t, self.width = t, width
        self.free = list(range(nslots))
        self.named = {}

    def get(self, name, n=None):
        if name not in self.named:
            self.named[name] = self.free.pop(0)
        return self.t[0:1, self.named[name], 0:(n or self.width)]

    def drop(self, *names):
        for name in names:
            if name in self.named:
                self.free.append(self.named.pop(name))


def _consts():
    k = np.arange(E)
    km = (np.outer(k, k) % E).astype(np.float64)
    ang = 2.0 * np.pi * km / E
    COS = np.cos(ang).astype(np.float32)
    SIN = np.sin(ang).astype(np.float32)
    ka = (np.abs(np.fft.fftfreq(E)) + 1e-8).astype(np.float32)
    A12 = np.array([[1,1,1,0,1,1,1,0,0,0,1,1],[1,1,0,1,1,1,0,0,0,1,1,0],
                    [1,0,1,1,1,0,0,0,1,1,0,1],[0,1,1,1,0,0,0,1,1,0,1,1],
                    [0,0,0,0,1,1,1,1,1,1,1,1],[1,1,1,1,0,0,0,0,0,0,0,0],
                    [0,0,0,0,1,1,1,1,0,0,0,0],[1,1,0,0,1,1,0,0,1,1,0,0],
                    [1,0,1,0,1,0,1,0,1,0,1,0],[0,1,0,1,0,1,0,1,0,1,0,1],
                    [1,0,0,1,1,0,0,1,1,0,0,1],[0,1,1,0,0,1,1,0,0,1,1,0]],
                   dtype=np.float32)
    Hm = np.concatenate([A12.T, np.eye(12, dtype=np.float32)], axis=1)
    lx = np.log(np.array([2.0, 4.0, 8.0, 16.0, 32.0], dtype=np.float32))
    return COS, SIN, ka, Hm, lx


def build_nc(ntpb=SEQ // T):
    import ml_dtypes
    nc = bass.Bass("TRN2", target_bir_lowering=False)
    cx = Ctx(nc)

    xh = nc.dram_tensor("xT", [BPC, E, SEQ], F32, kind="ExternalInput").ap()
    oh = nc.dram_tensor("outT", [BPC, E, SEQ], F32, kind="ExternalOutput").ap()
    x00h = nc.dram_tensor("x00row", [1, E], F32, kind="ExternalInput").ap()
    ch = nc.dram_tensor("cT", [E, 4], F32, kind="ExternalInput").ap()
    wmodh = nc.dram_tensor("wmod", [E, 6 * E], F32, kind="ExternalInput").ap()
    welh = nc.dram_tensor("W_el", [E, 24], F32, kind="ExternalInput").ap()
    wleh = nc.dram_tensor("Wle_ext", [25, E], F32, kind="ExternalInput").ap()
    w1h = nc.dram_tensor("W1T", [E, HID], F32, kind="ExternalInput").ap()
    w2h = nc.dram_tensor("W2T", [HID, E], F32, kind="ExternalInput").ap()
    rwh = nc.dram_tensor("RWT", [E, NB], F32, kind="ExternalInput").ap()
    uth = nc.dram_tensor("UT", [E, NB], F32, kind="ExternalInput").ap()
    vh = nc.dram_tensor("V", [NB, E], F32, kind="ExternalInput").ap()
    lngh = nc.dram_tensor("ln_gc", [128, 8], F32, kind="ExternalInput").ap()
    lnbh = nc.dram_tensor("ln_bc", [128, 8], F32, kind="ExternalInput").ap()
    rbh = nc.dram_tensor("router_b", [1, NB], F32, kind="ExternalInput").ap()
    gamh = nc.dram_tensor("gamma_r", [1, E], F32, kind="ExternalInput").ap()
    cosh_ = nc.dram_tensor("COS", [E, E], F32, kind="ExternalInput").ap()
    sinh_ = nc.dram_tensor("SIN", [E, E], F32, kind="ExternalInput").ap()
    kah = nc.dram_tensor("ka_r", [1, E], F32, kind="ExternalInput").ap()
    ka2h = nc.dram_tensor("ka2_r", [1, E], F32, kind="ExternalInput").ap()
    lxh = nc.dram_tensor("lx_r", [1, 5], F32, kind="ExternalInput").ap()
    hth = nc.dram_tensor("HT", [24, 12], F32, kind="ExternalInput").ap()
    wext_h = nc.dram_tensor("wext_scr", [2 * E], BF16, kind="Internal")
    w2s_h = nc.dram_tensor("w2bf_scr", [HID, E], BF16, kind="Internal")

    Jnp = np.zeros((128, 128), dtype=np.float32)
    Jnp[np.arange(128), 127 - np.arange(128)] = 1.0
    I_bf_t = nc.inline_tensor(np.eye(128, dtype=np.float32).astype(ml_dtypes.bfloat16), name="i_bf")
    I_f_t = nc.inline_tensor(np.eye(128, dtype=np.float32), name="i_f")
    J_bf_t = nc.inline_tensor(Jnp.astype(ml_dtypes.bfloat16), name="j_bf")

    A = nc.alloc_sbuf_tensor

    w1t = A("w1t", [128, 8, HID], BF16)
    w2tw = A("w2tw", [128, 2, 4, E], BF16)
    m_sb = A("m_sb", [128, 8, E], BF16)
    wg = A("wg", [128, 8, NB], BF16)
    ut = A("ut", [128, 8, NB], BF16)
    vg = A("vg", [128, 4, E], BF16)
    mwel = A("mwel", [128, 8, 24], BF16)
    welb = A("welb", [128, 8, 24], BF16)
    modt = A("modt", [128, 48, 4], F32)
    wle_sb = A("wle_sb", [25, E], BF16)
    ht_sb = A("ht_sb", [24, 12], BF16)
    idb = A("idb", [128, 128], BF16)
    idf = A("idf", [128, 128], F32)
    jb = A("jb", [128, 128], BF16)
    onecb = A("onecb", [128, 1], BF16)
    onecf = A("onecf", [128, 1], F32)
    onerf = A("onerf", [1, 128], F32)
    cvals = A("cvals", [128, 8], F32)

    xt = A("xt", [128, 8, T], F32)
    xtb = A("xtb", [128, 8, T], BF16)
    sqb = A("sqb", [128, 8, T], BF16)
    rhs1 = A("rhs1", [128, 8, T], BF16)
    x2t = A("x2t", [128, 2, 8, T], F32)
    hffnt = A("hffnt", [128, 2, 8, T], BF16)
    gelut = A("gelut", [128, 2, 4, T], BF16)
    stg32 = A("stg32", [128, 2, NB], F32)
    x3st = A("x3st", [128, 4, T], F32)
    tokA = A("tokA", [128, NB], F32)
    hlzb = A("hlzb", [128, NB], BF16)
    zht = A("zht", [128, 2, 4, T], BF16)
    phib = A("phib", [128, 16], F32)
    colb = A("colb", [128, 16], F32)
    corrb = A("corrb", [25, T], BF16)
    onerb = A("onerb", [1, T], BF16)
    golA = A("golA", [24, T], F32)
    golB = A("golB", [24, T], F32)
    synb = A("synb", [12, T], BF16)

    rows_t = A("rows_t", [1, 4, T], F32)
    rowsb_t = A("rowsb_t", [1, 4, T], BF16)
    RW = RowAlloc(rows_t, 4, T)
    RB = RowAlloc(rowsb_t, 4, T)
    csr = A("csr", [1, NB], BF16)
    cnr = A("cnr", [1, NB], BF16)
    r2x = A("r2x", [2, T], BF16)
    rows2 = A("rows2", [2, BPC, E], BF16)
    mwr2 = A("mwr2", [2, BPC, 32], BF16)
    vecs = A("vecs", [128, 2, 8, 8], F32)
    vecb = A("vecb", [128, 8, 4], BF16)
    lngc = A("lngc", [128, 8], F32)
    lnbcb = A("lnbcb", [128, 8], BF16)
    dsc = A("dsc", [1, 16], F32)
    silc = A("silc", [128, 8, 4], BF16)
    silc32 = A("silc32", [128, 8, 4], F32)
    cstage = A("cstage", [128, 8, 4], F32)
    mcol = A("mcol", [128, 8], F32)
    frcols = A("frcols", [128, 8], F32)
    ficols = A("ficols", [128, 8], F32)
    wcol = A("wcol", [128, 8], F32)
    mwtmp = A("mwtmp", [128, 8, 24], BF16)

    xtf = xt.ap().rearrange("p a b -> p (a b)")
    x2f = x2t.ap().rearrange("p q a b -> p (q a b)")
    sca0 = [xtf[0:1, 0:E], xtf[0:1, E:2 * E]]
    SCAK = [f"xt:{c}" for c in range(8)]
    x00r = x2f[0:1, 0:E]
    h00r = x2f[0:1, E:2 * E]
    gamr = x2f[0:1, 0:E]
    rbr = x2f[0:1, E:E + NB]
    X2RK = ["x2t:0", "x2t:1"]

    pbig = nc.alloc_psum_tensor("pbig", [128, 8, T], F32)
    pout = nc.alloc_psum_tensor("pout", [128, 8, T], F32)

    def BKB(c):
        return ("pb", c // 2)

    def BKO(c):
        return ("po", c // 2)

    pstats = pout.ap()[:, 0, :]
    pmisc = pout.ap()[:, 1, :]
    prho = pout.ap()[:, 2, :]
    pmrho = pout.ap()[:, 3, :]
    ptokv = pout.ap()[:, 4:6, :].rearrange("p a b -> p (a b)")
    pztb = pout.ap()[:, 6, :].bitcast(BF16)
    pcol = pout.ap()[:, 7, :]

    def t_mm(out_ap, lhs, rhs, start, stop, reads, writes):
        cx.op("tensor", lambda e: e.matmul(out_ap, lhs, rhs, start=start, stop=stop),
              reads=reads, writes=writes, pe_skip=True)

    def t_tr(out_ap, in_ap, ident, reads, writes):
        cx.op("tensor", lambda e: e.transpose(out_ap, in_ap, ident),
              reads=reads, writes=writes, pe_skip=True)

    v = lambda fn, r, w: cx.op("vector", fn, reads=r, writes=w)
    s = lambda fn, r, w: cx.op("scalar", fn, reads=r, writes=w)

    def preload_tab(set_id):
        if PRELOADS:
            s(lambda e: e.add_instruction(mybir.InstLoadActFuncSet(
                name=nc.get_next_instruction_name(), act_func_set_id=set_id,
                ins=[], outs=[])), [], [])

    silf = silc.ap().rearrange("p c b -> p (c b)")
    cstf = cstage.ap().rearrange("p c b -> p (c b)")

    cx.dma("sync", "cst", idb[:], I_bf_t.ap()[:], writes=["idb"])
    cx.dma("sync", "cst", idf[:], I_f_t.ap()[:], writes=["idf"])
    cx.dma("sync", "cst", jb[:], J_bf_t.ap()[:], writes=["jb"])
    cx.dma("sync", "cst", lngc[:], lngh[:], writes=["lngc"])
    cx.dma("sync", "cst", x00r, x00h[0:1, :], writes=["x2t:0"])
    lxrow = A("lxrow", [1, 8], F32)
    cx.dma("sync", "cst", lxrow[0:1, 0:5], lxh[0:1, 0:5], writes=["lx"])
    cx.dma("sync", "cst", cstage.ap()[:], ch.rearrange("(c p) b -> p c b", p=128),
           writes=["cstage"])
    cx.dma("gpsimd", "cstg", ht_sb[:], hth[:], writes=["ht_sb"])
    cx.dma("gpsimd", "cstg", wle_sb[:], wleh[:], writes=["wle_sb"])
    cx.dma("gpsimd", "cstg", lnbcb[:], lnbh[:], writes=["lnbcb"])
    cx.dma("gpsimd", "cstg", welb.ap()[:],
           welh.rearrange("(c p) j -> p c j", p=128), writes=["welb"])

    for ci, val in enumerate((1e-5, -0.5, PI, -1.0, PI / 2, -2.0, 0.5, 1.0)):
        v(lambda e, ci=ci, val=val: e.memset(cvals[:, ci:ci + 1], val), [], ["cvals"])
    v(lambda e: e.memset(onecb[:], 1.0), [], ["onecb"])
    v(lambda e: e.memset(onecf[:], 1.0), [], ["onecf"])
    v(lambda e: e.memset(onerf[:], 1.0), [], ["onerf"])
    v(lambda e: e.memset(onerb[:], 1.0), [], ["onerb"])
    cx.dma("gpsimd", "cstg", corrb[24:25, :], onerb[:], reads=["onerb"], writes=["corr24"])
    cx.dma("gpsimd", "cstg", r2x[1:2, :], onerb[:], reads=["onerb"], writes=["r2x"])

    preload_tab(18)
    s(lambda e: e.activation(silf, cstf, AF.Silu), ["cstage"], ["silc"])
    silf32 = silc32.ap().rearrange("p c b -> p (c b)")
    s(lambda e: e.activation(silf32, cstf, AF.Silu), ["cstage"], ["silc32"])

    w2f = w2tw.ap().bitcast(F32)
    xtfl32 = xt.ap().rearrange("p a b -> p (a b)")
    mstages = [
        (w2f[:, 0, :, :].rearrange("p a b -> p (a b)"), ["w2tw:0"], "gpsimd", "wmodA"),
        (w2f[:, 1, :, :].rearrange("p a b -> p (a b)"), ["w2tw:1"], "sync", "wmodB"),
        (xtfl32, [f"xt:{c}" for c in range(8)], "scalar", "wmodC"),
    ]
    msi = 0
    for piece in range(3):
        for dc in range(8):
            stg, skeys, engn, strm = mstages[msi % 3]
            msi += 1
            cx.dma(engn, strm, stg,
                   wmodh[128 * dc:128 * (dc + 1), 2048 * piece:2048 * (piece + 1)],
                   writes=skeys)
            for jc in range(16):
                t_mm(pbig[:, 0, 4 * jc:4 * (jc + 1)],
                     stg[:, 128 * jc:128 * (jc + 1)], silf32[:, 4 * dc:4 * (dc + 1)],
                     start=(dc == 0 and jc == 0), stop=(dc == 7 and jc == 15),
                     reads=skeys + ["silc32"], writes=[BKB(0)])
        for jc in range(16):
            j = 16 * piece + jc
            v(lambda e, j=j, jc=jc: e.tensor_copy(modt[:, j, :],
                                                  pbig[:, 0, 4 * jc:4 * (jc + 1)]),
              [BKB(0)], [f"modt:{j}"])

    MODK = [f"modt:{j}" for j in range(48)]

    for c in range(8):
        for vi, base in ((0, 0), (1, 8)):
            t_tr(pcol[0:1, 0:128], modt[:, base + c, 2:3], idf[:],
                 reads=[f"modt:{base + c}", "idf"], writes=[("po", 3)])
            v(lambda e, vi=vi, c=c: e.tensor_copy(sca0[vi][0:1, 128 * c:128 * (c + 1)],
                                                  pcol[0:1, 0:128]),
              [("po", 3)], SCAK)

    s(lambda e: e.activation(h00r, x00r, AF.Square, accum_out=dsc[0:1, 0:1]),
      ["x2t:0"], ["x2t:0", "dsc"])
    v(lambda e: e.reduce_sum(dsc[0:1, 1:2], x00r, axis=AXX), ["x2t:0"], ["dsc"])
    v(lambda e: e.tensor_scalar(dsc[0:1, 2:3], dsc[0:1, 1:2], 1.0 / E, None, ALU.mult),
      ["dsc"], ["dsc"])
    v(lambda e: e.tensor_tensor(dsc[0:1, 3:4], dsc[0:1, 2:3], dsc[0:1, 2:3], ALU.mult),
      ["dsc"], ["dsc"])
    v(lambda e: e.tensor_scalar(dsc[0:1, 4:5], dsc[0:1, 0:1], 1.0 / E, None, ALU.mult),
      ["dsc"], ["dsc"])
    v(lambda e: e.tensor_tensor(dsc[0:1, 5:6], dsc[0:1, 4:5], dsc[0:1, 3:4], ALU.subtract),
      ["dsc"], ["dsc"])
    preload_tab(6)
    s(lambda e: e.activation(dsc[0:1, 6:7], dsc[0:1, 5:6], AF.Ln, bias=cvals[0:1, 0:1]),
      ["dsc", "cvals"], ["dsc"])
    s(lambda e: e.activation(dsc[0:1, 7:8], dsc[0:1, 6:7], AF.Exp, scale=cvals[0:1, 1:2]),
      ["dsc", "cvals"], ["dsc"])
    v(lambda e: e.tensor_tensor(dsc[0:1, 8:9], dsc[0:1, 2:3], dsc[0:1, 7:8], ALU.mult),
      ["dsc"], ["dsc"])
    v(lambda e: e.tensor_scalar(dsc[0:1, 9:10], dsc[0:1, 8:9], -1.0, None, ALU.mult),
      ["dsc"], ["dsc"])
    v(lambda e: e.tensor_scalar(h00r, x00r, dsc[0:1, 7:8], dsc[0:1, 9:10],
                                ALU.mult, ALU.add),
      ["x2t:0", "dsc"], ["x2t:0"])
    v(lambda e: e.tensor_scalar(sca0[1], sca0[1], 1.0, None, ALU.add),
      SCAK, SCAK)
    v(lambda e: e.tensor_tensor(h00r, h00r, sca0[1], ALU.mult),
      ["x2t:0"] + SCAK, ["x2t:0"])
    v(lambda e: e.tensor_tensor(h00r, h00r, sca0[0], ALU.add),
      ["x2t:0"] + SCAK, ["x2t:0"])

    cnt_r = RW.get("cnts", 8)
    for bi, n in enumerate((4, 16, 64, 256, 1024)):
        v(lambda e, n=n: e.reduce_sum(dsc[0:1, 10:11], h00r[0:1, 0:n], axis=AXX),
          ["x2t:0"], ["dsc"])
        v(lambda e, n=n: e.tensor_scalar(dsc[0:1, 11:12], dsc[0:1, 10:11], 1.0 / n,
                                         None, ALU.mult),
          ["dsc"], ["dsc"])
        v(lambda e, n=n: e.tensor_scalar(x00r[0:1, 0:n], h00r[0:1, 0:n],
                                         dsc[0:1, 11:12], None, ALU.is_gt),
          ["x2t:0", "dsc"], ["x2t:0"])
        v(lambda e, n=n, bi=bi: e.reduce_sum(cnt_r[0:1, bi:bi + 1], x00r[0:1, 0:n],
                                             axis=AXX),
          ["x2t:0"], ["cnts"])
    v(lambda e: e.tensor_scalar(cnt_r[0:1, 0:5], cnt_r[0:1, 0:5], 1.0, None, ALU.max),
      ["cnts"], ["cnts"])
    ly_r = RW.get("ly", 8)
    s(lambda e: e.activation(ly_r[0:1, 0:5], cnt_r[0:1, 0:5], AF.Ln), ["cnts"], ["ly"])
    lx_r = lxrow
    wk = RW.get("dwork")
    v(lambda e: e.tensor_tensor(wk[0:1, 0:5], lx_r[0:1, 0:5], ly_r[0:1, 0:5], ALU.mult),
      ["lx", "ly"], ["dw"])
    v(lambda e: e.tensor_tensor(wk[0:1, 5:10], lx_r[0:1, 0:5], lx_r[0:1, 0:5], ALU.mult),
      ["lx", "dw"], ["dw"])
    v(lambda e: e.reduce_sum(wk[0:1, 10:11], wk[0:1, 0:5], axis=AXX), ["dw"], ["dw"])
    v(lambda e: e.reduce_sum(wk[0:1, 11:12], lx_r[0:1, 0:5], axis=AXX), ["lx", "dw"], ["dw"])
    v(lambda e: e.reduce_sum(wk[0:1, 12:13], ly_r[0:1, 0:5], axis=AXX), ["ly", "dw"], ["dw"])
    v(lambda e: e.reduce_sum(wk[0:1, 13:14], wk[0:1, 5:10], axis=AXX), ["dw"], ["dw"])
    v(lambda e: e.tensor_tensor(wk[0:1, 14:15], wk[0:1, 11:12], wk[0:1, 12:13], ALU.mult),
      ["dw"], ["dw"])
    v(lambda e: e.tensor_scalar(wk[0:1, 15:16], wk[0:1, 10:11], 5.0, None, ALU.mult),
      ["dw"], ["dw"])
    v(lambda e: e.tensor_tensor(wk[0:1, 16:17], wk[0:1, 15:16], wk[0:1, 14:15], ALU.subtract),
      ["dw"], ["dw"])
    v(lambda e: e.tensor_tensor(wk[0:1, 17:18], wk[0:1, 11:12], wk[0:1, 11:12], ALU.mult),
      ["dw"], ["dw"])
    v(lambda e: e.tensor_scalar(wk[0:1, 18:19], wk[0:1, 13:14], 5.0, None, ALU.mult),
      ["dw"], ["dw"])
    v(lambda e: e.tensor_tensor(wk[0:1, 19:20], wk[0:1, 18:19], wk[0:1, 17:18], ALU.subtract),
      ["dw"], ["dw"])
    v(lambda e: e.reciprocal(wk[0:1, 20:21], wk[0:1, 19:20]), ["dw"], ["dw"])
    v(lambda e: e.tensor_tensor(wk[0:1, 21:22], wk[0:1, 16:17], wk[0:1, 20:21], ALU.mult),
      ["dw"], ["dw"])
    v(lambda e: e.tensor_scalar(wk[0:1, 22:23], wk[0:1, 21:22], -1.0, 1.0, ALU.mult, ALU.max),
      ["dw"], ["dw"])
    v(lambda e: e.tensor_scalar(dsc[0:1, 0:1], wk[0:1, 22:23], 2.0, None, ALU.min),
      ["dw"], ["dsc"])
    v(lambda e: e.tensor_scalar(dsc[0:1, 1:2], dsc[0:1, 0:1], 0.2, 1.0, ALU.mult, ALU.add),
      ["dsc"], ["dsc"])
    v(lambda e: e.tensor_tensor(dsc[0:1, 2:3], dsc[0:1, 1:2], dsc[0:1, 0:1], ALU.mult),
      ["dsc"], ["dsc"])
    v(lambda e: e.tensor_scalar(dsc[0:1, 3:4], dsc[0:1, 0:1], 0.5, 1.0, ALU.mult, ALU.add),
      ["dsc"], ["dsc"])
    v(lambda e: e.tensor_scalar(dsc[0:1, 4:5], dsc[0:1, 0:1], 0.1, None, ALU.mult),
      ["dsc"], ["dsc"])
    RW.drop("cnts", "ly", "dwork")

    ka_row = sca0[0]
    ka2_row = sca0[1]
    cx.dma("sync", "cst2", ka_row, kah[0:1, :], reads=SCAK, writes=SCAK)
    cx.dma("sync", "cst2", ka2_row, ka2h[0:1, :], reads=SCAK, writes=SCAK)
    arg_row = x00r
    amp_row = h00r
    v(lambda e: e.tensor_scalar(arg_row, ka_row, dsc[0:1, 3:4], dsc[0:1, 2:3],
                                ALU.mult, ALU.add),
      SCAK + ["dsc", "x2t:0"], ["x2t:0"])
    preload_tab(9)
    s(lambda e: e.activation(amp_row, arg_row, AF.Sin, bias=cvals[0:1, 2:3], scale=cvals[0:1, 3:4]),
      ["x2t:0", "cvals"], ["x2t:0"])
    v(lambda e: e.tensor_scalar(arg_row, ka2_row, dsc[0:1, 4:5], dsc[0:1, 2:3],
                                ALU.mult, ALU.add),
      SCAK + ["dsc", "x2t:0"], ["x2t:0"])
    v(lambda e: e.tensor_tensor(arg_row, arg_row, ka_row, ALU.subtract),
      SCAK + ["x2t:0"], ["x2t:0"])
    s(lambda e: e.activation(sca0[0], arg_row, AF.Sin, bias=cvals[0:1, 4:5], scale=cvals[0:1, 3:4]),
      ["x2t:0", "cvals"] + SCAK, SCAK)
    s(lambda e: e.activation(sca0[1], arg_row, AF.Sin),
      ["x2t:0"] + SCAK, SCAK)
    v(lambda e: e.tensor_tensor(sca0[0], sca0[0], amp_row, ALU.mult),
      SCAK + ["x2t:0"], SCAK)
    v(lambda e: e.tensor_tensor(sca0[1], sca0[1], amp_row, ALU.mult),
      SCAK + ["x2t:0"], SCAK)
    v(lambda e: e.tensor_scalar(sca0[1], sca0[1], -1.0, None, ALU.mult),
      SCAK, SCAK)
    for j in range(8):
        t_tr(pcol[:, 0:1], sca0[0][0:1, 128 * j:128 * (j + 1)], idf[0:1, 0:1],
             reads=SCAK + ["idf"], writes=[("po", 3)])
        v(lambda e, j=j: e.tensor_copy(frcols[:, j:j + 1], pcol[:, 0:1]),
          [("po", 3)], ["frcols"])
        t_tr(pcol[:, 0:1], sca0[1][0:1, 128 * j:128 * (j + 1)], idf[0:1, 0:1],
             reads=SCAK + ["idf"], writes=[("po", 3)])
        v(lambda e, j=j: e.tensor_copy(ficols[:, j:j + 1], pcol[:, 0:1]),
          [("po", 3)], ["ficols"])

    first_w = True
    xtfl = xt.ap().rearrange("p a b -> p (a b)")
    wstg = [(stg32.ap()[:, 0, :], "stg32:0"), (stg32.ap()[:, 1, :], "stg32:1"),
            (xtfl[:, 0:512], "xt:0"), (xtfl[:, 512:1024], "xt:2"),
            (xtfl[:, 1024:1536], "xt:4"), (xtfl[:, 1536:2048], "xt:6")]
    wsi = 0
    for kc in range(8):
        for hf in range(2):
            stgc, skey = wstg[wsi % 6]
            cx.dma("scalar", f"cosin{wsi % 6}", stgc,
                   cosh_[128 * kc:128 * (kc + 1), 512 * hf:512 * (hf + 1)], writes=[skey])
            for ms in range(4):
                mc = 4 * hf + ms
                t_mm(pbig[:, 1, mc:mc + 1], stgc[:, 128 * ms:128 * (ms + 1)],
                     frcols[:, kc:kc + 1],
                     start=first_w, stop=False,
                     reads=[skey, "frcols"], writes=[BKB(0)])
                first_w = False
            wsi += 1
            stgc, skey = wstg[wsi % 6]
            cx.dma("scalar", f"cosin{wsi % 6}", stgc,
                   sinh_[128 * kc:128 * (kc + 1), 512 * hf:512 * (hf + 1)], writes=[skey])
            for ms in range(4):
                mc = 4 * hf + ms
                t_mm(pbig[:, 1, mc:mc + 1], stgc[:, 128 * ms:128 * (ms + 1)],
                     ficols[:, kc:kc + 1],
                     start=False, stop=(kc == 7 and hf == 1 and ms == 3),
                     reads=[skey, "ficols"], writes=[BKB(0)])
            wsi += 1
    v(lambda e: e.tensor_scalar(wcol[:], pbig[:, 1, 0:8], 1.0 / E, None, ALU.mult),
      [BKB(0)], ["wcol"])
    wcolb = A("wcolb", [128, 8], BF16)
    v(lambda e: e.tensor_copy(wcolb[:], wcol[:]), ["wcol"], ["wcolb"])

    for c in range(8):
        cx.dma("sync", "wex", AP(wext_h, 128 * c, [[1, 128]]), wcolb[:, c:c + 1],
               reads=["wcolb"], writes=[f"wext:{2 * c}"])
        cx.dma("sync", "wex", AP(wext_h, E + 128 * c, [[1, 128]]), wcolb[:, c:c + 1],
               reads=["wcolb"], writes=[f"wext:{2 * c + 1}"])
    WEXTK = [f"wext:{i}" for i in range(16)]
    xtbfl = xtb.ap().rearrange("p a b -> p (a b)")
    sqbfl = sqb.ap().rearrange("p a b -> p (a b)")
    MREV_ENG = ["sync", "scalar", "gpsimd", "scalar"]

    def mslot(i):
        s = i % 4
        if s < 2:
            return xtbfl[:, 1024 * s:1024 * (s + 1)], f"xtbh{s}"
        return sqbfl[:, 1024 * (s - 2):1024 * (s - 1)], f"sqbh{s - 2}"

    for g in range(8):
        gp = g % 2
        stgm, mkey = mslot(g)
        ch0 = 2 + 4 * gp
        ch1 = 4 - 4 * gp
        cx.dma(MREV_ENG[g % 4], f"mrev{g % 4}", stgm,
               AP(wext_h, 897 - 128 * g, [[1, 128], [1, E]]),
               reads=WEXTK, writes=[mkey])
        t_mm(pbig[:, ch0:ch0 + 2, :].rearrange("p a b -> p (a b)"), jb[:], stgm[:, 0:512],
             start=True, stop=True, reads=[mkey, "jb"], writes=[BKB(ch0)])
        t_mm(pbig[:, ch1:ch1 + 2, :].rearrange("p a b -> p (a b)"), jb[:], stgm[:, 512:1024],
             start=True, stop=True, reads=[mkey, "jb"], writes=[BKB(ch1)])
        v(lambda e, g=g, ch0=ch0: e.tensor_copy(
                m_sb[:, g, 0:512], pbig[:, ch0:ch0 + 2, :].rearrange("p a b -> p (a b)")),
          [BKB(ch0)], [f"m:{g}"])
        v(lambda e, g=g, ch1=ch1: e.tensor_copy(
                m_sb[:, g, 512:1024], pbig[:, ch1:ch1 + 2, :].rearrange("p a b -> p (a b)")),
          [BKB(ch1)], [f"m:{g}"])
    first = True
    for g in range(8):
        stgm, mkey = mslot(g)
        cx.dma(MREV_ENG[g % 4], f"mrev{g % 4}", stgm,
               AP(wext_h, 1 + 128 * g, [[1, 128], [1, E]]),
               reads=WEXTK, writes=[mkey])
        for q in range(8):
            t_mm(pbig[:, 6 + q // 4, 24 * (q % 4):24 * (q % 4) + 24],
                 stgm[:, 128 * q:128 * (q + 1)], welb[:, g, :],
                 start=first, stop=(g == 7 and q == 7),
                 reads=[mkey, "welb"], writes=[BKB(6)])
            first = False
    for q in range(8):
        v(lambda e, q=q: e.tensor_copy(mwtmp[:, q, :],
                                       pbig[:, 6 + q // 4, 24 * (q % 4):24 * (q % 4) + 24]),
          [BKB(6)], ["mwtmp"])
    for c in range(8):
        t_mm(pbig[:, 0, 0:24], jb[:], mwtmp[:, 7 - c, :], start=True, stop=True,
             reads=["mwtmp", "jb"], writes=[BKB(0)])
        v(lambda e, c=c: e.tensor_copy(mwel[:, c, :], pbig[:, 0, 0:24]),
          [BKB(0)], ["mwel"])
    MK = [f"m:{g}" for g in range(8)]

    for b in range(BPC):
        for c in range(8):
            v(lambda e, b=b, c=c: e.tensor_copy(vecb[:, c, 2 * b:2 * b + 1],
                                                modt[:, 0 + c, b:b + 1]),
              [f"modt:{c}"], ["vecb"])
            v(lambda e, b=b, c=c: e.tensor_scalar(vecb[:, c, 2 * b + 1:2 * b + 2],
                                                  modt[:, 8 + c, b:b + 1], 1.0, None, ALU.add),
              [f"modt:{8 + c}"], ["vecb"])
    for b in range(BPC):
        for vi, dst in ((0, 1), (1, 0)):
            for half in range(2):
                for mc in range(8):
                    t_mm(pbig[0:1, 2 * half:2 * half + 2, :].rearrange("p a b -> p (a b)"),
                         vecb[:, mc, 2 * b + vi:2 * b + vi + 1],
                         m_sb[:, mc, 512 * half:512 * (half + 1)],
                         start=(mc == 0), stop=(mc == 7),
                         reads=[f"m:{mc}", "vecb"], writes=[BKB(2 * half)])
            if dst == 0:
                for half in range(2):
                    v(lambda e, b=b, half=half: e.tensor_copy(
                            rows2[0:1, b, 512 * half:512 * (half + 1)],
                            pbig[0:1, 2 * half:2 * half + 2, :].rearrange("p a b -> p (a b)")),
                      [BKB(2 * half)], [f"rows2:{b}"])
            else:
                for half in range(2):
                    v(lambda e, half=half: e.tensor_copy(
                            tokA[0:1, 0:512],
                            pbig[0:1, 2 * half:2 * half + 2, :].rearrange("p a b -> p (a b)")),
                      [BKB(2 * half)], ["tokA"])
                    cx.dma("gpsimd", "cstg", rows2[1:2, b, 512 * half:512 * (half + 1)],
                           tokA[0:1, 0:512], reads=["tokA"], writes=[f"rows2:{b}"])
        for vi, dst in ((0, 1), (1, 0)):
            for mc in range(8):
                t_mm(pbig[0:1, 4, 0:24], vecb[:, mc, 2 * b + vi:2 * b + vi + 1],
                     mwel[:, mc, :],
                     start=(mc == 0), stop=(mc == 7),
                     reads=["mwel", "vecb"], writes=[BKB(4)])
            if dst == 0:
                v(lambda e, b=b: e.tensor_copy(mwr2[0:1, b, 0:24], pbig[0:1, 4, 0:24]),
                  [BKB(4)], [f"mwr2:{b}"])
            else:
                v(lambda e: e.tensor_copy(tokA[0:1, 0:24], pbig[0:1, 4, 0:24]),
                  [BKB(4)], ["tokA"])
                cx.dma("gpsimd", "cstg", mwr2[1:2, b, 0:24], tokA[0:1, 0:24],
                       reads=["tokA"], writes=[f"mwr2:{b}"])

    cx.dma("sync", "cst3", rbr, rbh[0:1, :], reads=["x2t:0"], writes=["x2t:0"])

    for dc in range(8):
        par = dc % 2
        stgb = stg32.ap()[:, par, 0:256].bitcast(BF16)
        skey = f"stg32:{par}"
        cx.dma("gpsimd", f"rw{par}", stgb, rwh[128 * dc:128 * (dc + 1), :], writes=[skey])
        v(lambda e, dc=dc, stgb=stgb: e.tensor_scalar(
                wg[:, dc, :], stgb, lngc[:, dc:dc + 1], None, ALU.mult),
          [skey, "lngc"], [f"wg:{dc}"])
    WGK = [f"wg:{dc}" for dc in range(8)]
    for half in range(2):
        for dc in range(8):
            t_mm(pbig[0:1, 2 + half, :], onecb[:], wg[:, dc, 256 * half:256 * (half + 1)],
                 start=(dc == 0 and half == 0), stop=(dc == 7 and half == 1),
                 reads=[f"wg:{dc}", "onecb"], writes=[BKB(2)])
    for half in range(2):
        v(lambda e, half=half: e.tensor_copy(csr[0:1, 256 * half:256 * (half + 1)],
                                             pbig[0:1, 2 + half, :]),
          [BKB(2)], ["csr"])
    for half in range(2):
        for dc in range(8):
            par = dc % 2
            stgb = stg32.ap()[:, par, 0:128].bitcast(BF16)
            skey = f"stg32:{par}"
            cx.dma("gpsimd", f"rw2{par}", stgb,
                   rwh[128 * dc:128 * (dc + 1), 256 * half:256 * (half + 1)],
                   writes=[skey])
            t_mm(pbig[0:1, 4 + half, :], lnbcb[:, dc:dc + 1], stgb,
                 start=(dc == 0 and half == 0), stop=(dc == 7 and half == 1),
                 reads=[skey, "lnbcb"], writes=[BKB(4)])
    for half in range(2):
        v(lambda e, half=half: e.tensor_tensor(
                tokA[0:1, 256 * half:256 * (half + 1)], pbig[0:1, 4 + half, :],
                rbr[0:1, 256 * half:256 * (half + 1)], ALU.add),
          [BKB(4), "x2t:0"], ["tokA"])
    v(lambda e: e.tensor_copy(cnr[0:1, :], tokA[0:1, 0:NB]), ["tokA"], ["cnr"])

    for dc in range(8):
        cx.dma("gpsimd", "ut0", ut[:, dc, :], uth[128 * dc:128 * (dc + 1), :],
               writes=[f"ut:{dc}"])
    usq = gelut.ap()[:, 0, 0:2, :].rearrange("p a b -> p (a b)")
    USQK = ["gelut:0:0", "gelut:0:1"]
    first = True
    for dc in range(8):
        s(lambda e, dc=dc: e.activation(usq, ut[:, dc, :], AF.Square),
          [f"ut:{dc}"], USQK)
        t_mm(pbig[0:1, 0, :], onecb[:], usq[:, 0:256], start=first, stop=False,
             reads=USQK + ["onecb"], writes=[BKB(0)])
        t_mm(pbig[0:1, 1, :], onecb[:], usq[:, 256:512], start=False, stop=(dc == 7),
             reads=USQK + ["onecb"], writes=[BKB(0)])
        first = False
    un_r = RW.get("unrm", T)
    first = True
    preload_tab(6)
    for half in range(2):
        s(lambda e, half=half: e.activation(un_r, pbig[0:1, half, :], AF.Ln),
          [BKB(0)], ["unr"])
        s(lambda e: e.activation(un_r, un_r, AF.Exp, scale=cvals[0:1, 6:7]), ["unr", "cvals"], ["unr"])
        v(lambda e: e.tensor_scalar(un_r, un_r, 1e-6, None, ALU.max), ["unr"], ["unr"])
        v(lambda e: e.reciprocal(un_r, un_r), ["unr"], ["unr"])
        t_mm(pbig[:, 2 + half, :], onerf[:], un_r, start=first, stop=(half == 1),
             reads=["unr", "onerf"], writes=[BKB(2)])
        first = False
    for dc in range(8):
        v(lambda e, dc=dc: e.scalar_tensor_tensor(
                ut[:, dc, :], ut[:, dc, :], 1.0,
                pbig[:, 2:4, :].rearrange("p a b -> p (a b)"), ALU.mult, ALU.mult),
          [f"ut:{dc}", BKB(2)], [f"ut:{dc}"])
    RW.drop("unrm")
    UTK = [f"ut:{dc}" for dc in range(8)]

    cx.dma("sync", "cst4", gamr, gamh[0:1, :], reads=["x2t:0"], writes=["x2t:0"])
    for half in range(2):
        t_mm(pbig[:, 4 + 2 * half:6 + 2 * half, :].rearrange("p a b -> p (a b)"),
             onerf[:], gamr[0:1, 512 * half:512 * (half + 1)],
             start=True, stop=True,
             reads=["x2t:0", "onerf"], writes=[BKB(4) if half == 0 else BKB(6)])
    vst = gelut.ap()[:, 1, :, :].rearrange("p a b -> p (a b)")
    VSTK = [f"gelut:1:{j}" for j in range(4)]
    for vt in range(4):
        cx.dma("gpsimd", "v0", vst, vh[128 * vt:128 * (vt + 1), :], writes=VSTK)
        s(lambda e: e.activation(usq, vst[:, 0:512], AF.Square, accum_out=mcol[:, 2:3]),
          VSTK, USQK + ["mcol"])
        s(lambda e: e.activation(usq, vst[:, 512:1024], AF.Square, accum_out=mcol[:, 3:4]),
          VSTK, USQK + ["mcol"])
        v(lambda e: e.tensor_tensor(mcol[:, 4:5], mcol[:, 2:3], mcol[:, 3:4], ALU.add),
          ["mcol"], ["mcol"])
        s(lambda e: e.activation(mcol[:, 5:6], mcol[:, 4:5], AF.Ln), ["mcol"], ["mcol"])
        s(lambda e: e.activation(mcol[:, 5:6], mcol[:, 5:6], AF.Exp, scale=cvals[:, 6:7]),
          ["mcol", "cvals"], ["mcol"])
        v(lambda e: e.tensor_scalar(mcol[:, 5:6], mcol[:, 5:6], 1e-6, None, ALU.max),
          ["mcol"], ["mcol"])
        v(lambda e: e.reciprocal(mcol[:, 6:7], mcol[:, 5:6]), ["mcol"], ["mcol"])
        v(lambda e, vt=vt: e.scalar_tensor_tensor(
                vg[:, vt, :], vst, mcol[:, 6:7],
                pbig[:, 4:8, :].rearrange("p a b -> p (a b)"), ALU.mult, ALU.mult),
          VSTK + ["mcol", BKB(4), BKB(6)], [f"vg:{vt}"])
    VGK = [f"vg:{vt}" for vt in range(4)]

    def emit_weight_tail():
        for dc in range(8):
            cx.dma("gpsimd", "w1", w1t[:, dc, :], w1h[128 * dc:128 * (dc + 1), :],
                   reads=(["mwel"] if dc == 0 else []), writes=[f"w1t:{dc}"])
        for hc in range(32):
            cx.dma("gpsimd", "w2cv", AP(w2s_h, 128 * E * hc, [[E, 128], [1, E]]),
                   w2h[128 * hc:128 * (hc + 1), :], writes=[f"w2scr:{hc}"])

    VL = {"a": 0, "cb1": 1, "ncb1": 2, "ga": 3, "gb": 4, "t0": 5, "t1": 6}

    aSR1 = pbig.ap()[0:1, 6, :]
    aSR2 = pbig.ap()[32:33, 6, :]
    aPRHO1 = pbig.ap()[:, 7, :]
    aLAT = pbig.ap()[0:24, 6, :]
    aSYN = pbig.ap()[32:44, 6, :]
    aWS = pbig.ap()[64:65, 6, :]
    aFIX = pbig.ap()[0:12, 7, :]
    aPRHO2 = pbig.ap()[:, 6, :]
    aPMRHO = pbig.ap()[:, 7, :]
    aTOK = pbig.ap()[:, 4:6, :].rearrange("p a b -> p (a b)")
    aZT = pbig.ap()[:, 6, :].bitcast(BF16)
    aZT2 = pbig.ap()[:, 4, :].bitcast(BF16)
    aCOL = pbig.ap()[:, 7, :]
    BK2, BK3 = ("pb", 2), ("pb", 3)

    def batch_prep(b, par):
        kb = f"vecs:{par}"
        v(lambda e: e.tensor_scalar(vecs[:, par, VL["a"], :], modt[:, 8:16, b], 1.0, None, ALU.add),
          MODK, [kb])
        v(lambda e: e.tensor_scalar(vecs[:, par, VL["cb1"], :], modt[:, 32:40, b], 1.0, None, ALU.add),
          MODK, [kb])
        v(lambda e: e.tensor_scalar(vecs[:, par, VL["ncb1"], :], vecs[:, par, VL["cb1"], :], -1.0,
                                    None, ALU.mult),
          [kb], [kb])
        for nm, base in (("ga", 16), ("gb", 40)):
            s(lambda e, base=base: e.activation(vecs[:, par, VL["t0"], :], modt[:, base:base + 8, b],
                                                AF.Abs),
              MODK, [kb])
            v(lambda e: e.tensor_scalar(vecs[:, par, VL["t0"], :], vecs[:, par, VL["t0"], :], 0.25,
                                        None, ALU.max),
              [kb], [kb])
            v(lambda e, base=base: e.tensor_scalar(vecs[:, par, VL["t1"], :],
                                                   modt[:, base:base + 8, b], 0.0, None, ALU.is_ge),
              MODK, [kb])
            v(lambda e: e.tensor_scalar(vecs[:, par, VL["t1"], :], vecs[:, par, VL["t1"], :], 2.0, -1.0,
                                        ALU.mult, ALU.add),
              [kb], [kb])
            v(lambda e, nm=nm: e.tensor_tensor(vecs[:, par, VL[nm], :], vecs[:, par, VL["t0"], :],
                                               vecs[:, par, VL["t1"], :], ALU.mult),
              [kb], [kb])

    n_tiles = BPC * ntpb

    def emit_xin(ti):
        if ti >= n_tiles:
            return
        b = ti // ntpb
        t0 = (ti % ntpb) * T
        par = ti % 2
        for c in range(8):
            cx.dma("sync", "xin", xt[:, c, :], xh[b, 128 * c:128 * (c + 1), t0:t0 + T],
                   writes=[f"xt:{c}"])
        xtbw = ["xtb"] + (["xtbh0", "xtbh1"] if ti == 0 else [])
        if ti == 1:
            v(lambda e: e.tensor_copy(xtb.ap().rearrange("p a b -> p (a b)"),
                                      xt.ap().rearrange("p a b -> p (a b)")),
              [f"xt:{c}" for c in range(8)], xtbw)
            return
        for c in range(8):
            cx.dma("gpsimd", f"xinb{par}", xtb[:, c, :],
                   xh[b, 128 * c:128 * (c + 1), t0:t0 + T], writes=xtbw)

    def phaseA(ti):
        b = ti // ntpb
        t0 = (ti % ntpb) * T
        par = ti % 2
        KB = [f"vecs:{b % 2}"]
        X2W = f"x2t:{par}"
        if ti % ntpb == 0:
            batch_prep(b, b % 2)
        yield
        XK = [f"xt:{c}" for c in range(8)]
        yield

        sqbw = ["sqb"] + (["sqbh0", "sqbh1"] if ti == 0 else [])
        s(lambda e: e.activation(sqb.ap().rearrange("p a b -> p (a b)"),
                                 xtb.ap().rearrange("p a b -> p (a b)"), AF.Square),
          ["xtb"], sqbw)
        for c in range(8):
            t_mm(aSR1, onecb[:], xtb[:, c, :], start=(c == 0), stop=(c == 7),
                 reads=["xtb", "onecb"], writes=[BK3])
        for c in range(8):
            t_mm(aSR2, onecb[:], sqb[:, c, :], start=(c == 0), stop=(c == 7),
                 reads=["sqb", "onecb"], writes=[BK3])
        yield
        r_mu = RW.get("r_mu")
        r_rho = RW.get("r_rho")
        tv = RW.get("r_tv")
        v(lambda e: e.tensor_scalar(r_mu, aSR1, 1.0 / E, None, ALU.mult), [BK3], ["r_mu"])
        v(lambda e: e.tensor_scalar(tv, aSR2, 1.0 / E, None, ALU.mult), [BK3], ["r_tv"])
        v(lambda e: e.tensor_tensor(r_rho, r_mu, r_mu, ALU.mult), ["r_mu"], ["r_rho"])
        v(lambda e: e.tensor_tensor(tv, tv, r_rho, ALU.subtract), ["r_tv", "r_rho"], ["r_tv"])
        preload_tab(6)
        s(lambda e: e.activation(tv, tv, AF.Ln, bias=cvals[0:1, 0:1]), ["r_tv", "cvals"], ["r_tv"])
        s(lambda e: e.activation(r_rho, tv, AF.Exp, scale=cvals[0:1, 1:2]), ["r_tv", "cvals"], ["r_rho"])
        preload_tab(10)
        v(lambda e: e.tensor_tensor(tv, r_mu, r_rho, ALU.mult), ["r_mu", "r_rho"], ["r_tv"])
        v(lambda e: e.tensor_scalar(tv, tv, -1.0, None, ALU.mult), ["r_tv"], ["r_tv"])
        v(lambda e: e.tensor_copy(r2x[0:1, :], tv), ["r_tv"], ["r2x"])
        rhob = RB.get("rhob")
        v(lambda e: e.tensor_copy(rhob, r_rho), ["r_rho"], ["rhob"])
        t_mm(aPRHO1, onerb[0:1, 0:128], rhob, start=True, stop=True,
             reads=["rhob", "onerb"], writes=[BK3])
        for c in range(8):
            v(lambda e, c=c: e.scalar_tensor_tensor(rhs1[:, c, :], xtb[:, c, :],
                                                    vecs[:, b % 2, VL["a"], c:c + 1], aPRHO1,
                                                    ALU.mult, ALU.mult),
              ["xtb", BK3] + KB, [f"rhs1:{c}"])
        yield

        for mc in range(8):
            t_mm(aLAT, mwel[:, mc, :], rhs1[:, mc, :],
                 start=(mc == 0), stop=False, reads=["mwel", f"rhs1:{mc}"], writes=[BK3])
        t_mm(aLAT, mwr2[:, b, 0:24], r2x[:], start=False, stop=True,
             reads=[f"mwr2:{b}", "r2x"], writes=[BK3])
        v(lambda e: e.tensor_scalar(corrb[0:24, :], aLAT, 0.0, None, ALU.is_gt),
          [BK3], ["corrb"])
        t_mm(aSYN, ht_sb[:], corrb[0:24, :], start=True, stop=True,
             reads=["corrb", "ht_sb"], writes=[BK3])
        v(lambda e: e.tensor_scalar(synb[:], aSYN, 0.5, None, ALU.is_gt), [BK3], ["synb"])
        t_mm(aWS, onecb[0:12, :], synb[:], start=True, stop=True,
             reads=["synb", "onecb"], writes=[BK3])
        fx = RW.get("r_fx")
        v(lambda e: e.tensor_scalar(fx, aWS, 3.5, None, ALU.is_le), [BK3], ["r_fx"])
        v(lambda e: e.scalar_tensor_tensor(fx, aWS, 0.5, fx, ALU.is_ge, ALU.mult),
          [BK3, "r_fx"], ["r_fx"])
        t_mm(aFIX, onerf[0:1, 0:12], fx, start=True, stop=True,
             reads=["r_fx", "onerf"], writes=[BK3])
        v(lambda e: e.tensor_tensor(golA[0:12, :], corrb[0:12, :], synb[:], ALU.mult),
          ["corrb", "synb"], ["golA"])
        v(lambda e: e.scalar_tensor_tensor(golA[0:12, :], golA[0:12, :], -2.0, synb[:],
                                           ALU.mult, ALU.add),
          ["golA", "synb"], ["golA"])
        v(lambda e: e.tensor_tensor(golA[0:12, :], golA[0:12, :], aFIX, ALU.mult),
          ["golA", BK3], ["golA"])
        v(lambda e: e.tensor_tensor(corrb[0:12, :], corrb[0:12, :], golA[0:12, :], ALU.add),
          ["golA", "corrb"], ["corrb"])
        yield

        for half in range(2):
            for cc in range(4):
                nc_ = 4 * half + cc
                for mc in range(8):
                    t_mm(pbig[:, 4 + cc, :], m_sb[:, mc, 128 * nc_:128 * (nc_ + 1)],
                         rhs1[:, mc, :],
                         start=(mc == 0 and cc % 2 == 0), stop=False,
                         reads=[f"m:{mc}", f"rhs1:{mc}"], writes=[pbank("x", 4 + cc) if False else ("pb", (4 + cc) // 2)])
                t_mm(pbig[:, 4 + cc, :], rows2[:, b, 128 * nc_:128 * (nc_ + 1)], r2x[:],
                     start=False, stop=False,
                     reads=[f"rows2:{b}", "r2x"], writes=[("pb", (4 + cc) // 2)])
            for cc in range(4):
                nc_ = 4 * half + cc
                t_mm(pbig[:, 4 + cc, :], wle_sb[:, 128 * nc_:128 * (nc_ + 1)], corrb[:],
                     start=False, stop=(cc % 2 == 1),
                     reads=["corrb", "corr24", "wle_sb"], writes=[("pb", (4 + cc) // 2)])
            for cc in range(4):
                nc_ = 4 * half + cc
                v(lambda e, nc_=nc_, cc=cc: e.scalar_tensor_tensor(
                        x2t[:, par, nc_, :], pbig[:, 4 + cc, :],
                        vecs[:, b % 2, VL["ga"], nc_:nc_ + 1], xt[:, nc_, :],
                        ALU.mult, ALU.add),
                  [("pb", (4 + cc) // 2)] + KB + XK, [X2W])
            yield

        s(lambda e: e.activation(sqb.ap().rearrange("p a b -> p (a b)"),
                                 x2t.ap()[:, par, :, :].rearrange("p a b -> p (a b)"),
                                 AF.Square),
          [X2W], ["sqb"])
        v(lambda e: e.tensor_copy(xtb.ap().rearrange("p a b -> p (a b)"),
                                  x2t.ap()[:, par, :, :].rearrange("p a b -> p (a b)")),
          [X2W] + [f"rhs1:{c}" for c in range(8)], ["xtb"])
        for c in range(8):
            t_mm(aSR1, onecb[:], xtb[:, c, :], start=(c == 0), stop=(c == 7),
                 reads=["xtb", "onecb"], writes=[BK3])
        for c in range(8):
            t_mm(aSR2, onecb[:], sqb[:, c, :], start=(c == 0), stop=(c == 7),
                 reads=["sqb", "onecb"], writes=[BK3])
        yield
        r_mu = RW.get("r_mu")
        r_rho = RW.get("r_rho")
        v(lambda e: e.tensor_scalar(r_mu, aSR1, 1.0 / E, None, ALU.mult), [BK3], ["r_mu"])
        v(lambda e: e.tensor_scalar(tv, aSR2, 1.0 / E, None, ALU.mult), [BK3], ["r_tv"])
        v(lambda e: e.tensor_tensor(r_rho, r_mu, r_mu, ALU.mult), ["r_mu"], ["r_rho"])
        v(lambda e: e.tensor_tensor(tv, tv, r_rho, ALU.subtract), ["r_tv", "r_rho"], ["r_tv"])
        preload_tab(6)
        s(lambda e: e.activation(tv, tv, AF.Ln, bias=cvals[0:1, 0:1]), ["r_tv", "cvals"], ["r_tv"])
        s(lambda e: e.activation(r_rho, tv, AF.Exp, scale=cvals[0:1, 1:2]), ["r_tv", "cvals"], ["r_rho"])
        preload_tab(10)
        v(lambda e: e.tensor_tensor(r_mu, r_mu, r_rho, ALU.mult), ["r_mu", "r_rho"], ["r_mu"])
        v(lambda e: e.tensor_copy(rhob, r_rho), ["r_rho"], ["rhob"])
        mrhob = RB.get("mrhob")
        v(lambda e: e.tensor_copy(mrhob, r_mu), ["r_mu"], ["mrhob"])
        t_mm(aPRHO2, onerb[0:1, 0:128], rhob, start=True, stop=False,
             reads=["rhob", "onerb"], writes=[BK3])
        t_mm(aPMRHO, onerb[0:1, 0:128], mrhob, start=False, stop=True,
             reads=["mrhob", "onerb"], writes=[BK3])
        for c in range(8):
            v(lambda e, c=c: e.scalar_tensor_tensor(sqb[:, c, :], xtb[:, c, :],
                                                    vecs[:, b % 2, VL["cb1"], c:c + 1], aPRHO2,
                                                    ALU.mult, ALU.mult),
              ["xtb", BK3] + KB, ["sqb"])
            v(lambda e, c=c: e.tensor_scalar(stg32[:, 0, 0:T], aPMRHO,
                                             vecs[:, b % 2, VL["ncb1"], c:c + 1],
                                             modt[:, 24 + c, b:b + 1], ALU.mult, ALU.add),
              [BK3, f"modt:{24 + c}"] + KB, ["stg32:0"])
            v(lambda e, c=c: e.tensor_tensor(hffnt[:, par, c, :], sqb[:, c, :],
                                             stg32[:, 0, 0:T], ALU.add),
              ["sqb", "stg32:0"], [f"hffnt:{par}:{c}"])
        yield

        HFK = [f"hffnt:{par}:{c}" for c in range(8)]
        s(lambda e: e.activation(sqb.ap().rearrange("p a b -> p (a b)"),
                                 hffnt.ap()[:, par, :, :].rearrange("p a b -> p (a b)"), AF.Square),
          HFK, ["sqb"])
        for c in range(8):
            t_mm(aSR1, onecb[:], hffnt[:, par, c, :], start=(c == 0), stop=(c == 7),
                 reads=[f"hffnt:{par}:{c}", "onecb"], writes=[BK3])
        for c in range(8):
            t_mm(aSR2, onecb[:], sqb[:, c, :], start=(c == 0), stop=(c == 7),
                 reads=["sqb", "onecb"], writes=[BK3])
        yield
        r_mu = RW.get("r_mu")
        r_rho = RW.get("r_rho")
        v(lambda e: e.tensor_scalar(r_mu, aSR1, 1.0 / E, None, ALU.mult), [BK3], ["r_mu"])
        v(lambda e: e.tensor_scalar(tv, aSR2, 1.0 / E, None, ALU.mult), [BK3], ["r_tv"])
        v(lambda e: e.tensor_tensor(r_rho, r_mu, r_mu, ALU.mult), ["r_mu"], ["r_rho"])
        v(lambda e: e.tensor_tensor(tv, tv, r_rho, ALU.subtract), ["r_tv", "r_rho"], ["r_tv"])
        preload_tab(6)
        s(lambda e: e.activation(tv, tv, AF.Ln, bias=cvals[0:1, 0:1]), ["r_tv", "cvals"], ["r_tv"])
        s(lambda e: e.activation(r_rho, tv, AF.Exp, scale=cvals[0:1, 1:2]), ["r_tv", "cvals"], ["r_rho"])
        preload_tab(10)
        nm3b = RB.get("nm3b")
        rc3b = RB.get("rc3b")
        v(lambda e: e.tensor_scalar(nm3b, r_mu, -1.0, None, ALU.mult), ["r_mu"], ["nm3b"])
        v(lambda e: e.reciprocal(tv, r_rho), ["r_rho"], ["r_tv"])
        v(lambda e: e.tensor_copy(rc3b, tv), ["r_tv"], ["rc3b"])
        yield

        for hf in range(2):
            tsl = slice(128 * hf, 128 * (hf + 1))
            for dc in range(8):
                t_mm(aTOK[:, 0:NB], hffnt[:, par, dc, tsl], wg[:, dc, :],
                     start=(dc == 0), stop=False,
                     reads=[f"hffnt:{par}:{dc}"] + WGK, writes=[BK2])
            t_tr(aCOL[:, 0:1], r_rho[0:1, tsl], idf[0:1, 0:1],
                 reads=["r_rho", "idf"], writes=[BK3])
            v(lambda e: e.tensor_copy(colb[:, 0:1], aCOL[:, 0:1]), [BK3], ["colb"])
            t_mm(aTOK[:, 0:NB], nm3b[0:1, tsl], csr[0:1, :], start=False, stop=False,
                 reads=["nm3b", "csr"], writes=[BK2])
            t_mm(aTOK[:, 0:NB], rc3b[0:1, tsl], cnr[0:1, :], start=False, stop=True,
                 reads=["rc3b", "cnr"], writes=[BK2])
            v(lambda e: e.tensor_scalar(tokA[:], aTOK[:, 0:NB], colb[:, 0:1], 10.0,
                                        ALU.mult, ALU.min),
              [BK2, "colb"], ["tokA"])
            v(lambda e: e.tensor_scalar(stg32[:, 1, :], tokA[:], -10.0, None, ALU.max),
              ["tokA"], ["stg32:1"])
            preload_tab(6)
            s(lambda e: e.activation(tokA[:], stg32[:, 1, :], AF.Exp), ["stg32:1"], ["tokA"])
            s(lambda e: e.activation(stg32[:, 1, :], tokA[:], AF.Ln, bias=cvals[:, 7:8]),
              ["tokA", "cvals"], ["stg32:1"])
            v(lambda e: e.max(phib[:, 0:8], stg32[:, 1, :]), ["stg32:1"], ["phib"])
            v(lambda e: e.reduce_sum(colb[:, 1:2], phib[:, 0:8], axis=AXX), ["phib"], ["colb"])
            s(lambda e: e.activation(colb[:, 2:3], colb[:, 1:2], AF.Exp, scale=cvals[:, 5:6]),
              ["colb", "cvals"], ["colb"])
            preload_tab(10)
            v(lambda e: e.tensor_scalar(colb[:, 3:4], colb[:, 2:3], -1.0, 1.0, ALU.mult, ALU.add),
              ["colb"], ["colb"])
            v(lambda e: e.tensor_scalar(colb[:, 4:5], colb[:, 2:3], 1.0, None, ALU.add),
              ["colb"], ["colb"])
            v(lambda e: e.tensor_scalar(colb[:, 5:6], colb[:, 1:2], 1e-6, None, ALU.add),
              ["colb"], ["colb"])
            v(lambda e: e.tensor_tensor(colb[:, 6:7], colb[:, 4:5], colb[:, 5:6], ALU.mult),
              ["colb"], ["colb"])
            v(lambda e: e.reciprocal(colb[:, 7:8], colb[:, 6:7]), ["colb"], ["colb"])
            v(lambda e: e.tensor_tensor(colb[:, 8:9], colb[:, 3:4], colb[:, 7:8], ALU.mult),
              ["colb"], ["colb"])
            v(lambda e: e.scalar_tensor_tensor(tokA[:], stg32[:, 1, :], phib[:, 7:8], stg32[:, 1, :],
                                               ALU.is_ge, ALU.mult),
              ["stg32:1", "phib"], ["tokA"])
            yield
            for dc in range(8):
                t_mm(aTOK[:, 0:NB], hffnt[:, par, dc, tsl], ut[:, dc, :],
                     start=(dc == 0), stop=(dc == 7),
                     reads=[f"hffnt:{par}:{dc}"] + UTK + ["tokA"], writes=[BK2])
            v(lambda e: e.scalar_tensor_tensor(hlzb[:], aTOK[:, 0:NB], colb[:, 8:9],
                                               tokA[:], ALU.mult, ALU.mult),
              [BK2, "colb", "tokA"], ["hlzb"])
            for uc in range(4):
                cell, bk = (aZT, BK3) if uc % 2 == 0 else (aZT2, BK2)
                t_tr(cell[:, 0:128], hlzb[:, 128 * uc:128 * (uc + 1)], idb[:],
                     reads=["hlzb", "idb"], writes=[bk])
                v(lambda e, uc=uc, hf=hf, cell=cell: e.tensor_copy(
                        zht[:, par, uc, tsl], cell[:, 0:128]),
                  [bk], [f"zht:{par}"])
            yield


    def phaseB(ti, gen):
        b = ti // ntpb
        t0 = (ti % ntpb) * T
        par = ti % 2
        HFK = [f"hffnt:{par}:{c}" for c in range(8)]

        def pump(n=1):
            for _ in range(n):
                if gen is not None:
                    try:
                        next(gen)
                    except StopIteration:
                        break

        def w2_load(w):
            wpar = w % 2
            for j in range(4):
                hc = 4 * w + j
                cx.dma("sync", f"w2{wpar}", w2tw[:, wpar, j, :],
                       AP(w2s_h, 128 * E * hc, [[E, 128], [1, E]]),
                       reads=[f"w2scr:{hc}"], writes=[f"w2tw:{wpar}"])

        w2_load(0)
        w2_load(1)
        pump(3)
        for w in range(8):
            wpar = w % 2
            GK = [f"gelut:{wpar}:{j}" for j in range(4)]
            for j in (0, 2, 1, 3):
                i = 4 * w + j
                for dc in range(8):
                    t_mm(pbig[:, j, :], w1t[:, dc, 128 * i:128 * (i + 1)], hffnt[:, par, dc, :],
                         start=(dc == 0), stop=(dc == 7),
                         reads=[f"w1t:{dc}", f"hffnt:{par}:{dc}"], writes=[BKB(j)])
                s(lambda e, j=j, wpar=wpar: e.activation(gelut[:, wpar, j, :], pbig[:, j, :],
                                                         AF.Gelu),
                  [BKB(j)], [f"gelut:{wpar}:{j}"])
            for j in range(4):
                for nc2 in range(8):
                    t_mm(pout[:, nc2, :], w2tw[:, wpar, j, 128 * nc2:128 * (nc2 + 1)],
                         gelut[:, wpar, j, :],
                         start=(w == 0 and j == 0 and nc2 % 2 == 0), stop=False,
                         reads=[f"w2tw:{wpar}", f"gelut:{wpar}:{j}"], writes=[BKO(nc2)])
            if w < 6:
                w2_load(w + 2)
            if w == 3:
                emit_xin(ti + 2)
            pump({0: 2, 1: 2, 2: 2, 3: 2, 4: 1, 5: 2, 6: 1, 7: 0}[w])
        for uc in range(4):
            for nc2 in range(8):
                t_mm(pout[:, nc2, :], vg[:, uc, 128 * nc2:128 * (nc2 + 1)], zht[:, par, uc, :],
                     start=False, stop=(uc == 3 and nc2 % 2 == 1),
                     reads=VGK + [f"zht:{par}"], writes=[BKO(nc2)])
        pump(2)
        for c in range(8):
            st_ = c % 4
            v(lambda e, c=c, st_=st_: e.scalar_tensor_tensor(
                    x3st[:, st_, :], pout[:, c, :], vecs[:, b % 2, VL["gb"], c:c + 1],
                    x2t[:, par, c, :], ALU.mult, ALU.add),
              [BKO(c), f"vecs:{b % 2}", f"x2t:{par}"], [f"x3st:{st_}"])
            cx.dma("gpsimd", f"out{st_}", oh[b, 128 * c:128 * (c + 1), t0:t0 + T],
                   x3st[:, st_, :], reads=[f"x3st:{st_}"], writes=[])
        pump(100)

    gen = phaseA(0)
    next(gen)
    next(gen)
    emit_xin(0)
    emit_weight_tail()
    for _ in range(100):
        try:
            next(gen)
        except StopIteration:
            break
    emit_xin(1)
    for ti in range(n_tiles):
        gen = phaseA(ti + 1) if ti + 1 < n_tiles else None
        phaseB(ti, gen)

    cx.finish("sync")
    return nc


def kernel(L0_attn_x, L0_attn_W_el, L0_attn_b_el, L0_attn_W_le, L0_attn_b_le,
           L0_ffn_W1, L0_ffn_W2, L0_ffn_ln_g, L0_ffn_ln_b, L0_ffn_router_W,
           L0_ffn_router_b, L0_ffn_raw_U, L0_ffn_raw_V, L0_ffn_gamma,
           L0_c, L0_wmod):
    COS, SIN, ka, Hm, lx = _consts()
    x = np.asarray(L0_attn_x, dtype=np.float32)
    xT = np.ascontiguousarray(x.transpose(0, 2, 1))
    x00row = np.ascontiguousarray(x[0, 0, :]).reshape(1, E)
    wle_ext = np.ascontiguousarray(
        np.concatenate([np.asarray(L0_attn_W_le, np.float32),
                        np.asarray(L0_attn_b_le, np.float32)[None, :]], axis=0))
    common = {
        "x00row": x00row,
        "wmod": np.ascontiguousarray(np.asarray(L0_wmod, np.float32)),
        "W_el": np.ascontiguousarray(np.asarray(L0_attn_W_el, np.float32)),
        "Wle_ext": wle_ext,
        "W1T": np.ascontiguousarray(np.asarray(L0_ffn_W1, np.float32).T),
        "W2T": np.ascontiguousarray(np.asarray(L0_ffn_W2, np.float32).T),
        "RWT": np.ascontiguousarray(np.asarray(L0_ffn_router_W, np.float32).T),
        "UT": np.ascontiguousarray(np.asarray(L0_ffn_raw_U, np.float32).T),
        "V": np.ascontiguousarray(np.asarray(L0_ffn_raw_V, np.float32)),
        "ln_gc": np.ascontiguousarray(np.asarray(L0_ffn_ln_g, np.float32).reshape(8, 128).T),
        "ln_bc": np.ascontiguousarray(np.asarray(L0_ffn_ln_b, np.float32).reshape(8, 128).T),
        "router_b": np.ascontiguousarray(np.asarray(L0_ffn_router_b, np.float32).reshape(1, NB)),
        "gamma_r": np.ascontiguousarray(np.asarray(L0_ffn_gamma, np.float32).reshape(1, E)),
        "COS": COS, "SIN": SIN,
        "ka_r": ka.reshape(1, E),
        "ka2_r": (ka * ka).reshape(1, E),
        "lx_r": lx.reshape(1, 5),
        "HT": np.ascontiguousarray(Hm.T),
    }
    nc = build_nc()
    cf = np.asarray(L0_c, np.float32)
    in_maps = []
    for k in range(NCORES):
        m = dict(common)
        m["xT"] = np.ascontiguousarray(xT[BPC * k:BPC * (k + 1)])
        m["cT"] = np.ascontiguousarray(
            np.stack([cf[BPC * k], cf[BPC * k + 1], cf[0], cf[0]], axis=1))
        in_maps.append(m)
    res = run_bass_kernel_spmd(nc, in_maps, core_ids=list(range(NCORES)))
    outs = [np.asarray(r["outT"]).reshape(BPC, E, SEQ).transpose(0, 2, 1)
            for r in res.results]
    return np.ascontiguousarray(np.concatenate(outs, axis=0)).astype(np.float32)


# version 23
# speedup vs baseline: 1.0068x; 1.0068x over previous
import numpy as np
import concourse.bass as bass
import concourse.mybir as mybir
from concourse.ap import AP
from concourse.bass_utils import run_bass_kernel_spmd

F32 = mybir.dt.float32
BF16 = mybir.dt.bfloat16
AF = mybir.ActivationFunctionType
ALU = mybir.AluOpType
AXX = mybir.AxisListType.X

E = 1024
T = 256
PRELOADS = True
SEQ = 2048
BPC = 2
NCORES = 8
HID = 4096
NB = 512
PI = float(np.pi)


class Ctx:

    ENGINES = ("tensor", "vector", "scalar", "gpsimd", "sync")

    def __init__(self, nc):
        self.nc = nc
        self.eng = {"tensor": nc.tensor, "vector": nc.vector,
                    "scalar": nc.scalar, "gpsimd": nc.gpsimd, "sync": nc.sync}
        self.eng_sem = {}
        self.instrs = {e: [] for e in self.ENGINES}
        self.inc_total = {e: 0 for e in self.ENGINES}
        self.drained_upto = {e: 0 for e in self.ENGINES}
        self.last_write = {}
        self.readers = {}
        self.waited = {}
        self.dma_sems = {}

    def sem_for(self, engname):
        if engname not in self.eng_sem:
            self.eng_sem[engname] = self.nc.alloc_semaphore(name=f"es_{engname}")
        return self.eng_sem[engname]

    def stream(self, name):
        if name not in self.dma_sems:
            self.dma_sems[name] = [self.nc.alloc_semaphore(name=f"ds_{name}"), 0]
        return self.dma_sems[name]

    def _ensure_inc(self, engname, idx):
        lst = self.instrs[engname]
        for j in range(idx, len(lst)):
            if lst[j][1] is not None:
                return lst[j][1]
        instr, _ = lst[idx]
        self.inc_total[engname] += 1
        v = self.inc_total[engname]
        instr.then_inc(self.sem_for(engname), 1)
        lst[idx] = (instr, v)
        return v

    def _wait(self, waiter, sem, val):
        k = (waiter, id(sem))
        if self.waited.get(k, 0) >= val:
            return
        self.eng[waiter].wait_ge(sem, val)
        self.waited[k] = val

    def _dep(self, waiter, prod, pe_skip):
        if prod[0] == "eng":
            _, pname, pidx = prod
            if pname == waiter:
                if pname == "tensor" and pe_skip:
                    return
                if pidx >= self.drained_upto[pname]:
                    ins = self.eng[pname].drain()
                    self.instrs[pname].append((ins, None))
                    self.drained_upto[pname] = len(self.instrs[pname])
            else:
                v = self._ensure_inc(pname, pidx)
                self._wait(waiter, self.sem_for(pname), v)
        else:
            _, sname, _cnt = prod
            sem, total = self.dma_sems[sname]
            self._wait(waiter, sem, 16 * total)

    def _edges(self, waiter, reads, writes, pe_skip):
        seen = set()

        def dep1(p):
            if p is None or p in seen:
                return
            seen.add(p)
            self._dep(waiter, p, pe_skip)

        for k in reads:
            dep1(self.last_write.get(k))
        for k in writes:
            dep1(self.last_write.get(k))
            for p2 in self.readers.get(k, ()):
                dep1(p2)

    def _record(self, rec, reads, writes):
        for k in writes:
            self.last_write[k] = rec
            self.readers[k] = []
        for k in reads:
            self.readers.setdefault(k, []).append(rec)

    def op(self, engname, fn, reads=(), writes=(), pe_skip=False):
        self._edges(engname, reads, writes, pe_skip)
        ins = fn(self.eng[engname])
        idx = len(self.instrs[engname])
        self.instrs[engname].append((ins, None))
        self._record(("eng", engname, idx), reads, writes)
        return ins

    def dma(self, engname, streamname, out, in_, reads=(), writes=()):
        self._edges(engname, reads, writes, False)
        st = self.stream(streamname)
        ins = self.eng[engname].dma_start(out=out, in_=in_)
        ins.then_inc(st[0], 16)
        st[1] += 1
        self.instrs[engname].append((ins, None))
        self._record(("dma", streamname, st[1]), reads, writes)
        return ins

    def finish(self, waiter="sync"):
        for name in self.dma_sems:
            sem, total = self.dma_sems[name]
            if total:
                self._wait(waiter, sem, 16 * total)


class RowAlloc:
    def __init__(self, t, nslots, width):
        self.t, self.width = t, width
        self.free = list(range(nslots))
        self.named = {}

    def get(self, name, n=None):
        if name not in self.named:
            self.named[name] = self.free.pop(0)
        return self.t[0:1, self.named[name], 0:(n or self.width)]

    def drop(self, *names):
        for name in names:
            if name in self.named:
                self.free.append(self.named.pop(name))


def _consts():
    k = np.arange(E)
    km = (np.outer(k, k) % E).astype(np.float64)
    ang = 2.0 * np.pi * km / E
    COS = np.cos(ang).astype(np.float32)
    SIN = np.sin(ang).astype(np.float32)
    ka = (np.abs(np.fft.fftfreq(E)) + 1e-8).astype(np.float32)
    A12 = np.array([[1,1,1,0,1,1,1,0,0,0,1,1],[1,1,0,1,1,1,0,0,0,1,1,0],
                    [1,0,1,1,1,0,0,0,1,1,0,1],[0,1,1,1,0,0,0,1,1,0,1,1],
                    [0,0,0,0,1,1,1,1,1,1,1,1],[1,1,1,1,0,0,0,0,0,0,0,0],
                    [0,0,0,0,1,1,1,1,0,0,0,0],[1,1,0,0,1,1,0,0,1,1,0,0],
                    [1,0,1,0,1,0,1,0,1,0,1,0],[0,1,0,1,0,1,0,1,0,1,0,1],
                    [1,0,0,1,1,0,0,1,1,0,0,1],[0,1,1,0,0,1,1,0,0,1,1,0]],
                   dtype=np.float32)
    Hm = np.concatenate([A12.T, np.eye(12, dtype=np.float32)], axis=1)
    lx = np.log(np.array([2.0, 4.0, 8.0, 16.0, 32.0], dtype=np.float32))
    return COS, SIN, ka, Hm, lx


def build_nc(ntpb=SEQ // T):
    import ml_dtypes
    nc = bass.Bass("TRN2", target_bir_lowering=False)
    cx = Ctx(nc)

    xh = nc.dram_tensor("xT", [BPC, E, SEQ], F32, kind="ExternalInput").ap()
    oh = nc.dram_tensor("outT", [BPC, E, SEQ], F32, kind="ExternalOutput").ap()
    x00h = nc.dram_tensor("x00row", [1, E], F32, kind="ExternalInput").ap()
    ch = nc.dram_tensor("cT", [E, 4], F32, kind="ExternalInput").ap()
    wmodh = nc.dram_tensor("wmod", [E, 6 * E], F32, kind="ExternalInput").ap()
    welh = nc.dram_tensor("W_el", [E, 24], F32, kind="ExternalInput").ap()
    wleh = nc.dram_tensor("Wle_ext", [25, E], F32, kind="ExternalInput").ap()
    w1h = nc.dram_tensor("W1T", [E, HID], F32, kind="ExternalInput").ap()
    w2h = nc.dram_tensor("W2T", [HID, E], F32, kind="ExternalInput").ap()
    rwh = nc.dram_tensor("RWT", [E, NB], F32, kind="ExternalInput").ap()
    uth = nc.dram_tensor("UT", [E, NB], F32, kind="ExternalInput").ap()
    vh = nc.dram_tensor("V", [NB, E], F32, kind="ExternalInput").ap()
    lngh = nc.dram_tensor("ln_gc", [128, 8], F32, kind="ExternalInput").ap()
    lnbh = nc.dram_tensor("ln_bc", [128, 8], F32, kind="ExternalInput").ap()
    rbh = nc.dram_tensor("router_b", [1, NB], F32, kind="ExternalInput").ap()
    gamh = nc.dram_tensor("gamma_r", [1, E], F32, kind="ExternalInput").ap()
    cosh_ = nc.dram_tensor("COS", [E, E], F32, kind="ExternalInput").ap()
    sinh_ = nc.dram_tensor("SIN", [E, E], F32, kind="ExternalInput").ap()
    kah = nc.dram_tensor("ka_r", [1, E], F32, kind="ExternalInput").ap()
    ka2h = nc.dram_tensor("ka2_r", [1, E], F32, kind="ExternalInput").ap()
    lxh = nc.dram_tensor("lx_r", [1, 5], F32, kind="ExternalInput").ap()
    hth = nc.dram_tensor("HT", [24, 12], F32, kind="ExternalInput").ap()
    wext_h = nc.dram_tensor("wext_scr", [2 * E], BF16, kind="Internal")
    w2s_h = nc.dram_tensor("w2bf_scr", [HID, E], BF16, kind="Internal")

    Jnp = np.zeros((128, 128), dtype=np.float32)
    Jnp[np.arange(128), 127 - np.arange(128)] = 1.0
    I_bf_t = nc.inline_tensor(np.eye(128, dtype=np.float32).astype(ml_dtypes.bfloat16), name="i_bf")
    I_f_t = nc.inline_tensor(np.eye(128, dtype=np.float32), name="i_f")
    J_bf_t = nc.inline_tensor(Jnp.astype(ml_dtypes.bfloat16), name="j_bf")

    A = nc.alloc_sbuf_tensor

    w1t = A("w1t", [128, 8, HID], BF16)
    w2tw = A("w2tw", [128, 2, 4, E], BF16)
    m_sb = A("m_sb", [128, 8, E], BF16)
    wg = A("wg", [128, 8, NB], BF16)
    ut = A("ut", [128, 8, NB], BF16)
    vg = A("vg", [128, 4, E], BF16)
    mwel = A("mwel", [128, 8, 24], BF16)
    welb = A("welb", [128, 8, 24], BF16)
    modt = A("modt", [128, 48, 4], F32)
    wle_sb = A("wle_sb", [25, E], BF16)
    ht_sb = A("ht_sb", [24, 12], BF16)
    idb = A("idb", [128, 128], BF16)
    idf = A("idf", [128, 128], F32)
    jb = A("jb", [128, 128], BF16)
    onecb = A("onecb", [128, 1], BF16)
    onecf = A("onecf", [128, 1], F32)
    onerf = A("onerf", [1, 128], F32)
    cvals = A("cvals", [128, 8], F32)

    xt = A("xt", [128, 8, T], F32)
    xtb = A("xtb", [128, 8, T], BF16)
    sqb = A("sqb", [128, 8, T], BF16)
    rhs1 = A("rhs1", [128, 8, T], BF16)
    x2t = A("x2t", [128, 2, 8, T], F32)
    hffnt = A("hffnt", [128, 2, 8, T], BF16)
    gelut = A("gelut", [128, 2, 4, T], BF16)
    stg32 = A("stg32", [128, 2, NB], F32)
    x3st = A("x3st", [128, 4, T], F32)
    tokA = A("tokA", [128, NB], F32)
    hlzb = A("hlzb", [128, NB], BF16)
    zht = A("zht", [128, 2, 4, T], BF16)
    phib = A("phib", [128, 16], F32)
    colb = A("colb", [128, 16], F32)
    corrb = A("corrb", [25, T], BF16)
    onerb = A("onerb", [1, T], BF16)
    golA = A("golA", [24, T], F32)
    golB = A("golB", [24, T], F32)
    synb = A("synb", [12, T], BF16)

    rows_t = A("rows_t", [1, 4, T], F32)
    rowsb_t = A("rowsb_t", [1, 4, T], BF16)
    RW = RowAlloc(rows_t, 4, T)
    RB = RowAlloc(rowsb_t, 4, T)
    csr = A("csr", [1, NB], BF16)
    cnr = A("cnr", [1, NB], BF16)
    r2x = A("r2x", [2, T], BF16)
    rows2 = A("rows2", [2, BPC, E], BF16)
    mwr2 = A("mwr2", [2, BPC, 32], BF16)
    vecs = A("vecs", [128, 2, 8, 8], F32)
    vecb = A("vecb", [128, 8, 4], BF16)
    lngc = A("lngc", [128, 8], F32)
    lnbcb = A("lnbcb", [128, 8], BF16)
    dsc = A("dsc", [1, 16], F32)
    silc = A("silc", [128, 8, 4], BF16)
    silc32 = A("silc32", [128, 8, 4], F32)
    cstage = A("cstage", [128, 8, 4], F32)
    mcol = A("mcol", [128, 8], F32)
    frcols = A("frcols", [128, 8], F32)
    ficols = A("ficols", [128, 8], F32)
    wcol = A("wcol", [128, 8], F32)
    mwtmp = A("mwtmp", [128, 8, 24], BF16)

    xtf = xt.ap().rearrange("p a b -> p (a b)")
    x2f = x2t.ap().rearrange("p q a b -> p (q a b)")
    sca0 = [xtf[0:1, 0:E], xtf[0:1, E:2 * E]]
    SCAK = [f"xt:{c}" for c in range(8)]
    x00r = x2f[0:1, 0:E]
    h00r = x2f[0:1, E:2 * E]
    gamr = x2f[0:1, 0:E]
    rbr = x2f[0:1, E:E + NB]
    X2RK = ["x2t:0", "x2t:1"]

    pbig = nc.alloc_psum_tensor("pbig", [128, 8, T], F32)
    pout = nc.alloc_psum_tensor("pout", [128, 8, T], F32)

    def BKB(c):
        return ("pb", c // 2)

    def BKO(c):
        return ("po", c // 2)

    pstats = pout.ap()[:, 0, :]
    pmisc = pout.ap()[:, 1, :]
    prho = pout.ap()[:, 2, :]
    pmrho = pout.ap()[:, 3, :]
    ptokv = pout.ap()[:, 4:6, :].rearrange("p a b -> p (a b)")
    pztb = pout.ap()[:, 6, :].bitcast(BF16)
    pcol = pout.ap()[:, 7, :]

    def t_mm(out_ap, lhs, rhs, start, stop, reads, writes):
        cx.op("tensor", lambda e: e.matmul(out_ap, lhs, rhs, start=start, stop=stop),
              reads=reads, writes=writes, pe_skip=True)

    def t_tr(out_ap, in_ap, ident, reads, writes):
        cx.op("tensor", lambda e: e.transpose(out_ap, in_ap, ident),
              reads=reads, writes=writes, pe_skip=True)

    v = lambda fn, r, w: cx.op("vector", fn, reads=r, writes=w)
    s = lambda fn, r, w: cx.op("scalar", fn, reads=r, writes=w)

    def preload_tab(set_id):
        if PRELOADS:
            s(lambda e: e.add_instruction(mybir.InstLoadActFuncSet(
                name=nc.get_next_instruction_name(), act_func_set_id=set_id,
                ins=[], outs=[])), [], [])

    silf = silc.ap().rearrange("p c b -> p (c b)")
    cstf = cstage.ap().rearrange("p c b -> p (c b)")

    cx.dma("sync", "cst", idb[:], I_bf_t.ap()[:], writes=["idb"])
    cx.dma("sync", "cst", idf[:], I_f_t.ap()[:], writes=["idf"])
    cx.dma("sync", "cst", jb[:], J_bf_t.ap()[:], writes=["jb"])
    cx.dma("sync", "cst", lngc[:], lngh[:], writes=["lngc"])
    cx.dma("sync", "cst", x00r, x00h[0:1, :], writes=["x2t:0"])
    lxrow = A("lxrow", [1, 8], F32)
    cx.dma("sync", "cst", lxrow[0:1, 0:5], lxh[0:1, 0:5], writes=["lx"])
    cx.dma("sync", "cst", cstage.ap()[:], ch.rearrange("(c p) b -> p c b", p=128),
           writes=["cstage"])
    cx.dma("gpsimd", "cstg", ht_sb[:], hth[:], writes=["ht_sb"])
    cx.dma("gpsimd", "cstg", wle_sb[:], wleh[:], writes=["wle_sb"])
    cx.dma("gpsimd", "cstg", lnbcb[:], lnbh[:], writes=["lnbcb"])
    cx.dma("gpsimd", "cstg", welb.ap()[:],
           welh.rearrange("(c p) j -> p c j", p=128), writes=["welb"])

    for ci, val in enumerate((1e-5, -0.5, PI, -1.0, PI / 2, -2.0, 0.5, 1.0)):
        v(lambda e, ci=ci, val=val: e.memset(cvals[:, ci:ci + 1], val), [], ["cvals"])
    v(lambda e: e.memset(onecb[:], 1.0), [], ["onecb"])
    v(lambda e: e.memset(onecf[:], 1.0), [], ["onecf"])
    v(lambda e: e.memset(onerf[:], 1.0), [], ["onerf"])
    v(lambda e: e.memset(onerb[:], 1.0), [], ["onerb"])
    cx.dma("gpsimd", "cstg", corrb[24:25, :], onerb[:], reads=["onerb"], writes=["corr24"])
    cx.dma("gpsimd", "cstg", r2x[1:2, :], onerb[:], reads=["onerb"], writes=["r2x"])

    preload_tab(18)
    s(lambda e: e.activation(silf, cstf, AF.Silu), ["cstage"], ["silc"])
    silf32 = silc32.ap().rearrange("p c b -> p (c b)")
    s(lambda e: e.activation(silf32, cstf, AF.Silu), ["cstage"], ["silc32"])

    w2f = w2tw.ap().bitcast(F32)
    xtfl32 = xt.ap().rearrange("p a b -> p (a b)")
    mstages = [
        (w2f[:, 0, :, :].rearrange("p a b -> p (a b)"), ["w2tw:0"], "gpsimd", "wmodA"),
        (w2f[:, 1, :, :].rearrange("p a b -> p (a b)"), ["w2tw:1"], "sync", "wmodB"),
        (xtfl32, [f"xt:{c}" for c in range(8)], "scalar", "wmodC"),
    ]
    msi = 0
    for piece in range(3):
        for dc in range(8):
            stg, skeys, engn, strm = mstages[msi % 3]
            msi += 1
            cx.dma(engn, strm, stg,
                   wmodh[128 * dc:128 * (dc + 1), 2048 * piece:2048 * (piece + 1)],
                   writes=skeys)
            for jc in range(16):
                t_mm(pbig[:, 0, 4 * jc:4 * (jc + 1)],
                     stg[:, 128 * jc:128 * (jc + 1)], silf32[:, 4 * dc:4 * (dc + 1)],
                     start=(dc == 0 and jc == 0), stop=(dc == 7 and jc == 15),
                     reads=skeys + ["silc32"], writes=[BKB(0)])
        for jc in range(16):
            j = 16 * piece + jc
            v(lambda e, j=j, jc=jc: e.tensor_copy(modt[:, j, :],
                                                  pbig[:, 0, 4 * jc:4 * (jc + 1)]),
              [BKB(0)], [f"modt:{j}"])

    MODK = [f"modt:{j}" for j in range(48)]

    for c in range(8):
        for vi, base in ((0, 0), (1, 8)):
            t_tr(pcol[0:1, 0:128], modt[:, base + c, 2:3], idf[:],
                 reads=[f"modt:{base + c}", "idf"], writes=[("po", 3)])
            v(lambda e, vi=vi, c=c: e.tensor_copy(sca0[vi][0:1, 128 * c:128 * (c + 1)],
                                                  pcol[0:1, 0:128]),
              [("po", 3)], SCAK)

    s(lambda e: e.activation(h00r, x00r, AF.Square, accum_out=dsc[0:1, 0:1]),
      ["x2t:0"], ["x2t:0", "dsc"])
    v(lambda e: e.reduce_sum(dsc[0:1, 1:2], x00r, axis=AXX), ["x2t:0"], ["dsc"])
    v(lambda e: e.tensor_scalar(dsc[0:1, 2:3], dsc[0:1, 1:2], 1.0 / E, None, ALU.mult),
      ["dsc"], ["dsc"])
    v(lambda e: e.tensor_tensor(dsc[0:1, 3:4], dsc[0:1, 2:3], dsc[0:1, 2:3], ALU.mult),
      ["dsc"], ["dsc"])
    v(lambda e: e.tensor_scalar(dsc[0:1, 4:5], dsc[0:1, 0:1], 1.0 / E, None, ALU.mult),
      ["dsc"], ["dsc"])
    v(lambda e: e.tensor_tensor(dsc[0:1, 5:6], dsc[0:1, 4:5], dsc[0:1, 3:4], ALU.subtract),
      ["dsc"], ["dsc"])
    preload_tab(6)
    s(lambda e: e.activation(dsc[0:1, 6:7], dsc[0:1, 5:6], AF.Ln, bias=cvals[0:1, 0:1]),
      ["dsc", "cvals"], ["dsc"])
    s(lambda e: e.activation(dsc[0:1, 7:8], dsc[0:1, 6:7], AF.Exp, scale=cvals[0:1, 1:2]),
      ["dsc", "cvals"], ["dsc"])
    v(lambda e: e.tensor_tensor(dsc[0:1, 8:9], dsc[0:1, 2:3], dsc[0:1, 7:8], ALU.mult),
      ["dsc"], ["dsc"])
    v(lambda e: e.tensor_scalar(dsc[0:1, 9:10], dsc[0:1, 8:9], -1.0, None, ALU.mult),
      ["dsc"], ["dsc"])
    v(lambda e: e.tensor_scalar(h00r, x00r, dsc[0:1, 7:8], dsc[0:1, 9:10],
                                ALU.mult, ALU.add),
      ["x2t:0", "dsc"], ["x2t:0"])
    v(lambda e: e.tensor_scalar(sca0[1], sca0[1], 1.0, None, ALU.add),
      SCAK, SCAK)
    v(lambda e: e.tensor_tensor(h00r, h00r, sca0[1], ALU.mult),
      ["x2t:0"] + SCAK, ["x2t:0"])
    v(lambda e: e.tensor_tensor(h00r, h00r, sca0[0], ALU.add),
      ["x2t:0"] + SCAK, ["x2t:0"])

    cnt_r = RW.get("cnts", 8)
    for bi, n in enumerate((4, 16, 64, 256, 1024)):
        v(lambda e, n=n: e.reduce_sum(dsc[0:1, 10:11], h00r[0:1, 0:n], axis=AXX),
          ["x2t:0"], ["dsc"])
        v(lambda e, n=n: e.tensor_scalar(dsc[0:1, 11:12], dsc[0:1, 10:11], 1.0 / n,
                                         None, ALU.mult),
          ["dsc"], ["dsc"])
        v(lambda e, n=n: e.tensor_scalar(x00r[0:1, 0:n], h00r[0:1, 0:n],
                                         dsc[0:1, 11:12], None, ALU.is_gt),
          ["x2t:0", "dsc"], ["x2t:0"])
        v(lambda e, n=n, bi=bi: e.reduce_sum(cnt_r[0:1, bi:bi + 1], x00r[0:1, 0:n],
                                             axis=AXX),
          ["x2t:0"], ["cnts"])
    v(lambda e: e.tensor_scalar(cnt_r[0:1, 0:5], cnt_r[0:1, 0:5], 1.0, None, ALU.max),
      ["cnts"], ["cnts"])
    ly_r = RW.get("ly", 8)
    s(lambda e: e.activation(ly_r[0:1, 0:5], cnt_r[0:1, 0:5], AF.Ln), ["cnts"], ["ly"])
    lx_r = lxrow
    wk = RW.get("dwork")
    v(lambda e: e.tensor_tensor(wk[0:1, 0:5], lx_r[0:1, 0:5], ly_r[0:1, 0:5], ALU.mult),
      ["lx", "ly"], ["dw"])
    v(lambda e: e.tensor_tensor(wk[0:1, 5:10], lx_r[0:1, 0:5], lx_r[0:1, 0:5], ALU.mult),
      ["lx", "dw"], ["dw"])
    v(lambda e: e.reduce_sum(wk[0:1, 10:11], wk[0:1, 0:5], axis=AXX), ["dw"], ["dw"])
    v(lambda e: e.reduce_sum(wk[0:1, 11:12], lx_r[0:1, 0:5], axis=AXX), ["lx", "dw"], ["dw"])
    v(lambda e: e.reduce_sum(wk[0:1, 12:13], ly_r[0:1, 0:5], axis=AXX), ["ly", "dw"], ["dw"])
    v(lambda e: e.reduce_sum(wk[0:1, 13:14], wk[0:1, 5:10], axis=AXX), ["dw"], ["dw"])
    v(lambda e: e.tensor_tensor(wk[0:1, 14:15], wk[0:1, 11:12], wk[0:1, 12:13], ALU.mult),
      ["dw"], ["dw"])
    v(lambda e: e.tensor_scalar(wk[0:1, 15:16], wk[0:1, 10:11], 5.0, None, ALU.mult),
      ["dw"], ["dw"])
    v(lambda e: e.tensor_tensor(wk[0:1, 16:17], wk[0:1, 15:16], wk[0:1, 14:15], ALU.subtract),
      ["dw"], ["dw"])
    v(lambda e: e.tensor_tensor(wk[0:1, 17:18], wk[0:1, 11:12], wk[0:1, 11:12], ALU.mult),
      ["dw"], ["dw"])
    v(lambda e: e.tensor_scalar(wk[0:1, 18:19], wk[0:1, 13:14], 5.0, None, ALU.mult),
      ["dw"], ["dw"])
    v(lambda e: e.tensor_tensor(wk[0:1, 19:20], wk[0:1, 18:19], wk[0:1, 17:18], ALU.subtract),
      ["dw"], ["dw"])
    v(lambda e: e.reciprocal(wk[0:1, 20:21], wk[0:1, 19:20]), ["dw"], ["dw"])
    v(lambda e: e.tensor_tensor(wk[0:1, 21:22], wk[0:1, 16:17], wk[0:1, 20:21], ALU.mult),
      ["dw"], ["dw"])
    v(lambda e: e.tensor_scalar(wk[0:1, 22:23], wk[0:1, 21:22], -1.0, 1.0, ALU.mult, ALU.max),
      ["dw"], ["dw"])
    v(lambda e: e.tensor_scalar(dsc[0:1, 0:1], wk[0:1, 22:23], 2.0, None, ALU.min),
      ["dw"], ["dsc"])
    v(lambda e: e.tensor_scalar(dsc[0:1, 1:2], dsc[0:1, 0:1], 0.2, 1.0, ALU.mult, ALU.add),
      ["dsc"], ["dsc"])
    v(lambda e: e.tensor_tensor(dsc[0:1, 2:3], dsc[0:1, 1:2], dsc[0:1, 0:1], ALU.mult),
      ["dsc"], ["dsc"])
    v(lambda e: e.tensor_scalar(dsc[0:1, 3:4], dsc[0:1, 0:1], 0.5, 1.0, ALU.mult, ALU.add),
      ["dsc"], ["dsc"])
    v(lambda e: e.tensor_scalar(dsc[0:1, 4:5], dsc[0:1, 0:1], 0.1, None, ALU.mult),
      ["dsc"], ["dsc"])
    RW.drop("cnts", "ly", "dwork")

    ka_row = sca0[0]
    ka2_row = sca0[1]
    cx.dma("sync", "cst2", ka_row, kah[0:1, :], reads=SCAK, writes=SCAK)
    cx.dma("sync", "cst2", ka2_row, ka2h[0:1, :], reads=SCAK, writes=SCAK)
    arg_row = x00r
    amp_row = h00r
    v(lambda e: e.tensor_scalar(arg_row, ka_row, dsc[0:1, 3:4], dsc[0:1, 2:3],
                                ALU.mult, ALU.add),
      SCAK + ["dsc", "x2t:0"], ["x2t:0"])
    preload_tab(9)
    s(lambda e: e.activation(amp_row, arg_row, AF.Sin, bias=cvals[0:1, 2:3], scale=cvals[0:1, 3:4]),
      ["x2t:0", "cvals"], ["x2t:0"])
    v(lambda e: e.tensor_scalar(arg_row, ka2_row, dsc[0:1, 4:5], dsc[0:1, 2:3],
                                ALU.mult, ALU.add),
      SCAK + ["dsc", "x2t:0"], ["x2t:0"])
    v(lambda e: e.tensor_tensor(arg_row, arg_row, ka_row, ALU.subtract),
      SCAK + ["x2t:0"], ["x2t:0"])
    s(lambda e: e.activation(sca0[0], arg_row, AF.Sin, bias=cvals[0:1, 4:5], scale=cvals[0:1, 3:4]),
      ["x2t:0", "cvals"] + SCAK, SCAK)
    s(lambda e: e.activation(sca0[1], arg_row, AF.Sin),
      ["x2t:0"] + SCAK, SCAK)
    v(lambda e: e.tensor_tensor(sca0[0], sca0[0], amp_row, ALU.mult),
      SCAK + ["x2t:0"], SCAK)
    v(lambda e: e.tensor_tensor(sca0[1], sca0[1], amp_row, ALU.mult),
      SCAK + ["x2t:0"], SCAK)
    v(lambda e: e.tensor_scalar(sca0[1], sca0[1], -1.0, None, ALU.mult),
      SCAK, SCAK)
    for j in range(8):
        t_tr(pcol[:, 0:1], sca0[0][0:1, 128 * j:128 * (j + 1)], idf[0:1, 0:1],
             reads=SCAK + ["idf"], writes=[("po", 3)])
        v(lambda e, j=j: e.tensor_copy(frcols[:, j:j + 1], pcol[:, 0:1]),
          [("po", 3)], ["frcols"])
        t_tr(pcol[:, 0:1], sca0[1][0:1, 128 * j:128 * (j + 1)], idf[0:1, 0:1],
             reads=SCAK + ["idf"], writes=[("po", 3)])
        v(lambda e, j=j: e.tensor_copy(ficols[:, j:j + 1], pcol[:, 0:1]),
          [("po", 3)], ["ficols"])

    first_w = True
    xtfl = xt.ap().rearrange("p a b -> p (a b)")
    wstg = [(stg32.ap()[:, 0, :], "stg32:0"), (stg32.ap()[:, 1, :], "stg32:1"),
            (xtfl[:, 0:512], "xt:0"), (xtfl[:, 512:1024], "xt:2"),
            (xtfl[:, 1024:1536], "xt:4"), (xtfl[:, 1536:2048], "xt:6")]
    wsi = 0
    for kc in range(8):
        for hf in range(2):
            stgc, skey = wstg[wsi % 6]
            cx.dma("scalar", f"cosin{wsi % 6}", stgc,
                   cosh_[128 * kc:128 * (kc + 1), 512 * hf:512 * (hf + 1)], writes=[skey])
            for ms in range(4):
                mc = 4 * hf + ms
                t_mm(pbig[:, 1, mc:mc + 1], stgc[:, 128 * ms:128 * (ms + 1)],
                     frcols[:, kc:kc + 1],
                     start=first_w, stop=False,
                     reads=[skey, "frcols"], writes=[BKB(0)])
                first_w = False
            wsi += 1
            stgc, skey = wstg[wsi % 6]
            cx.dma("scalar", f"cosin{wsi % 6}", stgc,
                   sinh_[128 * kc:128 * (kc + 1), 512 * hf:512 * (hf + 1)], writes=[skey])
            for ms in range(4):
                mc = 4 * hf + ms
                t_mm(pbig[:, 1, mc:mc + 1], stgc[:, 128 * ms:128 * (ms + 1)],
                     ficols[:, kc:kc + 1],
                     start=False, stop=(kc == 7 and hf == 1 and ms == 3),
                     reads=[skey, "ficols"], writes=[BKB(0)])
            wsi += 1
    v(lambda e: e.tensor_scalar(wcol[:], pbig[:, 1, 0:8], 1.0 / E, None, ALU.mult),
      [BKB(0)], ["wcol"])
    wcolb = A("wcolb", [128, 8], BF16)
    v(lambda e: e.tensor_copy(wcolb[:], wcol[:]), ["wcol"], ["wcolb"])

    for c in range(8):
        cx.dma("scalar", "wex", AP(wext_h, 128 * c, [[1, 128]]), wcolb[:, c:c + 1],
               reads=["wcolb"], writes=[f"wext:{2 * c}"])
        cx.dma("scalar", "wex", AP(wext_h, E + 128 * c, [[1, 128]]), wcolb[:, c:c + 1],
               reads=["wcolb"], writes=[f"wext:{2 * c + 1}"])
    WEXTK = [f"wext:{i}" for i in range(16)]
    xtbfl = xtb.ap().rearrange("p a b -> p (a b)")
    sqbfl = sqb.ap().rearrange("p a b -> p (a b)")
    MREV_ENG = ["sync", "scalar", "gpsimd", "scalar"]

    def mslot(i):
        s = i % 4
        if s < 2:
            return xtbfl[:, 1024 * s:1024 * (s + 1)], f"xtbh{s}"
        return sqbfl[:, 1024 * (s - 2):1024 * (s - 1)], f"sqbh{s - 2}"

    for g in range(8):
        gp = g % 2
        stgm, mkey = mslot(g)
        ch0 = 2 + 4 * gp
        ch1 = 4 - 4 * gp
        cx.dma(MREV_ENG[g % 4], f"mrev{g % 4}", stgm,
               AP(wext_h, 897 - 128 * g, [[1, 128], [1, E]]),
               reads=WEXTK, writes=[mkey])
        t_mm(pbig[:, ch0:ch0 + 2, :].rearrange("p a b -> p (a b)"), jb[:], stgm[:, 0:512],
             start=True, stop=True, reads=[mkey, "jb"], writes=[BKB(ch0)])
        t_mm(pbig[:, ch1:ch1 + 2, :].rearrange("p a b -> p (a b)"), jb[:], stgm[:, 512:1024],
             start=True, stop=True, reads=[mkey, "jb"], writes=[BKB(ch1)])
        v(lambda e, g=g, ch0=ch0: e.tensor_copy(
                m_sb[:, g, 0:512], pbig[:, ch0:ch0 + 2, :].rearrange("p a b -> p (a b)")),
          [BKB(ch0)], [f"m:{g}"])
        v(lambda e, g=g, ch1=ch1: e.tensor_copy(
                m_sb[:, g, 512:1024], pbig[:, ch1:ch1 + 2, :].rearrange("p a b -> p (a b)")),
          [BKB(ch1)], [f"m:{g}"])
    first = True
    for g in range(8):
        stgm, mkey = mslot(g)
        cx.dma(MREV_ENG[g % 4], f"mrev{g % 4}", stgm,
               AP(wext_h, 1 + 128 * g, [[1, 128], [1, E]]),
               reads=WEXTK, writes=[mkey])
        for q in range(8):
            t_mm(pbig[:, 6 + q // 4, 24 * (q % 4):24 * (q % 4) + 24],
                 stgm[:, 128 * q:128 * (q + 1)], welb[:, g, :],
                 start=first, stop=(g == 7 and q == 7),
                 reads=[mkey, "welb"], writes=[BKB(6)])
            first = False
    for q in range(8):
        v(lambda e, q=q: e.tensor_copy(mwtmp[:, q, :],
                                       pbig[:, 6 + q // 4, 24 * (q % 4):24 * (q % 4) + 24]),
          [BKB(6)], ["mwtmp"])
    for c in range(8):
        t_mm(pbig[:, 0, 0:24], jb[:], mwtmp[:, 7 - c, :], start=True, stop=True,
             reads=["mwtmp", "jb"], writes=[BKB(0)])
        v(lambda e, c=c: e.tensor_copy(mwel[:, c, :], pbig[:, 0, 0:24]),
          [BKB(0)], ["mwel"])
    MK = [f"m:{g}" for g in range(8)]

    for b in range(BPC):
        for c in range(8):
            v(lambda e, b=b, c=c: e.tensor_copy(vecb[:, c, 2 * b:2 * b + 1],
                                                modt[:, 0 + c, b:b + 1]),
              [f"modt:{c}"], ["vecb"])
            v(lambda e, b=b, c=c: e.tensor_scalar(vecb[:, c, 2 * b + 1:2 * b + 2],
                                                  modt[:, 8 + c, b:b + 1], 1.0, None, ALU.add),
              [f"modt:{8 + c}"], ["vecb"])
    for b in range(BPC):
        for vi, dst in ((0, 1), (1, 0)):
            for half in range(2):
                for mc in range(8):
                    t_mm(pbig[0:1, 2 * half:2 * half + 2, :].rearrange("p a b -> p (a b)"),
                         vecb[:, mc, 2 * b + vi:2 * b + vi + 1],
                         m_sb[:, mc, 512 * half:512 * (half + 1)],
                         start=(mc == 0), stop=(mc == 7),
                         reads=[f"m:{mc}", "vecb"], writes=[BKB(2 * half)])
            if dst == 0:
                for half in range(2):
                    v(lambda e, b=b, half=half: e.tensor_copy(
                            rows2[0:1, b, 512 * half:512 * (half + 1)],
                            pbig[0:1, 2 * half:2 * half + 2, :].rearrange("p a b -> p (a b)")),
                      [BKB(2 * half)], [f"rows2:{b}"])
            else:
                for half in range(2):
                    v(lambda e, half=half: e.tensor_copy(
                            tokA[0:1, 0:512],
                            pbig[0:1, 2 * half:2 * half + 2, :].rearrange("p a b -> p (a b)")),
                      [BKB(2 * half)], ["tokA"])
                    cx.dma("gpsimd", "cstg", rows2[1:2, b, 512 * half:512 * (half + 1)],
                           tokA[0:1, 0:512], reads=["tokA"], writes=[f"rows2:{b}"])
        for vi, dst in ((0, 1), (1, 0)):
            for mc in range(8):
                t_mm(pbig[0:1, 4, 0:24], vecb[:, mc, 2 * b + vi:2 * b + vi + 1],
                     mwel[:, mc, :],
                     start=(mc == 0), stop=(mc == 7),
                     reads=["mwel", "vecb"], writes=[BKB(4)])
            if dst == 0:
                v(lambda e, b=b: e.tensor_copy(mwr2[0:1, b, 0:24], pbig[0:1, 4, 0:24]),
                  [BKB(4)], [f"mwr2:{b}"])
            else:
                v(lambda e: e.tensor_copy(tokA[0:1, 0:24], pbig[0:1, 4, 0:24]),
                  [BKB(4)], ["tokA"])
                cx.dma("gpsimd", "cstg", mwr2[1:2, b, 0:24], tokA[0:1, 0:24],
                       reads=["tokA"], writes=[f"mwr2:{b}"])

    cx.dma("sync", "cst3", rbr, rbh[0:1, :], reads=["x2t:0"], writes=["x2t:0"])

    for dc in range(8):
        par = dc % 2
        stgb = stg32.ap()[:, par, 0:256].bitcast(BF16)
        skey = f"stg32:{par}"
        cx.dma("gpsimd", f"rw{par}", stgb, rwh[128 * dc:128 * (dc + 1), :], writes=[skey])
        v(lambda e, dc=dc, stgb=stgb: e.tensor_scalar(
                wg[:, dc, :], stgb, lngc[:, dc:dc + 1], None, ALU.mult),
          [skey, "lngc"], [f"wg:{dc}"])
    WGK = [f"wg:{dc}" for dc in range(8)]
    for half in range(2):
        for dc in range(8):
            t_mm(pbig[0:1, 2 + half, :], onecb[:], wg[:, dc, 256 * half:256 * (half + 1)],
                 start=(dc == 0 and half == 0), stop=(dc == 7 and half == 1),
                 reads=[f"wg:{dc}", "onecb"], writes=[BKB(2)])
    for half in range(2):
        v(lambda e, half=half: e.tensor_copy(csr[0:1, 256 * half:256 * (half + 1)],
                                             pbig[0:1, 2 + half, :]),
          [BKB(2)], ["csr"])
    for half in range(2):
        for dc in range(8):
            par = dc % 2
            stgb = stg32.ap()[:, par, 0:128].bitcast(BF16)
            skey = f"stg32:{par}"
            cx.dma("gpsimd", f"rw2{par}", stgb,
                   rwh[128 * dc:128 * (dc + 1), 256 * half:256 * (half + 1)],
                   writes=[skey])
            t_mm(pbig[0:1, 4 + half, :], lnbcb[:, dc:dc + 1], stgb,
                 start=(dc == 0 and half == 0), stop=(dc == 7 and half == 1),
                 reads=[skey, "lnbcb"], writes=[BKB(4)])
    for half in range(2):
        v(lambda e, half=half: e.tensor_tensor(
                tokA[0:1, 256 * half:256 * (half + 1)], pbig[0:1, 4 + half, :],
                rbr[0:1, 256 * half:256 * (half + 1)], ALU.add),
          [BKB(4), "x2t:0"], ["tokA"])
    v(lambda e: e.tensor_copy(cnr[0:1, :], tokA[0:1, 0:NB]), ["tokA"], ["cnr"])

    for dc in range(8):
        cx.dma("gpsimd", "ut0", ut[:, dc, :], uth[128 * dc:128 * (dc + 1), :],
               writes=[f"ut:{dc}"])
    usq = gelut.ap()[:, 0, 0:2, :].rearrange("p a b -> p (a b)")
    USQK = ["gelut:0:0", "gelut:0:1"]
    first = True
    for dc in range(8):
        s(lambda e, dc=dc: e.activation(usq, ut[:, dc, :], AF.Square),
          [f"ut:{dc}"], USQK)
        t_mm(pbig[0:1, 0, :], onecb[:], usq[:, 0:256], start=first, stop=False,
             reads=USQK + ["onecb"], writes=[BKB(0)])
        t_mm(pbig[0:1, 1, :], onecb[:], usq[:, 256:512], start=False, stop=(dc == 7),
             reads=USQK + ["onecb"], writes=[BKB(0)])
        first = False
    un_r = RW.get("unrm", T)
    first = True
    preload_tab(6)
    for half in range(2):
        s(lambda e, half=half: e.activation(un_r, pbig[0:1, half, :], AF.Ln),
          [BKB(0)], ["unr"])
        s(lambda e: e.activation(un_r, un_r, AF.Exp, scale=cvals[0:1, 6:7]), ["unr", "cvals"], ["unr"])
        v(lambda e: e.tensor_scalar(un_r, un_r, 1e-6, None, ALU.max), ["unr"], ["unr"])
        v(lambda e: e.reciprocal(un_r, un_r), ["unr"], ["unr"])
        t_mm(pbig[:, 2 + half, :], onerf[:], un_r, start=first, stop=(half == 1),
             reads=["unr", "onerf"], writes=[BKB(2)])
        first = False
    for dc in range(8):
        v(lambda e, dc=dc: e.scalar_tensor_tensor(
                ut[:, dc, :], ut[:, dc, :], 1.0,
                pbig[:, 2:4, :].rearrange("p a b -> p (a b)"), ALU.mult, ALU.mult),
          [f"ut:{dc}", BKB(2)], [f"ut:{dc}"])
    RW.drop("unrm")
    UTK = [f"ut:{dc}" for dc in range(8)]

    cx.dma("sync", "cst4", gamr, gamh[0:1, :], reads=["x2t:0"], writes=["x2t:0"])
    for half in range(2):
        t_mm(pbig[:, 4 + 2 * half:6 + 2 * half, :].rearrange("p a b -> p (a b)"),
             onerf[:], gamr[0:1, 512 * half:512 * (half + 1)],
             start=True, stop=True,
             reads=["x2t:0", "onerf"], writes=[BKB(4) if half == 0 else BKB(6)])
    vst = gelut.ap()[:, 1, :, :].rearrange("p a b -> p (a b)")
    VSTK = [f"gelut:1:{j}" for j in range(4)]
    for vt in range(4):
        cx.dma("gpsimd", "v0", vst, vh[128 * vt:128 * (vt + 1), :], writes=VSTK)
        s(lambda e: e.activation(usq, vst[:, 0:512], AF.Square, accum_out=mcol[:, 2:3]),
          VSTK, USQK + ["mcol"])
        s(lambda e: e.activation(usq, vst[:, 512:1024], AF.Square, accum_out=mcol[:, 3:4]),
          VSTK, USQK + ["mcol"])
        v(lambda e: e.tensor_tensor(mcol[:, 4:5], mcol[:, 2:3], mcol[:, 3:4], ALU.add),
          ["mcol"], ["mcol"])
        s(lambda e: e.activation(mcol[:, 5:6], mcol[:, 4:5], AF.Ln), ["mcol"], ["mcol"])
        s(lambda e: e.activation(mcol[:, 5:6], mcol[:, 5:6], AF.Exp, scale=cvals[:, 6:7]),
          ["mcol", "cvals"], ["mcol"])
        v(lambda e: e.tensor_scalar(mcol[:, 5:6], mcol[:, 5:6], 1e-6, None, ALU.max),
          ["mcol"], ["mcol"])
        v(lambda e: e.reciprocal(mcol[:, 6:7], mcol[:, 5:6]), ["mcol"], ["mcol"])
        v(lambda e, vt=vt: e.scalar_tensor_tensor(
                vg[:, vt, :], vst, mcol[:, 6:7],
                pbig[:, 4:8, :].rearrange("p a b -> p (a b)"), ALU.mult, ALU.mult),
          VSTK + ["mcol", BKB(4), BKB(6)], [f"vg:{vt}"])
    VGK = [f"vg:{vt}" for vt in range(4)]

    def emit_weight_tail():
        for dc in range(8):
            cx.dma("gpsimd", "w1", w1t[:, dc, :], w1h[128 * dc:128 * (dc + 1), :],
                   reads=(["mwel"] if dc == 0 else []), writes=[f"w1t:{dc}"])
        for hc in range(32):
            cx.dma("gpsimd", "w2cv", AP(w2s_h, 128 * E * hc, [[E, 128], [1, E]]),
                   w2h[128 * hc:128 * (hc + 1), :], writes=[f"w2scr:{hc}"])

    VL = {"a": 0, "cb1": 1, "ncb1": 2, "ga": 3, "gb": 4, "t0": 5, "t1": 6}

    aSR1 = pbig.ap()[0:1, 6, :]
    aSR2 = pbig.ap()[32:33, 6, :]
    aPRHO1 = pbig.ap()[:, 7, :]
    aLAT = pbig.ap()[0:24, 6, :]
    aSYN = pbig.ap()[32:44, 6, :]
    aWS = pbig.ap()[64:65, 6, :]
    aFIX = pbig.ap()[0:12, 7, :]
    aPRHO2 = pbig.ap()[:, 6, :]
    aPMRHO = pbig.ap()[:, 7, :]
    aTOK = pbig.ap()[:, 4:6, :].rearrange("p a b -> p (a b)")
    aZT = pbig.ap()[:, 6, :].bitcast(BF16)
    aZT2 = pbig.ap()[:, 4, :].bitcast(BF16)
    aCOL = pbig.ap()[:, 7, :]
    BK2, BK3 = ("pb", 2), ("pb", 3)

    def batch_prep(b, par):
        kb = f"vecs:{par}"
        v(lambda e: e.tensor_scalar(vecs[:, par, VL["a"], :], modt[:, 8:16, b], 1.0, None, ALU.add),
          MODK, [kb])
        v(lambda e: e.tensor_scalar(vecs[:, par, VL["cb1"], :], modt[:, 32:40, b], 1.0, None, ALU.add),
          MODK, [kb])
        v(lambda e: e.tensor_scalar(vecs[:, par, VL["ncb1"], :], vecs[:, par, VL["cb1"], :], -1.0,
                                    None, ALU.mult),
          [kb], [kb])
        for nm, base in (("ga", 16), ("gb", 40)):
            s(lambda e, base=base: e.activation(vecs[:, par, VL["t0"], :], modt[:, base:base + 8, b],
                                                AF.Abs),
              MODK, [kb])
            v(lambda e: e.tensor_scalar(vecs[:, par, VL["t0"], :], vecs[:, par, VL["t0"], :], 0.25,
                                        None, ALU.max),
              [kb], [kb])
            v(lambda e, base=base: e.tensor_scalar(vecs[:, par, VL["t1"], :],
                                                   modt[:, base:base + 8, b], 0.0, None, ALU.is_ge),
              MODK, [kb])
            v(lambda e: e.tensor_scalar(vecs[:, par, VL["t1"], :], vecs[:, par, VL["t1"], :], 2.0, -1.0,
                                        ALU.mult, ALU.add),
              [kb], [kb])
            v(lambda e, nm=nm: e.tensor_tensor(vecs[:, par, VL[nm], :], vecs[:, par, VL["t0"], :],
                                               vecs[:, par, VL["t1"], :], ALU.mult),
              [kb], [kb])

    n_tiles = BPC * ntpb

    def emit_xin(ti):
        if ti >= n_tiles:
            return
        b = ti // ntpb
        t0 = (ti % ntpb) * T
        par = ti % 2
        for c in range(8):
            cx.dma("sync", "xin", xt[:, c, :], xh[b, 128 * c:128 * (c + 1), t0:t0 + T],
                   writes=[f"xt:{c}"])
        xtbw = ["xtb"] + (["xtbh0", "xtbh1"] if ti == 0 else [])
        if ti == 1:
            v(lambda e: e.tensor_copy(xtb.ap().rearrange("p a b -> p (a b)"),
                                      xt.ap().rearrange("p a b -> p (a b)")),
              [f"xt:{c}" for c in range(8)], xtbw)
            return
        for c in range(8):
            cx.dma("gpsimd", f"xinb{par}", xtb[:, c, :],
                   xh[b, 128 * c:128 * (c + 1), t0:t0 + T], writes=xtbw)

    def phaseA(ti):
        b = ti // ntpb
        t0 = (ti % ntpb) * T
        par = ti % 2
        KB = [f"vecs:{b % 2}"]
        X2W = f"x2t:{par}"
        if ti % ntpb == 0:
            batch_prep(b, b % 2)
        yield
        XK = [f"xt:{c}" for c in range(8)]
        yield

        sqbw = ["sqb"] + (["sqbh0", "sqbh1"] if ti == 0 else [])
        s(lambda e: e.activation(sqb.ap().rearrange("p a b -> p (a b)"),
                                 xtb.ap().rearrange("p a b -> p (a b)"), AF.Square),
          ["xtb"], sqbw)
        for c in range(8):
            t_mm(aSR1, onecb[:], xtb[:, c, :], start=(c == 0), stop=(c == 7),
                 reads=["xtb", "onecb"], writes=[BK3])
        for c in range(8):
            t_mm(aSR2, onecb[:], sqb[:, c, :], start=(c == 0), stop=(c == 7),
                 reads=["sqb", "onecb"], writes=[BK3])
        yield
        r_mu = RW.get("r_mu")
        r_rho = RW.get("r_rho")
        tv = RW.get("r_tv")
        v(lambda e: e.tensor_scalar(r_mu, aSR1, 1.0 / E, None, ALU.mult), [BK3], ["r_mu"])
        v(lambda e: e.tensor_scalar(tv, aSR2, 1.0 / E, None, ALU.mult), [BK3], ["r_tv"])
        v(lambda e: e.tensor_tensor(r_rho, r_mu, r_mu, ALU.mult), ["r_mu"], ["r_rho"])
        v(lambda e: e.tensor_tensor(tv, tv, r_rho, ALU.subtract), ["r_tv", "r_rho"], ["r_tv"])
        preload_tab(6)
        s(lambda e: e.activation(tv, tv, AF.Ln, bias=cvals[0:1, 0:1]), ["r_tv", "cvals"], ["r_tv"])
        s(lambda e: e.activation(r_rho, tv, AF.Exp, scale=cvals[0:1, 1:2]), ["r_tv", "cvals"], ["r_rho"])
        preload_tab(10)
        v(lambda e: e.tensor_tensor(tv, r_mu, r_rho, ALU.mult), ["r_mu", "r_rho"], ["r_tv"])
        v(lambda e: e.tensor_scalar(tv, tv, -1.0, None, ALU.mult), ["r_tv"], ["r_tv"])
        v(lambda e: e.tensor_copy(r2x[0:1, :], tv), ["r_tv"], ["r2x"])
        rhob = RB.get("rhob")
        v(lambda e: e.tensor_copy(rhob, r_rho), ["r_rho"], ["rhob"])
        t_mm(aPRHO1, onerb[0:1, 0:128], rhob, start=True, stop=True,
             reads=["rhob", "onerb"], writes=[BK3])
        for c in range(8):
            v(lambda e, c=c: e.scalar_tensor_tensor(rhs1[:, c, :], xtb[:, c, :],
                                                    vecs[:, b % 2, VL["a"], c:c + 1], aPRHO1,
                                                    ALU.mult, ALU.mult),
              ["xtb", BK3] + KB, [f"rhs1:{c}"])
        yield

        for mc in range(8):
            t_mm(aLAT, mwel[:, mc, :], rhs1[:, mc, :],
                 start=(mc == 0), stop=False, reads=["mwel", f"rhs1:{mc}"], writes=[BK3])
        t_mm(aLAT, mwr2[:, b, 0:24], r2x[:], start=False, stop=True,
             reads=[f"mwr2:{b}", "r2x"], writes=[BK3])
        v(lambda e: e.tensor_scalar(corrb[0:24, :], aLAT, 0.0, None, ALU.is_gt),
          [BK3], ["corrb"])
        t_mm(aSYN, ht_sb[:], corrb[0:24, :], start=True, stop=True,
             reads=["corrb", "ht_sb"], writes=[BK3])
        v(lambda e: e.tensor_scalar(synb[:], aSYN, 0.5, None, ALU.is_gt), [BK3], ["synb"])
        t_mm(aWS, onecb[0:12, :], synb[:], start=True, stop=True,
             reads=["synb", "onecb"], writes=[BK3])
        fx = RW.get("r_fx")
        v(lambda e: e.tensor_scalar(fx, aWS, 3.5, None, ALU.is_le), [BK3], ["r_fx"])
        v(lambda e: e.scalar_tensor_tensor(fx, aWS, 0.5, fx, ALU.is_ge, ALU.mult),
          [BK3, "r_fx"], ["r_fx"])
        t_mm(aFIX, onerf[0:1, 0:12], fx, start=True, stop=True,
             reads=["r_fx", "onerf"], writes=[BK3])
        v(lambda e: e.tensor_tensor(golA[0:12, :], corrb[0:12, :], synb[:], ALU.mult),
          ["corrb", "synb"], ["golA"])
        v(lambda e: e.scalar_tensor_tensor(golA[0:12, :], golA[0:12, :], -2.0, synb[:],
                                           ALU.mult, ALU.add),
          ["golA", "synb"], ["golA"])
        v(lambda e: e.tensor_tensor(golA[0:12, :], golA[0:12, :], aFIX, ALU.mult),
          ["golA", BK3], ["golA"])
        v(lambda e: e.tensor_tensor(corrb[0:12, :], corrb[0:12, :], golA[0:12, :], ALU.add),
          ["golA", "corrb"], ["corrb"])
        yield

        for half in range(2):
            for cc in range(4):
                nc_ = 4 * half + cc
                for mc in range(8):
                    t_mm(pbig[:, 4 + cc, :], m_sb[:, mc, 128 * nc_:128 * (nc_ + 1)],
                         rhs1[:, mc, :],
                         start=(mc == 0 and cc % 2 == 0), stop=False,
                         reads=[f"m:{mc}", f"rhs1:{mc}"], writes=[pbank("x", 4 + cc) if False else ("pb", (4 + cc) // 2)])
                t_mm(pbig[:, 4 + cc, :], rows2[:, b, 128 * nc_:128 * (nc_ + 1)], r2x[:],
                     start=False, stop=False,
                     reads=[f"rows2:{b}", "r2x"], writes=[("pb", (4 + cc) // 2)])
            for cc in range(4):
                nc_ = 4 * half + cc
                t_mm(pbig[:, 4 + cc, :], wle_sb[:, 128 * nc_:128 * (nc_ + 1)], corrb[:],
                     start=False, stop=(cc % 2 == 1),
                     reads=["corrb", "corr24", "wle_sb"], writes=[("pb", (4 + cc) // 2)])
            for cc in range(4):
                nc_ = 4 * half + cc
                v(lambda e, nc_=nc_, cc=cc: e.scalar_tensor_tensor(
                        x2t[:, par, nc_, :], pbig[:, 4 + cc, :],
                        vecs[:, b % 2, VL["ga"], nc_:nc_ + 1], xt[:, nc_, :],
                        ALU.mult, ALU.add),
                  [("pb", (4 + cc) // 2)] + KB + XK, [X2W])
            yield

        s(lambda e: e.activation(sqb.ap().rearrange("p a b -> p (a b)"),
                                 x2t.ap()[:, par, :, :].rearrange("p a b -> p (a b)"),
                                 AF.Square),
          [X2W], ["sqb"])
        v(lambda e: e.tensor_copy(xtb.ap().rearrange("p a b -> p (a b)"),
                                  x2t.ap()[:, par, :, :].rearrange("p a b -> p (a b)")),
          [X2W] + [f"rhs1:{c}" for c in range(8)], ["xtb"])
        for c in range(8):
            t_mm(aSR1, onecb[:], xtb[:, c, :], start=(c == 0), stop=(c == 7),
                 reads=["xtb", "onecb"], writes=[BK3])
        for c in range(8):
            t_mm(aSR2, onecb[:], sqb[:, c, :], start=(c == 0), stop=(c == 7),
                 reads=["sqb", "onecb"], writes=[BK3])
        yield
        r_mu = RW.get("r_mu")
        r_rho = RW.get("r_rho")
        v(lambda e: e.tensor_scalar(r_mu, aSR1, 1.0 / E, None, ALU.mult), [BK3], ["r_mu"])
        v(lambda e: e.tensor_scalar(tv, aSR2, 1.0 / E, None, ALU.mult), [BK3], ["r_tv"])
        v(lambda e: e.tensor_tensor(r_rho, r_mu, r_mu, ALU.mult), ["r_mu"], ["r_rho"])
        v(lambda e: e.tensor_tensor(tv, tv, r_rho, ALU.subtract), ["r_tv", "r_rho"], ["r_tv"])
        preload_tab(6)
        s(lambda e: e.activation(tv, tv, AF.Ln, bias=cvals[0:1, 0:1]), ["r_tv", "cvals"], ["r_tv"])
        s(lambda e: e.activation(r_rho, tv, AF.Exp, scale=cvals[0:1, 1:2]), ["r_tv", "cvals"], ["r_rho"])
        preload_tab(10)
        v(lambda e: e.tensor_tensor(r_mu, r_mu, r_rho, ALU.mult), ["r_mu", "r_rho"], ["r_mu"])
        v(lambda e: e.tensor_copy(rhob, r_rho), ["r_rho"], ["rhob"])
        mrhob = RB.get("mrhob")
        v(lambda e: e.tensor_copy(mrhob, r_mu), ["r_mu"], ["mrhob"])
        t_mm(aPRHO2, onerb[0:1, 0:128], rhob, start=True, stop=False,
             reads=["rhob", "onerb"], writes=[BK3])
        t_mm(aPMRHO, onerb[0:1, 0:128], mrhob, start=False, stop=True,
             reads=["mrhob", "onerb"], writes=[BK3])
        for c in range(8):
            v(lambda e, c=c: e.scalar_tensor_tensor(sqb[:, c, :], xtb[:, c, :],
                                                    vecs[:, b % 2, VL["cb1"], c:c + 1], aPRHO2,
                                                    ALU.mult, ALU.mult),
              ["xtb", BK3] + KB, ["sqb"])
            v(lambda e, c=c: e.tensor_scalar(stg32[:, 0, 0:T], aPMRHO,
                                             vecs[:, b % 2, VL["ncb1"], c:c + 1],
                                             modt[:, 24 + c, b:b + 1], ALU.mult, ALU.add),
              [BK3, f"modt:{24 + c}"] + KB, ["stg32:0"])
            v(lambda e, c=c: e.tensor_tensor(hffnt[:, par, c, :], sqb[:, c, :],
                                             stg32[:, 0, 0:T], ALU.add),
              ["sqb", "stg32:0"], [f"hffnt:{par}:{c}"])
        yield

        HFK = [f"hffnt:{par}:{c}" for c in range(8)]
        s(lambda e: e.activation(sqb.ap().rearrange("p a b -> p (a b)"),
                                 hffnt.ap()[:, par, :, :].rearrange("p a b -> p (a b)"), AF.Square),
          HFK, ["sqb"])
        for c in range(8):
            t_mm(aSR1, onecb[:], hffnt[:, par, c, :], start=(c == 0), stop=(c == 7),
                 reads=[f"hffnt:{par}:{c}", "onecb"], writes=[BK3])
        for c in range(8):
            t_mm(aSR2, onecb[:], sqb[:, c, :], start=(c == 0), stop=(c == 7),
                 reads=["sqb", "onecb"], writes=[BK3])
        yield
        r_mu = RW.get("r_mu")
        r_rho = RW.get("r_rho")
        v(lambda e: e.tensor_scalar(r_mu, aSR1, 1.0 / E, None, ALU.mult), [BK3], ["r_mu"])
        v(lambda e: e.tensor_scalar(tv, aSR2, 1.0 / E, None, ALU.mult), [BK3], ["r_tv"])
        v(lambda e: e.tensor_tensor(r_rho, r_mu, r_mu, ALU.mult), ["r_mu"], ["r_rho"])
        v(lambda e: e.tensor_tensor(tv, tv, r_rho, ALU.subtract), ["r_tv", "r_rho"], ["r_tv"])
        preload_tab(6)
        s(lambda e: e.activation(tv, tv, AF.Ln, bias=cvals[0:1, 0:1]), ["r_tv", "cvals"], ["r_tv"])
        s(lambda e: e.activation(r_rho, tv, AF.Exp, scale=cvals[0:1, 1:2]), ["r_tv", "cvals"], ["r_rho"])
        preload_tab(10)
        nm3b = RB.get("nm3b")
        rc3b = RB.get("rc3b")
        v(lambda e: e.tensor_scalar(nm3b, r_mu, -1.0, None, ALU.mult), ["r_mu"], ["nm3b"])
        v(lambda e: e.reciprocal(tv, r_rho), ["r_rho"], ["r_tv"])
        v(lambda e: e.tensor_copy(rc3b, tv), ["r_tv"], ["rc3b"])
        yield

        for hf in range(2):
            tsl = slice(128 * hf, 128 * (hf + 1))
            for dc in range(8):
                t_mm(aTOK[:, 0:NB], hffnt[:, par, dc, tsl], wg[:, dc, :],
                     start=(dc == 0), stop=False,
                     reads=[f"hffnt:{par}:{dc}"] + WGK, writes=[BK2])
            t_tr(aCOL[:, 0:1], r_rho[0:1, tsl], idf[0:1, 0:1],
                 reads=["r_rho", "idf"], writes=[BK3])
            v(lambda e: e.tensor_copy(colb[:, 0:1], aCOL[:, 0:1]), [BK3], ["colb"])
            t_mm(aTOK[:, 0:NB], nm3b[0:1, tsl], csr[0:1, :], start=False, stop=False,
                 reads=["nm3b", "csr"], writes=[BK2])
            t_mm(aTOK[:, 0:NB], rc3b[0:1, tsl], cnr[0:1, :], start=False, stop=True,
                 reads=["rc3b", "cnr"], writes=[BK2])
            v(lambda e: e.tensor_scalar(tokA[:], aTOK[:, 0:NB], colb[:, 0:1], 10.0,
                                        ALU.mult, ALU.min),
              [BK2, "colb"], ["tokA"])
            v(lambda e: e.tensor_scalar(stg32[:, 1, :], tokA[:], -10.0, None, ALU.max),
              ["tokA"], ["stg32:1"])
            preload_tab(6)
            s(lambda e: e.activation(tokA[:], stg32[:, 1, :], AF.Exp), ["stg32:1"], ["tokA"])
            s(lambda e: e.activation(stg32[:, 1, :], tokA[:], AF.Ln, bias=cvals[:, 7:8]),
              ["tokA", "cvals"], ["stg32:1"])
            v(lambda e: e.max(phib[:, 0:8], stg32[:, 1, :]), ["stg32:1"], ["phib"])
            v(lambda e: e.reduce_sum(colb[:, 1:2], phib[:, 0:8], axis=AXX), ["phib"], ["colb"])
            s(lambda e: e.activation(colb[:, 2:3], colb[:, 1:2], AF.Exp, scale=cvals[:, 5:6]),
              ["colb", "cvals"], ["colb"])
            preload_tab(10)
            v(lambda e: e.tensor_scalar(colb[:, 3:4], colb[:, 2:3], -1.0, 1.0, ALU.mult, ALU.add),
              ["colb"], ["colb"])
            v(lambda e: e.tensor_scalar(colb[:, 4:5], colb[:, 2:3], 1.0, None, ALU.add),
              ["colb"], ["colb"])
            v(lambda e: e.tensor_scalar(colb[:, 5:6], colb[:, 1:2], 1e-6, None, ALU.add),
              ["colb"], ["colb"])
            v(lambda e: e.tensor_tensor(colb[:, 6:7], colb[:, 4:5], colb[:, 5:6], ALU.mult),
              ["colb"], ["colb"])
            v(lambda e: e.reciprocal(colb[:, 7:8], colb[:, 6:7]), ["colb"], ["colb"])
            v(lambda e: e.tensor_tensor(colb[:, 8:9], colb[:, 3:4], colb[:, 7:8], ALU.mult),
              ["colb"], ["colb"])
            v(lambda e: e.scalar_tensor_tensor(tokA[:], stg32[:, 1, :], phib[:, 7:8], stg32[:, 1, :],
                                               ALU.is_ge, ALU.mult),
              ["stg32:1", "phib"], ["tokA"])
            yield
            for dc in range(8):
                t_mm(aTOK[:, 0:NB], hffnt[:, par, dc, tsl], ut[:, dc, :],
                     start=(dc == 0), stop=(dc == 7),
                     reads=[f"hffnt:{par}:{dc}"] + UTK + ["tokA"], writes=[BK2])
            v(lambda e: e.scalar_tensor_tensor(hlzb[:], aTOK[:, 0:NB], colb[:, 8:9],
                                               tokA[:], ALU.mult, ALU.mult),
              [BK2, "colb", "tokA"], ["hlzb"])
            for uc in range(4):
                cell, bk = (aZT, BK3) if uc % 2 == 0 else (aZT2, BK2)
                t_tr(cell[:, 0:128], hlzb[:, 128 * uc:128 * (uc + 1)], idb[:],
                     reads=["hlzb", "idb"], writes=[bk])
                v(lambda e, uc=uc, hf=hf, cell=cell: e.tensor_copy(
                        zht[:, par, uc, tsl], cell[:, 0:128]),
                  [bk], [f"zht:{par}"])
            yield


    def phaseB(ti, gen):
        b = ti // ntpb
        t0 = (ti % ntpb) * T
        par = ti % 2
        HFK = [f"hffnt:{par}:{c}" for c in range(8)]

        def pump(n=1):
            for _ in range(n):
                if gen is not None:
                    try:
                        next(gen)
                    except StopIteration:
                        break

        def w2_load(w):
            wpar = w % 2
            for j in range(4):
                hc = 4 * w + j
                cx.dma("sync", f"w2{wpar}", w2tw[:, wpar, j, :],
                       AP(w2s_h, 128 * E * hc, [[E, 128], [1, E]]),
                       reads=[f"w2scr:{hc}"], writes=[f"w2tw:{wpar}"])

        w2_load(0)
        w2_load(1)
        pump(3)
        for w in range(8):
            wpar = w % 2
            GK = [f"gelut:{wpar}:{j}" for j in range(4)]
            for j in (0, 2, 1, 3):
                i = 4 * w + j
                for dc in range(8):
                    t_mm(pbig[:, j, :], w1t[:, dc, 128 * i:128 * (i + 1)], hffnt[:, par, dc, :],
                         start=(dc == 0), stop=(dc == 7),
                         reads=[f"w1t:{dc}", f"hffnt:{par}:{dc}"], writes=[BKB(j)])
                s(lambda e, j=j, wpar=wpar: e.activation(gelut[:, wpar, j, :], pbig[:, j, :],
                                                         AF.Gelu),
                  [BKB(j)], [f"gelut:{wpar}:{j}"])
            if w == 0:
                pump(1)
            for j in range(4):
                for nc2 in range(8):
                    t_mm(pout[:, nc2, :], w2tw[:, wpar, j, 128 * nc2:128 * (nc2 + 1)],
                         gelut[:, wpar, j, :],
                         start=(w == 0 and j == 0 and nc2 % 2 == 0), stop=False,
                         reads=[f"w2tw:{wpar}", f"gelut:{wpar}:{j}"], writes=[BKO(nc2)])
            if w < 6:
                w2_load(w + 2)
            if w == 3:
                emit_xin(ti + 2)
            pump({0: 1, 1: 2, 2: 2, 3: 2, 4: 1, 5: 2, 6: 1, 7: 0}[w])
        for uc in range(4):
            for nc2 in range(8):
                t_mm(pout[:, nc2, :], vg[:, uc, 128 * nc2:128 * (nc2 + 1)], zht[:, par, uc, :],
                     start=False, stop=(uc == 3 and nc2 % 2 == 1),
                     reads=VGK + [f"zht:{par}"], writes=[BKO(nc2)])
        pump(2)
        for c in range(8):
            st_ = c % 4
            v(lambda e, c=c, st_=st_: e.scalar_tensor_tensor(
                    x3st[:, st_, :], pout[:, c, :], vecs[:, b % 2, VL["gb"], c:c + 1],
                    x2t[:, par, c, :], ALU.mult, ALU.add),
              [BKO(c), f"vecs:{b % 2}", f"x2t:{par}"], [f"x3st:{st_}"])
            cx.dma("gpsimd", f"out{st_}", oh[b, 128 * c:128 * (c + 1), t0:t0 + T],
                   x3st[:, st_, :], reads=[f"x3st:{st_}"], writes=[])
        pump(100)

    gen = phaseA(0)
    next(gen)
    next(gen)
    emit_xin(0)
    emit_weight_tail()
    for _ in range(100):
        try:
            next(gen)
        except StopIteration:
            break
    emit_xin(1)
    for ti in range(n_tiles):
        gen = phaseA(ti + 1) if ti + 1 < n_tiles else None
        phaseB(ti, gen)

    cx.finish("sync")
    return nc


def kernel(L0_attn_x, L0_attn_W_el, L0_attn_b_el, L0_attn_W_le, L0_attn_b_le,
           L0_ffn_W1, L0_ffn_W2, L0_ffn_ln_g, L0_ffn_ln_b, L0_ffn_router_W,
           L0_ffn_router_b, L0_ffn_raw_U, L0_ffn_raw_V, L0_ffn_gamma,
           L0_c, L0_wmod):
    COS, SIN, ka, Hm, lx = _consts()
    x = np.asarray(L0_attn_x, dtype=np.float32)
    xT = np.ascontiguousarray(x.transpose(0, 2, 1))
    x00row = np.ascontiguousarray(x[0, 0, :]).reshape(1, E)
    wle_ext = np.ascontiguousarray(
        np.concatenate([np.asarray(L0_attn_W_le, np.float32),
                        np.asarray(L0_attn_b_le, np.float32)[None, :]], axis=0))
    common = {
        "x00row": x00row,
        "wmod": np.ascontiguousarray(np.asarray(L0_wmod, np.float32)),
        "W_el": np.ascontiguousarray(np.asarray(L0_attn_W_el, np.float32)),
        "Wle_ext": wle_ext,
        "W1T": np.ascontiguousarray(np.asarray(L0_ffn_W1, np.float32).T),
        "W2T": np.ascontiguousarray(np.asarray(L0_ffn_W2, np.float32).T),
        "RWT": np.ascontiguousarray(np.asarray(L0_ffn_router_W, np.float32).T),
        "UT": np.ascontiguousarray(np.asarray(L0_ffn_raw_U, np.float32).T),
        "V": np.ascontiguousarray(np.asarray(L0_ffn_raw_V, np.float32)),
        "ln_gc": np.ascontiguousarray(np.asarray(L0_ffn_ln_g, np.float32).reshape(8, 128).T),
        "ln_bc": np.ascontiguousarray(np.asarray(L0_ffn_ln_b, np.float32).reshape(8, 128).T),
        "router_b": np.ascontiguousarray(np.asarray(L0_ffn_router_b, np.float32).reshape(1, NB)),
        "gamma_r": np.ascontiguousarray(np.asarray(L0_ffn_gamma, np.float32).reshape(1, E)),
        "COS": COS, "SIN": SIN,
        "ka_r": ka.reshape(1, E),
        "ka2_r": (ka * ka).reshape(1, E),
        "lx_r": lx.reshape(1, 5),
        "HT": np.ascontiguousarray(Hm.T),
    }
    nc = build_nc()
    cf = np.asarray(L0_c, np.float32)
    in_maps = []
    for k in range(NCORES):
        m = dict(common)
        m["xT"] = np.ascontiguousarray(xT[BPC * k:BPC * (k + 1)])
        m["cT"] = np.ascontiguousarray(
            np.stack([cf[BPC * k], cf[BPC * k + 1], cf[0], cf[0]], axis=1))
        in_maps.append(m)
    res = run_bass_kernel_spmd(nc, in_maps, core_ids=list(range(NCORES)))
    outs = [np.asarray(r["outT"]).reshape(BPC, E, SEQ).transpose(0, 2, 1)
            for r in res.results]
    return np.ascontiguousarray(np.concatenate(outs, axis=0)).astype(np.float32)


# version 24 (final)
# speedup vs baseline: 1.0127x; 1.0059x over previous
import numpy as np
import concourse.bass as bass
import concourse.mybir as mybir
from concourse.ap import AP
from concourse.bass_utils import run_bass_kernel_spmd

F32 = mybir.dt.float32
BF16 = mybir.dt.bfloat16
AF = mybir.ActivationFunctionType
ALU = mybir.AluOpType
AXX = mybir.AxisListType.X

E = 1024
T = 256
PRELOADS = True
SEQ = 2048
BPC = 2
NCORES = 8
HID = 4096
NB = 512
PI = float(np.pi)


class Ctx:

    ENGINES = ("tensor", "vector", "scalar", "gpsimd", "sync")

    def __init__(self, nc):
        self.nc = nc
        self.eng = {"tensor": nc.tensor, "vector": nc.vector,
                    "scalar": nc.scalar, "gpsimd": nc.gpsimd, "sync": nc.sync}
        self.eng_sem = {}
        self.instrs = {e: [] for e in self.ENGINES}
        self.inc_total = {e: 0 for e in self.ENGINES}
        self.drained_upto = {e: 0 for e in self.ENGINES}
        self.last_write = {}
        self.readers = {}
        self.waited = {}
        self.dma_sems = {}

    def sem_for(self, engname):
        if engname not in self.eng_sem:
            self.eng_sem[engname] = self.nc.alloc_semaphore(name=f"es_{engname}")
        return self.eng_sem[engname]

    def stream(self, name):
        if name not in self.dma_sems:
            self.dma_sems[name] = [self.nc.alloc_semaphore(name=f"ds_{name}"), 0]
        return self.dma_sems[name]

    def _ensure_inc(self, engname, idx):
        lst = self.instrs[engname]
        for j in range(idx, len(lst)):
            if lst[j][1] is not None:
                return lst[j][1]
        instr, _ = lst[idx]
        self.inc_total[engname] += 1
        v = self.inc_total[engname]
        instr.then_inc(self.sem_for(engname), 1)
        lst[idx] = (instr, v)
        return v

    def _wait(self, waiter, sem, val):
        k = (waiter, id(sem))
        if self.waited.get(k, 0) >= val:
            return
        self.eng[waiter].wait_ge(sem, val)
        self.waited[k] = val

    def _dep(self, waiter, prod, pe_skip):
        if prod[0] == "eng":
            _, pname, pidx = prod
            if pname == waiter:
                if pname == "tensor" and pe_skip:
                    return
                if pidx >= self.drained_upto[pname]:
                    ins = self.eng[pname].drain()
                    self.instrs[pname].append((ins, None))
                    self.drained_upto[pname] = len(self.instrs[pname])
            else:
                v = self._ensure_inc(pname, pidx)
                self._wait(waiter, self.sem_for(pname), v)
        else:
            _, sname, _cnt = prod
            sem, total = self.dma_sems[sname]
            self._wait(waiter, sem, 16 * total)

    def _edges(self, waiter, reads, writes, pe_skip):
        seen = set()

        def dep1(p):
            if p is None or p in seen:
                return
            seen.add(p)
            self._dep(waiter, p, pe_skip)

        for k in reads:
            dep1(self.last_write.get(k))
        for k in writes:
            dep1(self.last_write.get(k))
            for p2 in self.readers.get(k, ()):
                dep1(p2)

    def _record(self, rec, reads, writes):
        for k in writes:
            self.last_write[k] = rec
            self.readers[k] = []
        for k in reads:
            self.readers.setdefault(k, []).append(rec)

    def op(self, engname, fn, reads=(), writes=(), pe_skip=False):
        self._edges(engname, reads, writes, pe_skip)
        ins = fn(self.eng[engname])
        idx = len(self.instrs[engname])
        self.instrs[engname].append((ins, None))
        self._record(("eng", engname, idx), reads, writes)
        return ins

    def dma(self, engname, streamname, out, in_, reads=(), writes=()):
        self._edges(engname, reads, writes, False)
        st = self.stream(streamname)
        ins = self.eng[engname].dma_start(out=out, in_=in_)
        ins.then_inc(st[0], 16)
        st[1] += 1
        self.instrs[engname].append((ins, None))
        self._record(("dma", streamname, st[1]), reads, writes)
        return ins

    def finish(self, waiter="sync"):
        for name in self.dma_sems:
            sem, total = self.dma_sems[name]
            if total:
                self._wait(waiter, sem, 16 * total)


class RowAlloc:
    def __init__(self, t, nslots, width):
        self.t, self.width = t, width
        self.free = list(range(nslots))
        self.named = {}

    def get(self, name, n=None):
        if name not in self.named:
            self.named[name] = self.free.pop(0)
        return self.t[0:1, self.named[name], 0:(n or self.width)]

    def drop(self, *names):
        for name in names:
            if name in self.named:
                self.free.append(self.named.pop(name))


def _consts():
    k = np.arange(E)
    km = (np.outer(k, k) % E).astype(np.float64)
    ang = 2.0 * np.pi * km / E
    COS = np.cos(ang).astype(np.float32)
    SIN = np.sin(ang).astype(np.float32)
    ka = (np.abs(np.fft.fftfreq(E)) + 1e-8).astype(np.float32)
    A12 = np.array([[1,1,1,0,1,1,1,0,0,0,1,1],[1,1,0,1,1,1,0,0,0,1,1,0],
                    [1,0,1,1,1,0,0,0,1,1,0,1],[0,1,1,1,0,0,0,1,1,0,1,1],
                    [0,0,0,0,1,1,1,1,1,1,1,1],[1,1,1,1,0,0,0,0,0,0,0,0],
                    [0,0,0,0,1,1,1,1,0,0,0,0],[1,1,0,0,1,1,0,0,1,1,0,0],
                    [1,0,1,0,1,0,1,0,1,0,1,0],[0,1,0,1,0,1,0,1,0,1,0,1],
                    [1,0,0,1,1,0,0,1,1,0,0,1],[0,1,1,0,0,1,1,0,0,1,1,0]],
                   dtype=np.float32)
    Hm = np.concatenate([A12.T, np.eye(12, dtype=np.float32)], axis=1)
    lx = np.log(np.array([2.0, 4.0, 8.0, 16.0, 32.0], dtype=np.float32))
    return COS, SIN, ka, Hm, lx


def build_nc(ntpb=SEQ // T):
    import ml_dtypes
    nc = bass.Bass("TRN2", target_bir_lowering=False)
    cx = Ctx(nc)

    xh = nc.dram_tensor("xT", [BPC, E, SEQ], F32, kind="ExternalInput").ap()
    oh = nc.dram_tensor("outT", [BPC, E, SEQ], F32, kind="ExternalOutput").ap()
    x00h = nc.dram_tensor("x00row", [1, E], F32, kind="ExternalInput").ap()
    ch = nc.dram_tensor("cT", [E, 4], F32, kind="ExternalInput").ap()
    wmodh = nc.dram_tensor("wmod", [E, 6 * E], F32, kind="ExternalInput").ap()
    welh = nc.dram_tensor("W_el", [E, 24], F32, kind="ExternalInput").ap()
    wleh = nc.dram_tensor("Wle_ext", [25, E], F32, kind="ExternalInput").ap()
    w1h = nc.dram_tensor("W1T", [E, HID], F32, kind="ExternalInput").ap()
    w2h = nc.dram_tensor("W2T", [HID, E], F32, kind="ExternalInput").ap()
    rwh = nc.dram_tensor("RWT", [E, NB], F32, kind="ExternalInput").ap()
    uth = nc.dram_tensor("UT", [E, NB], F32, kind="ExternalInput").ap()
    vh = nc.dram_tensor("V", [NB, E], F32, kind="ExternalInput").ap()
    lngh = nc.dram_tensor("ln_gc", [128, 8], F32, kind="ExternalInput").ap()
    lnbh = nc.dram_tensor("ln_bc", [128, 8], F32, kind="ExternalInput").ap()
    rbh = nc.dram_tensor("router_b", [1, NB], F32, kind="ExternalInput").ap()
    gamh = nc.dram_tensor("gamma_r", [1, E], F32, kind="ExternalInput").ap()
    cosh_ = nc.dram_tensor("COS", [E, E], F32, kind="ExternalInput").ap()
    sinh_ = nc.dram_tensor("SIN", [E, E], F32, kind="ExternalInput").ap()
    kah = nc.dram_tensor("ka_r", [1, E], F32, kind="ExternalInput").ap()
    ka2h = nc.dram_tensor("ka2_r", [1, E], F32, kind="ExternalInput").ap()
    lxh = nc.dram_tensor("lx_r", [1, 5], F32, kind="ExternalInput").ap()
    hth = nc.dram_tensor("HT", [24, 12], F32, kind="ExternalInput").ap()
    wext_h = nc.dram_tensor("wext_scr", [2 * E], BF16, kind="Internal")
    w2s_h = nc.dram_tensor("w2bf_scr", [HID, E], BF16, kind="Internal")

    Jnp = np.zeros((128, 128), dtype=np.float32)
    Jnp[np.arange(128), 127 - np.arange(128)] = 1.0
    I_bf_t = nc.inline_tensor(np.eye(128, dtype=np.float32).astype(ml_dtypes.bfloat16), name="i_bf")
    I_f_t = nc.inline_tensor(np.eye(128, dtype=np.float32), name="i_f")
    J_bf_t = nc.inline_tensor(Jnp.astype(ml_dtypes.bfloat16), name="j_bf")

    A = nc.alloc_sbuf_tensor

    w1t = A("w1t", [128, 8, HID], BF16)
    w2tw = A("w2tw", [128, 2, 4, E], BF16)
    m_sb = A("m_sb", [128, 8, E], BF16)
    wg = A("wg", [128, 8, NB], BF16)
    ut = A("ut", [128, 8, NB], BF16)
    vg = A("vg", [128, 4, E], BF16)
    mwel = A("mwel", [128, 8, 24], BF16)
    welb = A("welb", [128, 8, 24], BF16)
    modt = A("modt", [128, 48, 4], F32)
    wle_sb = A("wle_sb", [25, E], BF16)
    ht_sb = A("ht_sb", [24, 12], BF16)
    idb = A("idb", [128, 128], BF16)
    idf = A("idf", [128, 128], F32)
    jb = A("jb", [128, 128], BF16)
    onecb = A("onecb", [128, 1], BF16)
    onecf = A("onecf", [128, 1], F32)
    onerf = A("onerf", [1, 128], F32)
    cvals = A("cvals", [128, 8], F32)

    xt = A("xt", [128, 8, T], F32)
    xtb = A("xtb", [128, 8, T], BF16)
    sqb = A("sqb", [128, 8, T], BF16)
    rhs1 = A("rhs1", [128, 8, T], BF16)
    x2t = A("x2t", [128, 2, 8, T], F32)
    hffnt = A("hffnt", [128, 2, 8, T], BF16)
    gelut = A("gelut", [128, 2, 4, T], BF16)
    stg32 = A("stg32", [128, 2, NB], F32)
    x3st = A("x3st", [128, 4, T], F32)
    tokA = A("tokA", [128, NB], F32)
    hlzb = A("hlzb", [128, NB], BF16)
    zht = A("zht", [128, 2, 4, T], BF16)
    phib = A("phib", [128, 16], F32)
    colb = A("colb", [128, 16], F32)
    corrb = A("corrb", [25, T], BF16)
    onerb = A("onerb", [1, T], BF16)
    golA = A("golA", [24, T], F32)
    golB = A("golB", [24, T], F32)
    synb = A("synb", [12, T], BF16)

    rows_t = A("rows_t", [1, 4, T], F32)
    rowsb_t = A("rowsb_t", [1, 4, T], BF16)
    RW = RowAlloc(rows_t, 4, T)
    RB = RowAlloc(rowsb_t, 4, T)
    csr = A("csr", [1, NB], BF16)
    cnr = A("cnr", [1, NB], BF16)
    r2x = A("r2x", [2, T], BF16)
    rows2 = A("rows2", [2, BPC, E], BF16)
    mwr2 = A("mwr2", [2, BPC, 32], BF16)
    vecs = A("vecs", [128, 2, 8, 8], F32)
    vecb = A("vecb", [128, 8, 4], BF16)
    lngc = A("lngc", [128, 8], F32)
    lnbcb = A("lnbcb", [128, 8], BF16)
    dsc = A("dsc", [1, 16], F32)
    silc = A("silc", [128, 8, 4], BF16)
    silc32 = A("silc32", [128, 8, 4], F32)
    cstage = A("cstage", [128, 8, 4], F32)
    mcol = A("mcol", [128, 8], F32)
    frcols = A("frcols", [128, 8], F32)
    ficols = A("ficols", [128, 8], F32)
    wcol = A("wcol", [128, 8], F32)
    mwtmp = A("mwtmp", [128, 8, 24], BF16)

    xtf = xt.ap().rearrange("p a b -> p (a b)")
    x2f = x2t.ap().rearrange("p q a b -> p (q a b)")
    sca0 = [xtf[0:1, 0:E], xtf[0:1, E:2 * E]]
    SCAK = [f"xt:{c}" for c in range(8)]
    x00r = x2f[0:1, 0:E]
    h00r = x2f[0:1, E:2 * E]
    gamr = x2f[0:1, 0:E]
    rbr = x2f[0:1, E:E + NB]
    X2RK = ["x2t:0", "x2t:1"]

    pbig = nc.alloc_psum_tensor("pbig", [128, 8, T], F32)
    pout = nc.alloc_psum_tensor("pout", [128, 8, T], F32)

    def BKB(c):
        return ("pb", c // 2)

    def BKO(c):
        return ("po", c // 2)

    pstats = pout.ap()[:, 0, :]
    pmisc = pout.ap()[:, 1, :]
    prho = pout.ap()[:, 2, :]
    pmrho = pout.ap()[:, 3, :]
    ptokv = pout.ap()[:, 4:6, :].rearrange("p a b -> p (a b)")
    pztb = pout.ap()[:, 6, :].bitcast(BF16)
    pcol = pout.ap()[:, 7, :]

    def t_mm(out_ap, lhs, rhs, start, stop, reads, writes):
        cx.op("tensor", lambda e: e.matmul(out_ap, lhs, rhs, start=start, stop=stop),
              reads=reads, writes=writes, pe_skip=True)

    def t_tr(out_ap, in_ap, ident, reads, writes):
        cx.op("tensor", lambda e: e.transpose(out_ap, in_ap, ident),
              reads=reads, writes=writes, pe_skip=True)

    v = lambda fn, r, w: cx.op("vector", fn, reads=r, writes=w)
    s = lambda fn, r, w: cx.op("scalar", fn, reads=r, writes=w)

    def preload_tab(set_id):
        if PRELOADS:
            s(lambda e: e.add_instruction(mybir.InstLoadActFuncSet(
                name=nc.get_next_instruction_name(), act_func_set_id=set_id,
                ins=[], outs=[])), [], [])

    silf = silc.ap().rearrange("p c b -> p (c b)")
    cstf = cstage.ap().rearrange("p c b -> p (c b)")

    cx.dma("sync", "cst", idb[:], I_bf_t.ap()[:], writes=["idb"])
    cx.dma("sync", "cst", idf[:], I_f_t.ap()[:], writes=["idf"])
    cx.dma("sync", "cst", jb[:], J_bf_t.ap()[:], writes=["jb"])
    cx.dma("sync", "cst", lngc[:], lngh[:], writes=["lngc"])
    cx.dma("sync", "cst", x00r, x00h[0:1, :], writes=["x2t:0"])
    lxrow = A("lxrow", [1, 8], F32)
    cx.dma("sync", "cst", lxrow[0:1, 0:5], lxh[0:1, 0:5], writes=["lx"])
    cx.dma("sync", "cst", cstage.ap()[:], ch.rearrange("(c p) b -> p c b", p=128),
           writes=["cstage"])
    cx.dma("gpsimd", "cstg", ht_sb[:], hth[:], writes=["ht_sb"])
    cx.dma("gpsimd", "cstg", wle_sb[:], wleh[:], writes=["wle_sb"])
    cx.dma("gpsimd", "cstg", lnbcb[:], lnbh[:], writes=["lnbcb"])
    cx.dma("gpsimd", "cstg", welb.ap()[:],
           welh.rearrange("(c p) j -> p c j", p=128), writes=["welb"])

    for ci, val in enumerate((1e-5, -0.5, PI, -1.0, PI / 2, -2.0, 0.5, 1.0)):
        v(lambda e, ci=ci, val=val: e.memset(cvals[:, ci:ci + 1], val), [], ["cvals"])
    v(lambda e: e.memset(onecb[:], 1.0), [], ["onecb"])
    v(lambda e: e.memset(onecf[:], 1.0), [], ["onecf"])
    v(lambda e: e.memset(onerf[:], 1.0), [], ["onerf"])
    v(lambda e: e.memset(onerb[:], 1.0), [], ["onerb"])
    cx.dma("gpsimd", "cstg", corrb[24:25, :], onerb[:], reads=["onerb"], writes=["corr24"])
    cx.dma("gpsimd", "cstg", r2x[1:2, :], onerb[:], reads=["onerb"], writes=["r2x"])

    preload_tab(18)
    s(lambda e: e.activation(silf, cstf, AF.Silu), ["cstage"], ["silc"])
    silf32 = silc32.ap().rearrange("p c b -> p (c b)")
    s(lambda e: e.activation(silf32, cstf, AF.Silu), ["cstage"], ["silc32"])

    s(lambda e: e.activation(h00r, x00r, AF.Square, accum_out=dsc[0:1, 0:1]),
      ["x2t:0"], ["x2t:0", "dsc"])
    v(lambda e: e.reduce_sum(dsc[0:1, 1:2], x00r, axis=AXX), ["x2t:0"], ["dsc"])
    v(lambda e: e.tensor_scalar(dsc[0:1, 2:3], dsc[0:1, 1:2], 1.0 / E, None, ALU.mult),
      ["dsc"], ["dsc"])
    v(lambda e: e.tensor_tensor(dsc[0:1, 3:4], dsc[0:1, 2:3], dsc[0:1, 2:3], ALU.mult),
      ["dsc"], ["dsc"])
    v(lambda e: e.tensor_scalar(dsc[0:1, 4:5], dsc[0:1, 0:1], 1.0 / E, None, ALU.mult),
      ["dsc"], ["dsc"])
    v(lambda e: e.tensor_tensor(dsc[0:1, 5:6], dsc[0:1, 4:5], dsc[0:1, 3:4], ALU.subtract),
      ["dsc"], ["dsc"])
    preload_tab(6)
    s(lambda e: e.activation(dsc[0:1, 6:7], dsc[0:1, 5:6], AF.Ln, bias=cvals[0:1, 0:1]),
      ["dsc", "cvals"], ["dsc"])
    s(lambda e: e.activation(dsc[0:1, 7:8], dsc[0:1, 6:7], AF.Exp, scale=cvals[0:1, 1:2]),
      ["dsc", "cvals"], ["dsc"])
    v(lambda e: e.tensor_tensor(dsc[0:1, 8:9], dsc[0:1, 2:3], dsc[0:1, 7:8], ALU.mult),
      ["dsc"], ["dsc"])
    v(lambda e: e.tensor_scalar(dsc[0:1, 9:10], dsc[0:1, 8:9], -1.0, None, ALU.mult),
      ["dsc"], ["dsc"])

    w2f = w2tw.ap().bitcast(F32)
    xtfl32 = xt.ap().rearrange("p a b -> p (a b)")
    mstages = [
        (w2f[:, 0, :, :].rearrange("p a b -> p (a b)"), ["w2tw:0"], "gpsimd", "wmodA"),
        (w2f[:, 1, :, :].rearrange("p a b -> p (a b)"), ["w2tw:1"], "sync", "wmodB"),
        (xtfl32, [f"xt:{c}" for c in range(8)], "scalar", "wmodC"),
    ]
    msi = 0
    for piece in range(3):
        for dc in range(8):
            stg, skeys, engn, strm = mstages[msi % 3]
            msi += 1
            cx.dma(engn, strm, stg,
                   wmodh[128 * dc:128 * (dc + 1), 2048 * piece:2048 * (piece + 1)],
                   writes=skeys)
            for jc in range(16):
                t_mm(pbig[:, 0, 4 * jc:4 * (jc + 1)],
                     stg[:, 128 * jc:128 * (jc + 1)], silf32[:, 4 * dc:4 * (dc + 1)],
                     start=(dc == 0 and jc == 0), stop=(dc == 7 and jc == 15),
                     reads=skeys + ["silc32"], writes=[BKB(0)])
        for jc in range(16):
            j = 16 * piece + jc
            v(lambda e, j=j, jc=jc: e.tensor_copy(modt[:, j, :],
                                                  pbig[:, 0, 4 * jc:4 * (jc + 1)]),
              [BKB(0)], [f"modt:{j}"])

    MODK = [f"modt:{j}" for j in range(48)]

    for c in range(8):
        for vi, base in ((0, 0), (1, 8)):
            t_tr(pcol[0:1, 0:128], modt[:, base + c, 2:3], idf[:],
                 reads=[f"modt:{base + c}", "idf"], writes=[("po", 3)])
            v(lambda e, vi=vi, c=c: e.tensor_copy(sca0[vi][0:1, 128 * c:128 * (c + 1)],
                                                  pcol[0:1, 0:128]),
              [("po", 3)], SCAK)

    v(lambda e: e.tensor_scalar(h00r, x00r, dsc[0:1, 7:8], dsc[0:1, 9:10],
                                ALU.mult, ALU.add),
      ["x2t:0", "dsc"], ["x2t:0"])
    v(lambda e: e.tensor_scalar(sca0[1], sca0[1], 1.0, None, ALU.add),
      SCAK, SCAK)
    v(lambda e: e.tensor_tensor(h00r, h00r, sca0[1], ALU.mult),
      ["x2t:0"] + SCAK, ["x2t:0"])
    v(lambda e: e.tensor_tensor(h00r, h00r, sca0[0], ALU.add),
      ["x2t:0"] + SCAK, ["x2t:0"])

    cnt_r = RW.get("cnts", 8)
    for bi, n in enumerate((4, 16, 64, 256, 1024)):
        v(lambda e, n=n: e.reduce_sum(dsc[0:1, 10:11], h00r[0:1, 0:n], axis=AXX),
          ["x2t:0"], ["dsc"])
        v(lambda e, n=n: e.tensor_scalar(dsc[0:1, 11:12], dsc[0:1, 10:11], 1.0 / n,
                                         None, ALU.mult),
          ["dsc"], ["dsc"])
        v(lambda e, n=n: e.tensor_scalar(x00r[0:1, 0:n], h00r[0:1, 0:n],
                                         dsc[0:1, 11:12], None, ALU.is_gt),
          ["x2t:0", "dsc"], ["x2t:0"])
        v(lambda e, n=n, bi=bi: e.reduce_sum(cnt_r[0:1, bi:bi + 1], x00r[0:1, 0:n],
                                             axis=AXX),
          ["x2t:0"], ["cnts"])
    v(lambda e: e.tensor_scalar(cnt_r[0:1, 0:5], cnt_r[0:1, 0:5], 1.0, None, ALU.max),
      ["cnts"], ["cnts"])
    ly_r = RW.get("ly", 8)
    s(lambda e: e.activation(ly_r[0:1, 0:5], cnt_r[0:1, 0:5], AF.Ln), ["cnts"], ["ly"])
    lx_r = lxrow
    wk = RW.get("dwork")
    v(lambda e: e.tensor_tensor(wk[0:1, 0:5], lx_r[0:1, 0:5], ly_r[0:1, 0:5], ALU.mult),
      ["lx", "ly"], ["dw"])
    v(lambda e: e.tensor_tensor(wk[0:1, 5:10], lx_r[0:1, 0:5], lx_r[0:1, 0:5], ALU.mult),
      ["lx", "dw"], ["dw"])
    v(lambda e: e.reduce_sum(wk[0:1, 10:11], wk[0:1, 0:5], axis=AXX), ["dw"], ["dw"])
    v(lambda e: e.reduce_sum(wk[0:1, 11:12], lx_r[0:1, 0:5], axis=AXX), ["lx", "dw"], ["dw"])
    v(lambda e: e.reduce_sum(wk[0:1, 12:13], ly_r[0:1, 0:5], axis=AXX), ["ly", "dw"], ["dw"])
    v(lambda e: e.reduce_sum(wk[0:1, 13:14], wk[0:1, 5:10], axis=AXX), ["dw"], ["dw"])
    v(lambda e: e.tensor_tensor(wk[0:1, 14:15], wk[0:1, 11:12], wk[0:1, 12:13], ALU.mult),
      ["dw"], ["dw"])
    v(lambda e: e.tensor_scalar(wk[0:1, 15:16], wk[0:1, 10:11], 5.0, None, ALU.mult),
      ["dw"], ["dw"])
    v(lambda e: e.tensor_tensor(wk[0:1, 16:17], wk[0:1, 15:16], wk[0:1, 14:15], ALU.subtract),
      ["dw"], ["dw"])
    v(lambda e: e.tensor_tensor(wk[0:1, 17:18], wk[0:1, 11:12], wk[0:1, 11:12], ALU.mult),
      ["dw"], ["dw"])
    v(lambda e: e.tensor_scalar(wk[0:1, 18:19], wk[0:1, 13:14], 5.0, None, ALU.mult),
      ["dw"], ["dw"])
    v(lambda e: e.tensor_tensor(wk[0:1, 19:20], wk[0:1, 18:19], wk[0:1, 17:18], ALU.subtract),
      ["dw"], ["dw"])
    v(lambda e: e.reciprocal(wk[0:1, 20:21], wk[0:1, 19:20]), ["dw"], ["dw"])
    v(lambda e: e.tensor_tensor(wk[0:1, 21:22], wk[0:1, 16:17], wk[0:1, 20:21], ALU.mult),
      ["dw"], ["dw"])
    v(lambda e: e.tensor_scalar(wk[0:1, 22:23], wk[0:1, 21:22], -1.0, 1.0, ALU.mult, ALU.max),
      ["dw"], ["dw"])
    v(lambda e: e.tensor_scalar(dsc[0:1, 0:1], wk[0:1, 22:23], 2.0, None, ALU.min),
      ["dw"], ["dsc"])
    v(lambda e: e.tensor_scalar(dsc[0:1, 1:2], dsc[0:1, 0:1], 0.2, 1.0, ALU.mult, ALU.add),
      ["dsc"], ["dsc"])
    v(lambda e: e.tensor_tensor(dsc[0:1, 2:3], dsc[0:1, 1:2], dsc[0:1, 0:1], ALU.mult),
      ["dsc"], ["dsc"])
    v(lambda e: e.tensor_scalar(dsc[0:1, 3:4], dsc[0:1, 0:1], 0.5, 1.0, ALU.mult, ALU.add),
      ["dsc"], ["dsc"])
    v(lambda e: e.tensor_scalar(dsc[0:1, 4:5], dsc[0:1, 0:1], 0.1, None, ALU.mult),
      ["dsc"], ["dsc"])
    RW.drop("cnts", "ly", "dwork")

    ka_row = sca0[0]
    ka2_row = sca0[1]
    cx.dma("sync", "cst2", ka_row, kah[0:1, :], reads=SCAK, writes=SCAK)
    cx.dma("sync", "cst2", ka2_row, ka2h[0:1, :], reads=SCAK, writes=SCAK)
    arg_row = x00r
    amp_row = h00r
    v(lambda e: e.tensor_scalar(arg_row, ka_row, dsc[0:1, 3:4], dsc[0:1, 2:3],
                                ALU.mult, ALU.add),
      SCAK + ["dsc", "x2t:0"], ["x2t:0"])
    preload_tab(9)
    s(lambda e: e.activation(amp_row, arg_row, AF.Sin, bias=cvals[0:1, 2:3], scale=cvals[0:1, 3:4]),
      ["x2t:0", "cvals"], ["x2t:0"])
    v(lambda e: e.tensor_scalar(arg_row, ka2_row, dsc[0:1, 4:5], dsc[0:1, 2:3],
                                ALU.mult, ALU.add),
      SCAK + ["dsc", "x2t:0"], ["x2t:0"])
    v(lambda e: e.tensor_tensor(arg_row, arg_row, ka_row, ALU.subtract),
      SCAK + ["x2t:0"], ["x2t:0"])
    s(lambda e: e.activation(sca0[0], arg_row, AF.Sin, bias=cvals[0:1, 4:5], scale=cvals[0:1, 3:4]),
      ["x2t:0", "cvals"] + SCAK, SCAK)
    s(lambda e: e.activation(sca0[1], arg_row, AF.Sin),
      ["x2t:0"] + SCAK, SCAK)
    v(lambda e: e.tensor_tensor(sca0[0], sca0[0], amp_row, ALU.mult),
      SCAK + ["x2t:0"], SCAK)
    v(lambda e: e.tensor_tensor(sca0[1], sca0[1], amp_row, ALU.mult),
      SCAK + ["x2t:0"], SCAK)
    v(lambda e: e.tensor_scalar(sca0[1], sca0[1], -1.0, None, ALU.mult),
      SCAK, SCAK)
    for j in range(8):
        t_tr(pcol[:, 0:1], sca0[0][0:1, 128 * j:128 * (j + 1)], idf[0:1, 0:1],
             reads=SCAK + ["idf"], writes=[("po", 3)])
        v(lambda e, j=j: e.tensor_copy(frcols[:, j:j + 1], pcol[:, 0:1]),
          [("po", 3)], ["frcols"])
        t_tr(pcol[:, 0:1], sca0[1][0:1, 128 * j:128 * (j + 1)], idf[0:1, 0:1],
             reads=SCAK + ["idf"], writes=[("po", 3)])
        v(lambda e, j=j: e.tensor_copy(ficols[:, j:j + 1], pcol[:, 0:1]),
          [("po", 3)], ["ficols"])

    first_w = True
    xtfl = xt.ap().rearrange("p a b -> p (a b)")
    wstg = [(stg32.ap()[:, 0, :], "stg32:0"), (stg32.ap()[:, 1, :], "stg32:1"),
            (xtfl[:, 0:512], "xt:0"), (xtfl[:, 512:1024], "xt:2"),
            (xtfl[:, 1024:1536], "xt:4"), (xtfl[:, 1536:2048], "xt:6")]
    wsi = 0
    for kc in range(8):
        for hf in range(2):
            stgc, skey = wstg[wsi % 6]
            cx.dma("scalar", f"cosin{wsi % 6}", stgc,
                   cosh_[128 * kc:128 * (kc + 1), 512 * hf:512 * (hf + 1)], writes=[skey])
            for ms in range(4):
                mc = 4 * hf + ms
                t_mm(pbig[:, 1, mc:mc + 1], stgc[:, 128 * ms:128 * (ms + 1)],
                     frcols[:, kc:kc + 1],
                     start=first_w, stop=False,
                     reads=[skey, "frcols"], writes=[BKB(0)])
                first_w = False
            wsi += 1
            stgc, skey = wstg[wsi % 6]
            cx.dma("scalar", f"cosin{wsi % 6}", stgc,
                   sinh_[128 * kc:128 * (kc + 1), 512 * hf:512 * (hf + 1)], writes=[skey])
            for ms in range(4):
                mc = 4 * hf + ms
                t_mm(pbig[:, 1, mc:mc + 1], stgc[:, 128 * ms:128 * (ms + 1)],
                     ficols[:, kc:kc + 1],
                     start=False, stop=(kc == 7 and hf == 1 and ms == 3),
                     reads=[skey, "ficols"], writes=[BKB(0)])
            wsi += 1
    v(lambda e: e.tensor_scalar(wcol[:], pbig[:, 1, 0:8], 1.0 / E, None, ALU.mult),
      [BKB(0)], ["wcol"])
    wcolb = A("wcolb", [128, 8], BF16)
    v(lambda e: e.tensor_copy(wcolb[:], wcol[:]), ["wcol"], ["wcolb"])

    for c in range(8):
        cx.dma("scalar", "wex", AP(wext_h, 128 * c, [[1, 128]]), wcolb[:, c:c + 1],
               reads=["wcolb"], writes=[f"wext:{2 * c}"])
        cx.dma("scalar", "wex", AP(wext_h, E + 128 * c, [[1, 128]]), wcolb[:, c:c + 1],
               reads=["wcolb"], writes=[f"wext:{2 * c + 1}"])
    WEXTK = [f"wext:{i}" for i in range(16)]
    xtbfl = xtb.ap().rearrange("p a b -> p (a b)")
    sqbfl = sqb.ap().rearrange("p a b -> p (a b)")
    MREV_ENG = ["sync", "scalar", "gpsimd", "scalar"]

    def mslot(i):
        s = i % 4
        if s < 2:
            return xtbfl[:, 1024 * s:1024 * (s + 1)], f"xtbh{s}"
        return sqbfl[:, 1024 * (s - 2):1024 * (s - 1)], f"sqbh{s - 2}"

    for g in range(8):
        gp = g % 2
        stgm, mkey = mslot(g)
        ch0 = 2 + 4 * gp
        ch1 = 4 - 4 * gp
        cx.dma(MREV_ENG[g % 4], f"mrev{g % 4}", stgm,
               AP(wext_h, 897 - 128 * g, [[1, 128], [1, E]]),
               reads=WEXTK, writes=[mkey])
        t_mm(pbig[:, ch0:ch0 + 2, :].rearrange("p a b -> p (a b)"), jb[:], stgm[:, 0:512],
             start=True, stop=True, reads=[mkey, "jb"], writes=[BKB(ch0)])
        t_mm(pbig[:, ch1:ch1 + 2, :].rearrange("p a b -> p (a b)"), jb[:], stgm[:, 512:1024],
             start=True, stop=True, reads=[mkey, "jb"], writes=[BKB(ch1)])
        v(lambda e, g=g, ch0=ch0: e.tensor_copy(
                m_sb[:, g, 0:512], pbig[:, ch0:ch0 + 2, :].rearrange("p a b -> p (a b)")),
          [BKB(ch0)], [f"m:{g}"])
        v(lambda e, g=g, ch1=ch1: e.tensor_copy(
                m_sb[:, g, 512:1024], pbig[:, ch1:ch1 + 2, :].rearrange("p a b -> p (a b)")),
          [BKB(ch1)], [f"m:{g}"])
    first = True
    for g in range(8):
        stgm, mkey = mslot(g)
        cx.dma(MREV_ENG[g % 4], f"mrev{g % 4}", stgm,
               AP(wext_h, 1 + 128 * g, [[1, 128], [1, E]]),
               reads=WEXTK, writes=[mkey])
        for q in range(8):
            t_mm(pbig[:, 6 + q // 4, 24 * (q % 4):24 * (q % 4) + 24],
                 stgm[:, 128 * q:128 * (q + 1)], welb[:, g, :],
                 start=first, stop=(g == 7 and q == 7),
                 reads=[mkey, "welb"], writes=[BKB(6)])
            first = False
    for q in range(8):
        v(lambda e, q=q: e.tensor_copy(mwtmp[:, q, :],
                                       pbig[:, 6 + q // 4, 24 * (q % 4):24 * (q % 4) + 24]),
          [BKB(6)], ["mwtmp"])
    for c in range(8):
        t_mm(pbig[:, 0, 0:24], jb[:], mwtmp[:, 7 - c, :], start=True, stop=True,
             reads=["mwtmp", "jb"], writes=[BKB(0)])
        v(lambda e, c=c: e.tensor_copy(mwel[:, c, :], pbig[:, 0, 0:24]),
          [BKB(0)], ["mwel"])
    MK = [f"m:{g}" for g in range(8)]

    for b in range(BPC):
        for c in range(8):
            v(lambda e, b=b, c=c: e.tensor_copy(vecb[:, c, 2 * b:2 * b + 1],
                                                modt[:, 0 + c, b:b + 1]),
              [f"modt:{c}"], ["vecb"])
            v(lambda e, b=b, c=c: e.tensor_scalar(vecb[:, c, 2 * b + 1:2 * b + 2],
                                                  modt[:, 8 + c, b:b + 1], 1.0, None, ALU.add),
              [f"modt:{8 + c}"], ["vecb"])
    for b in range(BPC):
        for vi, dst in ((0, 1), (1, 0)):
            for half in range(2):
                for mc in range(8):
                    t_mm(pbig[0:1, 2 * half:2 * half + 2, :].rearrange("p a b -> p (a b)"),
                         vecb[:, mc, 2 * b + vi:2 * b + vi + 1],
                         m_sb[:, mc, 512 * half:512 * (half + 1)],
                         start=(mc == 0), stop=(mc == 7),
                         reads=[f"m:{mc}", "vecb"], writes=[BKB(2 * half)])
            if dst == 0:
                for half in range(2):
                    v(lambda e, b=b, half=half: e.tensor_copy(
                            rows2[0:1, b, 512 * half:512 * (half + 1)],
                            pbig[0:1, 2 * half:2 * half + 2, :].rearrange("p a b -> p (a b)")),
                      [BKB(2 * half)], [f"rows2:{b}"])
            else:
                for half in range(2):
                    v(lambda e, half=half: e.tensor_copy(
                            tokA[0:1, 0:512],
                            pbig[0:1, 2 * half:2 * half + 2, :].rearrange("p a b -> p (a b)")),
                      [BKB(2 * half)], ["tokA"])
                    cx.dma("gpsimd", "cstg", rows2[1:2, b, 512 * half:512 * (half + 1)],
                           tokA[0:1, 0:512], reads=["tokA"], writes=[f"rows2:{b}"])
        for vi, dst in ((0, 1), (1, 0)):
            for mc in range(8):
                t_mm(pbig[0:1, 4, 0:24], vecb[:, mc, 2 * b + vi:2 * b + vi + 1],
                     mwel[:, mc, :],
                     start=(mc == 0), stop=(mc == 7),
                     reads=["mwel", "vecb"], writes=[BKB(4)])
            if dst == 0:
                v(lambda e, b=b: e.tensor_copy(mwr2[0:1, b, 0:24], pbig[0:1, 4, 0:24]),
                  [BKB(4)], [f"mwr2:{b}"])
            else:
                v(lambda e: e.tensor_copy(tokA[0:1, 0:24], pbig[0:1, 4, 0:24]),
                  [BKB(4)], ["tokA"])
                cx.dma("gpsimd", "cstg", mwr2[1:2, b, 0:24], tokA[0:1, 0:24],
                       reads=["tokA"], writes=[f"mwr2:{b}"])

    cx.dma("sync", "cst3", rbr, rbh[0:1, :], reads=["x2t:0"], writes=["x2t:0"])

    for dc in range(8):
        par = dc % 2
        stgb = stg32.ap()[:, par, 0:256].bitcast(BF16)
        skey = f"stg32:{par}"
        cx.dma("gpsimd", f"rw{par}", stgb, rwh[128 * dc:128 * (dc + 1), :], writes=[skey])
        v(lambda e, dc=dc, stgb=stgb: e.tensor_scalar(
                wg[:, dc, :], stgb, lngc[:, dc:dc + 1], None, ALU.mult),
          [skey, "lngc"], [f"wg:{dc}"])
    WGK = [f"wg:{dc}" for dc in range(8)]
    for half in range(2):
        for dc in range(8):
            t_mm(pbig[0:1, 2 + half, :], onecb[:], wg[:, dc, 256 * half:256 * (half + 1)],
                 start=(dc == 0 and half == 0), stop=(dc == 7 and half == 1),
                 reads=[f"wg:{dc}", "onecb"], writes=[BKB(2)])
    for half in range(2):
        v(lambda e, half=half: e.tensor_copy(csr[0:1, 256 * half:256 * (half + 1)],
                                             pbig[0:1, 2 + half, :]),
          [BKB(2)], ["csr"])
    for half in range(2):
        for dc in range(8):
            par = dc % 2
            stgb = stg32.ap()[:, par, 0:128].bitcast(BF16)
            skey = f"stg32:{par}"
            cx.dma("gpsimd", f"rw2{par}", stgb,
                   rwh[128 * dc:128 * (dc + 1), 256 * half:256 * (half + 1)],
                   writes=[skey])
            t_mm(pbig[0:1, 4 + half, :], lnbcb[:, dc:dc + 1], stgb,
                 start=(dc == 0 and half == 0), stop=(dc == 7 and half == 1),
                 reads=[skey, "lnbcb"], writes=[BKB(4)])
    for half in range(2):
        v(lambda e, half=half: e.tensor_tensor(
                tokA[0:1, 256 * half:256 * (half + 1)], pbig[0:1, 4 + half, :],
                rbr[0:1, 256 * half:256 * (half + 1)], ALU.add),
          [BKB(4), "x2t:0"], ["tokA"])
    v(lambda e: e.tensor_copy(cnr[0:1, :], tokA[0:1, 0:NB]), ["tokA"], ["cnr"])

    for dc in range(8):
        cx.dma("gpsimd", "ut0", ut[:, dc, :], uth[128 * dc:128 * (dc + 1), :],
               writes=[f"ut:{dc}"])
    usq = gelut.ap()[:, 0, 0:2, :].rearrange("p a b -> p (a b)")
    USQK = ["gelut:0:0", "gelut:0:1"]
    first = True
    for dc in range(8):
        s(lambda e, dc=dc: e.activation(usq, ut[:, dc, :], AF.Square),
          [f"ut:{dc}"], USQK)
        t_mm(pbig[0:1, 0, :], onecb[:], usq[:, 0:256], start=first, stop=False,
             reads=USQK + ["onecb"], writes=[BKB(0)])
        t_mm(pbig[0:1, 1, :], onecb[:], usq[:, 256:512], start=False, stop=(dc == 7),
             reads=USQK + ["onecb"], writes=[BKB(0)])
        first = False
    un_r = RW.get("unrm", T)
    first = True
    preload_tab(6)
    for half in range(2):
        s(lambda e, half=half: e.activation(un_r, pbig[0:1, half, :], AF.Ln),
          [BKB(0)], ["unr"])
        s(lambda e: e.activation(un_r, un_r, AF.Exp, scale=cvals[0:1, 6:7]), ["unr", "cvals"], ["unr"])
        v(lambda e: e.tensor_scalar(un_r, un_r, 1e-6, None, ALU.max), ["unr"], ["unr"])
        v(lambda e: e.reciprocal(un_r, un_r), ["unr"], ["unr"])
        t_mm(pbig[:, 2 + half, :], onerf[:], un_r, start=first, stop=(half == 1),
             reads=["unr", "onerf"], writes=[BKB(2)])
        first = False
    for dc in range(8):
        v(lambda e, dc=dc: e.scalar_tensor_tensor(
                ut[:, dc, :], ut[:, dc, :], 1.0,
                pbig[:, 2:4, :].rearrange("p a b -> p (a b)"), ALU.mult, ALU.mult),
          [f"ut:{dc}", BKB(2)], [f"ut:{dc}"])
    RW.drop("unrm")
    UTK = [f"ut:{dc}" for dc in range(8)]

    cx.dma("sync", "cst4", gamr, gamh[0:1, :], reads=["x2t:0"], writes=["x2t:0"])
    for half in range(2):
        t_mm(pbig[:, 4 + 2 * half:6 + 2 * half, :].rearrange("p a b -> p (a b)"),
             onerf[:], gamr[0:1, 512 * half:512 * (half + 1)],
             start=True, stop=True,
             reads=["x2t:0", "onerf"], writes=[BKB(4) if half == 0 else BKB(6)])
    vst = gelut.ap()[:, 1, :, :].rearrange("p a b -> p (a b)")
    VSTK = [f"gelut:1:{j}" for j in range(4)]
    for vt in range(4):
        cx.dma("gpsimd", "v0", vst, vh[128 * vt:128 * (vt + 1), :], writes=VSTK)
        s(lambda e: e.activation(usq, vst[:, 0:512], AF.Square, accum_out=mcol[:, 2:3]),
          VSTK, USQK + ["mcol"])
        s(lambda e: e.activation(usq, vst[:, 512:1024], AF.Square, accum_out=mcol[:, 3:4]),
          VSTK, USQK + ["mcol"])
        v(lambda e: e.tensor_tensor(mcol[:, 4:5], mcol[:, 2:3], mcol[:, 3:4], ALU.add),
          ["mcol"], ["mcol"])
        s(lambda e: e.activation(mcol[:, 5:6], mcol[:, 4:5], AF.Ln), ["mcol"], ["mcol"])
        s(lambda e: e.activation(mcol[:, 5:6], mcol[:, 5:6], AF.Exp, scale=cvals[:, 6:7]),
          ["mcol", "cvals"], ["mcol"])
        v(lambda e: e.tensor_scalar(mcol[:, 5:6], mcol[:, 5:6], 1e-6, None, ALU.max),
          ["mcol"], ["mcol"])
        v(lambda e: e.reciprocal(mcol[:, 6:7], mcol[:, 5:6]), ["mcol"], ["mcol"])
        v(lambda e, vt=vt: e.scalar_tensor_tensor(
                vg[:, vt, :], vst, mcol[:, 6:7],
                pbig[:, 4:8, :].rearrange("p a b -> p (a b)"), ALU.mult, ALU.mult),
          VSTK + ["mcol", BKB(4), BKB(6)], [f"vg:{vt}"])
    VGK = [f"vg:{vt}" for vt in range(4)]

    def emit_weight_tail():
        for dc in range(8):
            cx.dma("gpsimd", "w1", w1t[:, dc, :], w1h[128 * dc:128 * (dc + 1), :],
                   reads=(["mwel"] if dc == 0 else []), writes=[f"w1t:{dc}"])
        for hc in range(32):
            cx.dma("gpsimd", "w2cv", AP(w2s_h, 128 * E * hc, [[E, 128], [1, E]]),
                   w2h[128 * hc:128 * (hc + 1), :], writes=[f"w2scr:{hc}"])

    VL = {"a": 0, "cb1": 1, "ncb1": 2, "ga": 3, "gb": 4, "t0": 5, "t1": 6}

    aSR1 = pbig.ap()[0:1, 6, :]
    aSR2 = pbig.ap()[32:33, 6, :]
    aPRHO1 = pbig.ap()[:, 7, :]
    aLAT = pbig.ap()[0:24, 6, :]
    aSYN = pbig.ap()[32:44, 6, :]
    aWS = pbig.ap()[64:65, 6, :]
    aFIX = pbig.ap()[0:12, 7, :]
    aPRHO2 = pbig.ap()[:, 6, :]
    aPMRHO = pbig.ap()[:, 7, :]
    aTOK = pbig.ap()[:, 4:6, :].rearrange("p a b -> p (a b)")
    aZT = pbig.ap()[:, 6, :].bitcast(BF16)
    aZT2 = pbig.ap()[:, 4, :].bitcast(BF16)
    aCOL = pbig.ap()[:, 7, :]
    BK2, BK3 = ("pb", 2), ("pb", 3)

    def batch_prep(b, par):
        kb = f"vecs:{par}"
        v(lambda e: e.tensor_scalar(vecs[:, par, VL["a"], :], modt[:, 8:16, b], 1.0, None, ALU.add),
          MODK, [kb])
        v(lambda e: e.tensor_scalar(vecs[:, par, VL["cb1"], :], modt[:, 32:40, b], 1.0, None, ALU.add),
          MODK, [kb])
        v(lambda e: e.tensor_scalar(vecs[:, par, VL["ncb1"], :], vecs[:, par, VL["cb1"], :], -1.0,
                                    None, ALU.mult),
          [kb], [kb])
        for nm, base in (("ga", 16), ("gb", 40)):
            s(lambda e, base=base: e.activation(vecs[:, par, VL["t0"], :], modt[:, base:base + 8, b],
                                                AF.Abs),
              MODK, [kb])
            v(lambda e: e.tensor_scalar(vecs[:, par, VL["t0"], :], vecs[:, par, VL["t0"], :], 0.25,
                                        None, ALU.max),
              [kb], [kb])
            v(lambda e, base=base: e.tensor_scalar(vecs[:, par, VL["t1"], :],
                                                   modt[:, base:base + 8, b], 0.0, None, ALU.is_ge),
              MODK, [kb])
            v(lambda e: e.tensor_scalar(vecs[:, par, VL["t1"], :], vecs[:, par, VL["t1"], :], 2.0, -1.0,
                                        ALU.mult, ALU.add),
              [kb], [kb])
            v(lambda e, nm=nm: e.tensor_tensor(vecs[:, par, VL[nm], :], vecs[:, par, VL["t0"], :],
                                               vecs[:, par, VL["t1"], :], ALU.mult),
              [kb], [kb])

    n_tiles = BPC * ntpb

    def emit_xin(ti):
        if ti >= n_tiles:
            return
        b = ti // ntpb
        t0 = (ti % ntpb) * T
        par = ti % 2
        for c in range(8):
            cx.dma("sync", "xin", xt[:, c, :], xh[b, 128 * c:128 * (c + 1), t0:t0 + T],
                   writes=[f"xt:{c}"])
        xtbw = ["xtb"] + (["xtbh0", "xtbh1"] if ti == 0 else [])
        if ti == 1:
            v(lambda e: e.tensor_copy(xtb.ap().rearrange("p a b -> p (a b)"),
                                      xt.ap().rearrange("p a b -> p (a b)")),
              [f"xt:{c}" for c in range(8)], xtbw)
            return
        for c in range(8):
            cx.dma("gpsimd", f"xinb{par}", xtb[:, c, :],
                   xh[b, 128 * c:128 * (c + 1), t0:t0 + T], writes=xtbw)

    def phaseA(ti):
        b = ti // ntpb
        t0 = (ti % ntpb) * T
        par = ti % 2
        KB = [f"vecs:{b % 2}"]
        X2W = f"x2t:{par}"
        if ti % ntpb == 0:
            batch_prep(b, b % 2)
        yield
        XK = [f"xt:{c}" for c in range(8)]
        yield

        sqbw = ["sqb"] + (["sqbh0", "sqbh1"] if ti == 0 else [])
        s(lambda e: e.activation(sqb.ap().rearrange("p a b -> p (a b)"),
                                 xtb.ap().rearrange("p a b -> p (a b)"), AF.Square),
          ["xtb"], sqbw)
        for c in range(8):
            t_mm(aSR1, onecb[:], xtb[:, c, :], start=(c == 0), stop=(c == 7),
                 reads=["xtb", "onecb"], writes=[BK3])
        for c in range(8):
            t_mm(aSR2, onecb[:], sqb[:, c, :], start=(c == 0), stop=(c == 7),
                 reads=["sqb", "onecb"], writes=[BK3])
        yield
        r_mu = RW.get("r_mu")
        r_rho = RW.get("r_rho")
        tv = RW.get("r_tv")
        v(lambda e: e.tensor_scalar(r_mu, aSR1, 1.0 / E, None, ALU.mult), [BK3], ["r_mu"])
        v(lambda e: e.tensor_scalar(tv, aSR2, 1.0 / E, None, ALU.mult), [BK3], ["r_tv"])
        v(lambda e: e.tensor_tensor(r_rho, r_mu, r_mu, ALU.mult), ["r_mu"], ["r_rho"])
        v(lambda e: e.tensor_tensor(tv, tv, r_rho, ALU.subtract), ["r_tv", "r_rho"], ["r_tv"])
        preload_tab(6)
        s(lambda e: e.activation(tv, tv, AF.Ln, bias=cvals[0:1, 0:1]), ["r_tv", "cvals"], ["r_tv"])
        s(lambda e: e.activation(r_rho, tv, AF.Exp, scale=cvals[0:1, 1:2]), ["r_tv", "cvals"], ["r_rho"])
        preload_tab(10)
        v(lambda e: e.tensor_tensor(tv, r_mu, r_rho, ALU.mult), ["r_mu", "r_rho"], ["r_tv"])
        v(lambda e: e.tensor_scalar(tv, tv, -1.0, None, ALU.mult), ["r_tv"], ["r_tv"])
        v(lambda e: e.tensor_copy(r2x[0:1, :], tv), ["r_tv"], ["r2x"])
        rhob = RB.get("rhob")
        v(lambda e: e.tensor_copy(rhob, r_rho), ["r_rho"], ["rhob"])
        t_mm(aPRHO1, onerb[0:1, 0:128], rhob, start=True, stop=True,
             reads=["rhob", "onerb"], writes=[BK3])
        for c in range(8):
            v(lambda e, c=c: e.scalar_tensor_tensor(rhs1[:, c, :], xtb[:, c, :],
                                                    vecs[:, b % 2, VL["a"], c:c + 1], aPRHO1,
                                                    ALU.mult, ALU.mult),
              ["xtb", BK3] + KB, [f"rhs1:{c}"])
        yield

        for mc in range(8):
            t_mm(aLAT, mwel[:, mc, :], rhs1[:, mc, :],
                 start=(mc == 0), stop=False, reads=["mwel", f"rhs1:{mc}"], writes=[BK3])
        t_mm(aLAT, mwr2[:, b, 0:24], r2x[:], start=False, stop=True,
             reads=[f"mwr2:{b}", "r2x"], writes=[BK3])
        v(lambda e: e.tensor_scalar(corrb[0:24, :], aLAT, 0.0, None, ALU.is_gt),
          [BK3], ["corrb"])
        t_mm(aSYN, ht_sb[:], corrb[0:24, :], start=True, stop=True,
             reads=["corrb", "ht_sb"], writes=[BK3])
        v(lambda e: e.tensor_scalar(synb[:], aSYN, 0.5, None, ALU.is_gt), [BK3], ["synb"])
        t_mm(aWS, onecb[0:12, :], synb[:], start=True, stop=True,
             reads=["synb", "onecb"], writes=[BK3])
        fx = RW.get("r_fx")
        v(lambda e: e.tensor_scalar(fx, aWS, 3.5, None, ALU.is_le), [BK3], ["r_fx"])
        v(lambda e: e.scalar_tensor_tensor(fx, aWS, 0.5, fx, ALU.is_ge, ALU.mult),
          [BK3, "r_fx"], ["r_fx"])
        t_mm(aFIX, onerf[0:1, 0:12], fx, start=True, stop=True,
             reads=["r_fx", "onerf"], writes=[BK3])
        v(lambda e: e.tensor_tensor(golA[0:12, :], corrb[0:12, :], synb[:], ALU.mult),
          ["corrb", "synb"], ["golA"])
        v(lambda e: e.scalar_tensor_tensor(golA[0:12, :], golA[0:12, :], -2.0, synb[:],
                                           ALU.mult, ALU.add),
          ["golA", "synb"], ["golA"])
        v(lambda e: e.tensor_tensor(golA[0:12, :], golA[0:12, :], aFIX, ALU.mult),
          ["golA", BK3], ["golA"])
        v(lambda e: e.tensor_tensor(corrb[0:12, :], corrb[0:12, :], golA[0:12, :], ALU.add),
          ["golA", "corrb"], ["corrb"])
        yield

        for half in range(2):
            for cc in range(4):
                nc_ = 4 * half + cc
                for mc in range(8):
                    t_mm(pbig[:, 4 + cc, :], m_sb[:, mc, 128 * nc_:128 * (nc_ + 1)],
                         rhs1[:, mc, :],
                         start=(mc == 0 and cc % 2 == 0), stop=False,
                         reads=[f"m:{mc}", f"rhs1:{mc}"], writes=[pbank("x", 4 + cc) if False else ("pb", (4 + cc) // 2)])
                t_mm(pbig[:, 4 + cc, :], rows2[:, b, 128 * nc_:128 * (nc_ + 1)], r2x[:],
                     start=False, stop=False,
                     reads=[f"rows2:{b}", "r2x"], writes=[("pb", (4 + cc) // 2)])
            for cc in range(4):
                nc_ = 4 * half + cc
                t_mm(pbig[:, 4 + cc, :], wle_sb[:, 128 * nc_:128 * (nc_ + 1)], corrb[:],
                     start=False, stop=(cc % 2 == 1),
                     reads=["corrb", "corr24", "wle_sb"], writes=[("pb", (4 + cc) // 2)])
            for cc in range(4):
                nc_ = 4 * half + cc
                v(lambda e, nc_=nc_, cc=cc: e.scalar_tensor_tensor(
                        x2t[:, par, nc_, :], pbig[:, 4 + cc, :],
                        vecs[:, b % 2, VL["ga"], nc_:nc_ + 1], xt[:, nc_, :],
                        ALU.mult, ALU.add),
                  [("pb", (4 + cc) // 2)] + KB + XK, [X2W])
            yield

        s(lambda e: e.activation(sqb.ap().rearrange("p a b -> p (a b)"),
                                 x2t.ap()[:, par, :, :].rearrange("p a b -> p (a b)"),
                                 AF.Square),
          [X2W], ["sqb"])
        v(lambda e: e.tensor_copy(xtb.ap().rearrange("p a b -> p (a b)"),
                                  x2t.ap()[:, par, :, :].rearrange("p a b -> p (a b)")),
          [X2W] + [f"rhs1:{c}" for c in range(8)], ["xtb"])
        for c in range(8):
            t_mm(aSR1, onecb[:], xtb[:, c, :], start=(c == 0), stop=(c == 7),
                 reads=["xtb", "onecb"], writes=[BK3])
        for c in range(8):
            t_mm(aSR2, onecb[:], sqb[:, c, :], start=(c == 0), stop=(c == 7),
                 reads=["sqb", "onecb"], writes=[BK3])
        yield
        r_mu = RW.get("r_mu")
        r_rho = RW.get("r_rho")
        v(lambda e: e.tensor_scalar(r_mu, aSR1, 1.0 / E, None, ALU.mult), [BK3], ["r_mu"])
        v(lambda e: e.tensor_scalar(tv, aSR2, 1.0 / E, None, ALU.mult), [BK3], ["r_tv"])
        v(lambda e: e.tensor_tensor(r_rho, r_mu, r_mu, ALU.mult), ["r_mu"], ["r_rho"])
        v(lambda e: e.tensor_tensor(tv, tv, r_rho, ALU.subtract), ["r_tv", "r_rho"], ["r_tv"])
        preload_tab(6)
        s(lambda e: e.activation(tv, tv, AF.Ln, bias=cvals[0:1, 0:1]), ["r_tv", "cvals"], ["r_tv"])
        s(lambda e: e.activation(r_rho, tv, AF.Exp, scale=cvals[0:1, 1:2]), ["r_tv", "cvals"], ["r_rho"])
        preload_tab(10)
        v(lambda e: e.tensor_tensor(r_mu, r_mu, r_rho, ALU.mult), ["r_mu", "r_rho"], ["r_mu"])
        v(lambda e: e.tensor_copy(rhob, r_rho), ["r_rho"], ["rhob"])
        mrhob = RB.get("mrhob")
        v(lambda e: e.tensor_copy(mrhob, r_mu), ["r_mu"], ["mrhob"])
        t_mm(aPRHO2, onerb[0:1, 0:128], rhob, start=True, stop=False,
             reads=["rhob", "onerb"], writes=[BK3])
        t_mm(aPMRHO, onerb[0:1, 0:128], mrhob, start=False, stop=True,
             reads=["mrhob", "onerb"], writes=[BK3])
        for c in range(8):
            v(lambda e, c=c: e.scalar_tensor_tensor(sqb[:, c, :], xtb[:, c, :],
                                                    vecs[:, b % 2, VL["cb1"], c:c + 1], aPRHO2,
                                                    ALU.mult, ALU.mult),
              ["xtb", BK3] + KB, ["sqb"])
            v(lambda e, c=c: e.tensor_scalar(stg32[:, 0, 0:T], aPMRHO,
                                             vecs[:, b % 2, VL["ncb1"], c:c + 1],
                                             modt[:, 24 + c, b:b + 1], ALU.mult, ALU.add),
              [BK3, f"modt:{24 + c}"] + KB, ["stg32:0"])
            v(lambda e, c=c: e.tensor_tensor(hffnt[:, par, c, :], sqb[:, c, :],
                                             stg32[:, 0, 0:T], ALU.add),
              ["sqb", "stg32:0"], [f"hffnt:{par}:{c}"])
        yield

        HFK = [f"hffnt:{par}:{c}" for c in range(8)]
        s(lambda e: e.activation(sqb.ap().rearrange("p a b -> p (a b)"),
                                 hffnt.ap()[:, par, :, :].rearrange("p a b -> p (a b)"), AF.Square),
          HFK, ["sqb"])
        for c in range(8):
            t_mm(aSR1, onecb[:], hffnt[:, par, c, :], start=(c == 0), stop=(c == 7),
                 reads=[f"hffnt:{par}:{c}", "onecb"], writes=[BK3])
        for c in range(8):
            t_mm(aSR2, onecb[:], sqb[:, c, :], start=(c == 0), stop=(c == 7),
                 reads=["sqb", "onecb"], writes=[BK3])
        yield
        r_mu = RW.get("r_mu")
        r_rho = RW.get("r_rho")
        v(lambda e: e.tensor_scalar(r_mu, aSR1, 1.0 / E, None, ALU.mult), [BK3], ["r_mu"])
        v(lambda e: e.tensor_scalar(tv, aSR2, 1.0 / E, None, ALU.mult), [BK3], ["r_tv"])
        v(lambda e: e.tensor_tensor(r_rho, r_mu, r_mu, ALU.mult), ["r_mu"], ["r_rho"])
        v(lambda e: e.tensor_tensor(tv, tv, r_rho, ALU.subtract), ["r_tv", "r_rho"], ["r_tv"])
        preload_tab(6)
        s(lambda e: e.activation(tv, tv, AF.Ln, bias=cvals[0:1, 0:1]), ["r_tv", "cvals"], ["r_tv"])
        s(lambda e: e.activation(r_rho, tv, AF.Exp, scale=cvals[0:1, 1:2]), ["r_tv", "cvals"], ["r_rho"])
        preload_tab(10)
        nm3b = RB.get("nm3b")
        rc3b = RB.get("rc3b")
        v(lambda e: e.tensor_scalar(nm3b, r_mu, -1.0, None, ALU.mult), ["r_mu"], ["nm3b"])
        v(lambda e: e.reciprocal(tv, r_rho), ["r_rho"], ["r_tv"])
        v(lambda e: e.tensor_copy(rc3b, tv), ["r_tv"], ["rc3b"])
        yield

        for hf in range(2):
            tsl = slice(128 * hf, 128 * (hf + 1))
            for dc in range(8):
                t_mm(aTOK[:, 0:NB], hffnt[:, par, dc, tsl], wg[:, dc, :],
                     start=(dc == 0), stop=False,
                     reads=[f"hffnt:{par}:{dc}"] + WGK, writes=[BK2])
            t_tr(aCOL[:, 0:1], r_rho[0:1, tsl], idf[0:1, 0:1],
                 reads=["r_rho", "idf"], writes=[BK3])
            v(lambda e: e.tensor_copy(colb[:, 0:1], aCOL[:, 0:1]), [BK3], ["colb"])
            t_mm(aTOK[:, 0:NB], nm3b[0:1, tsl], csr[0:1, :], start=False, stop=False,
                 reads=["nm3b", "csr"], writes=[BK2])
            t_mm(aTOK[:, 0:NB], rc3b[0:1, tsl], cnr[0:1, :], start=False, stop=True,
                 reads=["rc3b", "cnr"], writes=[BK2])
            v(lambda e: e.tensor_scalar(tokA[:], aTOK[:, 0:NB], colb[:, 0:1], 10.0,
                                        ALU.mult, ALU.min),
              [BK2, "colb"], ["tokA"])
            v(lambda e: e.tensor_scalar(stg32[:, 1, :], tokA[:], -10.0, None, ALU.max),
              ["tokA"], ["stg32:1"])
            preload_tab(6)
            s(lambda e: e.activation(tokA[:], stg32[:, 1, :], AF.Exp), ["stg32:1"], ["tokA"])
            s(lambda e: e.activation(stg32[:, 1, :], tokA[:], AF.Ln, bias=cvals[:, 7:8]),
              ["tokA", "cvals"], ["stg32:1"])
            v(lambda e: e.max(phib[:, 0:8], stg32[:, 1, :]), ["stg32:1"], ["phib"])
            v(lambda e: e.reduce_sum(colb[:, 1:2], phib[:, 0:8], axis=AXX), ["phib"], ["colb"])
            s(lambda e: e.activation(colb[:, 2:3], colb[:, 1:2], AF.Exp, scale=cvals[:, 5:6]),
              ["colb", "cvals"], ["colb"])
            preload_tab(10)
            v(lambda e: e.tensor_scalar(colb[:, 3:4], colb[:, 2:3], -1.0, 1.0, ALU.mult, ALU.add),
              ["colb"], ["colb"])
            v(lambda e: e.tensor_scalar(colb[:, 4:5], colb[:, 2:3], 1.0, None, ALU.add),
              ["colb"], ["colb"])
            v(lambda e: e.tensor_scalar(colb[:, 5:6], colb[:, 1:2], 1e-6, None, ALU.add),
              ["colb"], ["colb"])
            v(lambda e: e.tensor_tensor(colb[:, 6:7], colb[:, 4:5], colb[:, 5:6], ALU.mult),
              ["colb"], ["colb"])
            v(lambda e: e.reciprocal(colb[:, 7:8], colb[:, 6:7]), ["colb"], ["colb"])
            v(lambda e: e.tensor_tensor(colb[:, 8:9], colb[:, 3:4], colb[:, 7:8], ALU.mult),
              ["colb"], ["colb"])
            v(lambda e: e.scalar_tensor_tensor(tokA[:], stg32[:, 1, :], phib[:, 7:8], stg32[:, 1, :],
                                               ALU.is_ge, ALU.mult),
              ["stg32:1", "phib"], ["tokA"])
            yield
            for dc in range(8):
                t_mm(aTOK[:, 0:NB], hffnt[:, par, dc, tsl], ut[:, dc, :],
                     start=(dc == 0), stop=(dc == 7),
                     reads=[f"hffnt:{par}:{dc}"] + UTK + ["tokA"], writes=[BK2])
            v(lambda e: e.scalar_tensor_tensor(hlzb[:], aTOK[:, 0:NB], colb[:, 8:9],
                                               tokA[:], ALU.mult, ALU.mult),
              [BK2, "colb", "tokA"], ["hlzb"])
            for uc in range(4):
                cell, bk = (aZT, BK3) if uc % 2 == 0 else (aZT2, BK2)
                t_tr(cell[:, 0:128], hlzb[:, 128 * uc:128 * (uc + 1)], idb[:],
                     reads=["hlzb", "idb"], writes=[bk])
                v(lambda e, uc=uc, hf=hf, cell=cell: e.tensor_copy(
                        zht[:, par, uc, tsl], cell[:, 0:128]),
                  [bk], [f"zht:{par}"])
            yield


    def phaseB(ti, gen):
        b = ti // ntpb
        t0 = (ti % ntpb) * T
        par = ti % 2
        HFK = [f"hffnt:{par}:{c}" for c in range(8)]

        def pump(n=1):
            for _ in range(n):
                if gen is not None:
                    try:
                        next(gen)
                    except StopIteration:
                        break

        def w2_load(w):
            wpar = w % 2
            for j in range(4):
                hc = 4 * w + j
                cx.dma("sync", f"w2{wpar}", w2tw[:, wpar, j, :],
                       AP(w2s_h, 128 * E * hc, [[E, 128], [1, E]]),
                       reads=[f"w2scr:{hc}"], writes=[f"w2tw:{wpar}"])

        w2_load(0)
        w2_load(1)
        pump(3)
        for w in range(8):
            wpar = w % 2
            GK = [f"gelut:{wpar}:{j}" for j in range(4)]
            for j in (0, 2, 1, 3):
                i = 4 * w + j
                for dc in range(8):
                    t_mm(pbig[:, j, :], w1t[:, dc, 128 * i:128 * (i + 1)], hffnt[:, par, dc, :],
                         start=(dc == 0), stop=(dc == 7),
                         reads=[f"w1t:{dc}", f"hffnt:{par}:{dc}"], writes=[BKB(j)])
                s(lambda e, j=j, wpar=wpar: e.activation(gelut[:, wpar, j, :], pbig[:, j, :],
                                                         AF.Gelu),
                  [BKB(j)], [f"gelut:{wpar}:{j}"])
            for j in range(4):
                for nc2 in range(8):
                    t_mm(pout[:, nc2, :], w2tw[:, wpar, j, 128 * nc2:128 * (nc2 + 1)],
                         gelut[:, wpar, j, :],
                         start=(w == 0 and j == 0 and nc2 % 2 == 0), stop=False,
                         reads=[f"w2tw:{wpar}", f"gelut:{wpar}:{j}"], writes=[BKO(nc2)])
            if w < 6:
                w2_load(w + 2)
            if w == 3:
                emit_xin(ti + 2)
            pump({0: 2, 1: 2, 2: 2, 3: 2, 4: 1, 5: 2, 6: 1, 7: 0}[w])
        for uc in range(4):
            for nc2 in range(8):
                t_mm(pout[:, nc2, :], vg[:, uc, 128 * nc2:128 * (nc2 + 1)], zht[:, par, uc, :],
                     start=False, stop=(uc == 3 and nc2 % 2 == 1),
                     reads=VGK + [f"zht:{par}"], writes=[BKO(nc2)])
        pump(2)
        for c in range(8):
            st_ = c % 4
            v(lambda e, c=c, st_=st_: e.scalar_tensor_tensor(
                    x3st[:, st_, :], pout[:, c, :], vecs[:, b % 2, VL["gb"], c:c + 1],
                    x2t[:, par, c, :], ALU.mult, ALU.add),
              [BKO(c), f"vecs:{b % 2}", f"x2t:{par}"], [f"x3st:{st_}"])
            cx.dma("gpsimd", f"out{st_}", oh[b, 128 * c:128 * (c + 1), t0:t0 + T],
                   x3st[:, st_, :], reads=[f"x3st:{st_}"], writes=[])
        pump(100)

    gen = phaseA(0)
    next(gen)
    next(gen)
    emit_xin(0)
    emit_weight_tail()
    for _ in range(100):
        try:
            next(gen)
        except StopIteration:
            break
    emit_xin(1)
    for ti in range(n_tiles):
        gen = phaseA(ti + 1) if ti + 1 < n_tiles else None
        phaseB(ti, gen)

    cx.finish("sync")
    return nc


def kernel(L0_attn_x, L0_attn_W_el, L0_attn_b_el, L0_attn_W_le, L0_attn_b_le,
           L0_ffn_W1, L0_ffn_W2, L0_ffn_ln_g, L0_ffn_ln_b, L0_ffn_router_W,
           L0_ffn_router_b, L0_ffn_raw_U, L0_ffn_raw_V, L0_ffn_gamma,
           L0_c, L0_wmod):
    COS, SIN, ka, Hm, lx = _consts()
    x = np.asarray(L0_attn_x, dtype=np.float32)
    xT = np.ascontiguousarray(x.transpose(0, 2, 1))
    x00row = np.ascontiguousarray(x[0, 0, :]).reshape(1, E)
    wle_ext = np.ascontiguousarray(
        np.concatenate([np.asarray(L0_attn_W_le, np.float32),
                        np.asarray(L0_attn_b_le, np.float32)[None, :]], axis=0))
    common = {
        "x00row": x00row,
        "wmod": np.ascontiguousarray(np.asarray(L0_wmod, np.float32)),
        "W_el": np.ascontiguousarray(np.asarray(L0_attn_W_el, np.float32)),
        "Wle_ext": wle_ext,
        "W1T": np.ascontiguousarray(np.asarray(L0_ffn_W1, np.float32).T),
        "W2T": np.ascontiguousarray(np.asarray(L0_ffn_W2, np.float32).T),
        "RWT": np.ascontiguousarray(np.asarray(L0_ffn_router_W, np.float32).T),
        "UT": np.ascontiguousarray(np.asarray(L0_ffn_raw_U, np.float32).T),
        "V": np.ascontiguousarray(np.asarray(L0_ffn_raw_V, np.float32)),
        "ln_gc": np.ascontiguousarray(np.asarray(L0_ffn_ln_g, np.float32).reshape(8, 128).T),
        "ln_bc": np.ascontiguousarray(np.asarray(L0_ffn_ln_b, np.float32).reshape(8, 128).T),
        "router_b": np.ascontiguousarray(np.asarray(L0_ffn_router_b, np.float32).reshape(1, NB)),
        "gamma_r": np.ascontiguousarray(np.asarray(L0_ffn_gamma, np.float32).reshape(1, E)),
        "COS": COS, "SIN": SIN,
        "ka_r": ka.reshape(1, E),
        "ka2_r": (ka * ka).reshape(1, E),
        "lx_r": lx.reshape(1, 5),
        "HT": np.ascontiguousarray(Hm.T),
    }
    nc = build_nc()
    cf = np.asarray(L0_c, np.float32)
    in_maps = []
    for k in range(NCORES):
        m = dict(common)
        m["xT"] = np.ascontiguousarray(xT[BPC * k:BPC * (k + 1)])
        m["cT"] = np.ascontiguousarray(
            np.stack([cf[BPC * k], cf[BPC * k + 1], cf[0], cf[0]], axis=1))
        in_maps.append(m)
    res = run_bass_kernel_spmd(nc, in_maps, core_ids=list(range(NCORES)))
    outs = [np.asarray(r["outT"]).reshape(BPC, E, SEQ).transpose(0, 2, 1)
            for r in res.results]
    return np.ascontiguousarray(np.concatenate(outs, axis=0)).astype(np.float32)
